# version 09
# baseline (speedup 1.0000x reference)
import os
import concourse.bass as bass
import concourse.mybir as mybir

SELF_EDGE_WAITS = os.environ.get("FW_SELF_WAITS", "1") == "1"


class OpRef:
    __slots__ = ("eng", "idx")
    def __init__(self, eng, idx):
        self.eng = eng
        self.idx = idx


class DmaRef:
    __slots__ = ("q", "sem_name", "value")
    def __init__(self, q, sem_name, value):
        self.q = q
        self.sem_name = sem_name
        self.value = value


_SEQ = [0]

class _Rec:
    __slots__ = ("opname", "args", "kwargs", "waits", "inc_needed", "is_dma",
                 "dma_sem", "seq")
    def __init__(self, opname, args, kwargs, is_dma=False, dma_sem=None):
        self.opname = opname
        self.args = args
        self.kwargs = kwargs
        self.waits = []
        self.inc_needed = False
        self.is_dma = is_dma
        self.dma_sem = dma_sem
        _SEQ[0] += 1
        self.seq = _SEQ[0]


class Eng:
    def __init__(self, mk, name, self_edges=True):
        self.mk = mk
        self.name = name
        self.self_edges = self_edges
        self.recs = []
        self.pending_waits = []

    def wait(self, ref):
        assert ref is not None
        if isinstance(ref, OpRef) and not ref.eng.self_edges:
            ref.eng.mark(ref)
        self.pending_waits.append(ref)

    def __getattr__(self, opname):
        if opname.startswith("_"):
            raise AttributeError(opname)
        def rec(*args, **kwargs):
            r = _Rec(opname, args, kwargs)
            r.waits = self.pending_waits
            self.pending_waits = []
            self.recs.append(r)
            ref = OpRef(self, len(self.recs))
            if self.self_edges:
                r.inc_needed = True
            return ref
        return rec

    def mark(self, ref):
        self.recs[ref.idx - 1].inc_needed = True


class Q:
    def __init__(self, mk, name):
        self.mk = mk
        self.name = name
        self.recs = []
        self.pending_waits = []
        self.sem_counts = {}

    def wait(self, ref):
        if isinstance(ref, OpRef) and not ref.eng.self_edges:
            ref.eng.mark(ref)
        self.pending_waits.append(ref)

    def dma(self, sem_name, out, in_, **kwargs):
        key = (self.name, sem_name)
        self.mk.dma_sems.setdefault(sem_name, 0)
        self.mk.dma_sems[sem_name] += 16
        r = _Rec("dma_start", (), dict(out=out, in_=in_, **kwargs),
                 is_dma=True, dma_sem=sem_name)
        r.waits = self.pending_waits
        self.pending_waits = []
        self.recs.append(r)
        return DmaRef(self, sem_name, self.mk.dma_sems[sem_name])

    def dma_transpose(self, sem_name, out, in_):
        self.mk.dma_sems.setdefault(sem_name, 0)
        self.mk.dma_sems[sem_name] += 16
        r = _Rec("dma_start_transpose", (), dict(out=out, in_=in_),
                 is_dma=True, dma_sem=sem_name)
        r.waits = self.pending_waits
        self.pending_waits = []
        self.recs.append(r)
        return DmaRef(self, sem_name, self.mk.dma_sems[sem_name])

    def indirect(self, sem_name, out, out_offset, in_, in_offset,
                 bounds=None, **kwargs):
        self.mk.dma_sems.setdefault(sem_name, 0)
        self.mk.dma_sems[sem_name] += 16
        r = _Rec("indirect_dma_start", (),
                 dict(out=out, out_offset=out_offset, in_=in_,
                      in_offset=in_offset, _bounds=bounds, **kwargs),
                 is_dma=True, dma_sem=sem_name)
        r.waits = self.pending_waits
        self.pending_waits = []
        self.recs.append(r)
        return DmaRef(self, sem_name, self.mk.dma_sems[sem_name])

    def fence(self, *refs):
        r = _Rec("__fence__", (), {})
        r.waits = self.pending_waits + list(refs)
        self.pending_waits = []
        for ref in r.waits:
            if isinstance(ref, OpRef) and not ref.eng.self_edges:
                ref.eng.mark(ref)
        self.recs.append(r)

    def collective(self, sem_name, *args, **kwargs):
        self.mk.dma_sems.setdefault(sem_name, 0)
        self.mk.dma_sems[sem_name] += 16
        r = _Rec("collective_compute", args, kwargs, is_dma=True,
                 dma_sem=sem_name)
        r.waits = self.pending_waits
        self.pending_waits = []
        self.recs.append(r)
        return DmaRef(self, sem_name, self.mk.dma_sems[sem_name])


class MKProg:
    ENGINES = ("tensor", "vector", "scalar", "gpsimd")
    QUEUES = ("sync", "gpsimd", "scalar")

    def __init__(self):
        self.engs = {}
        self.qs = {}
        self.dma_sems = {}

    def eng(self, name):
        if name not in self.engs:
            self.engs[name] = Eng(self, name, self_edges=(name != "tensor"))
        return self.engs[name]

    def q(self, name):
        if name not in self.qs:
            self.qs[name] = Q(self, name)
        return self.qs[name]

    def barrier(self):
        refs = []
        for e in self.engs.values():
            if e.recs:
                refs.append(OpRef(e, len(e.recs)))
        for semname, val in self.dma_sems.items():
            refs.append(DmaRef(None, semname, val))
        for e in self.engs.values():
            for r in refs:
                if isinstance(r, OpRef) and r.eng is e:
                    continue
                e.wait(r)
        for qq in self.qs.values():
            for r in refs:
                if isinstance(r, OpRef):
                    qq.wait(r)
                else:
                    qq.wait(r)

    def emit(self, p):
        nc = p.nc
        esems = {n: p.sem(f"E_{n}") for n in self.engs}
        dsems = {n: p.sem(f"D_{n}") for n in self.dma_sems}

        inc_prefix = {}
        for n, e in self.engs.items():
            pref = []
            c = 0
            for r in e.recs:
                if r.inc_needed:
                    c += 1
                pref.append(c)
            inc_prefix[n] = pref

        def wait_val(ref):
            if isinstance(ref, OpRef):
                return esems[ref.eng.name], inc_prefix[ref.eng.name][ref.idx - 1]
            else:
                return dsems[ref.sem_name], ref.value

        blk = p.block()

        def emit_engine_ops(handle, ename, recs):
            own_sem = esems.get(ename)
            self_edges = ename in self.engs and self.engs[ename].self_edges
            prev_incs = 0
            for i, r in enumerate(recs):
                for w in r.waits:
                    sem, val = wait_val(w)
                    handle.wait_ge(sem, val)
                if SELF_EDGE_WAITS and self_edges and prev_incs > 0:
                    handle.wait_ge(own_sem, prev_incs)
                try:
                    inst = getattr(handle, r.opname)(*r.args, **r.kwargs)
                except Exception as ex:
                    print('FAIL op', ename, i, r.opname,
                          [getattr(a, 'shape', a) for a in r.args],
                          {k: getattr(vv, 'shape', None) for k, vv in r.kwargs.items()})
                    raise
                if r.inc_needed:
                    inst.then_inc(own_sem, 1)
                    prev_incs += 1

        def emit_queue_ops(handle, q, bounds_regs):
            for r in q.recs:
                for w in r.waits:
                    sem, val = wait_val(w)
                    handle.wait_ge(sem, val)
                if r.opname == "__fence__":
                    continue
                kw = dict(r.kwargs)
                if r.opname == "indirect_dma_start":
                    b = kw.pop("_bounds")
                    if b is not None:
                        if b not in bounds_regs:
                            bounds_regs[b] = handle.to_reg(b)
                        kw["bounds_check"] = bounds_regs[b]
                inst = getattr(handle, r.opname)(*r.args, **kw)
                inst.then_inc(dsems[r.dma_sem], 16)

        sections = {}
        for n in self.engs:
            sections[n] = ("eng", n)
        def make_body(ename):
            def body(handle):
                bounds_regs = {}
                has_e = ename in self.engs
                has_q = ename in self.qs
                if has_e and not has_q:
                    emit_engine_ops(handle, ename, self.engs[ename].recs)
                elif has_q and not has_e:
                    emit_queue_ops(handle, self.qs[ename], bounds_regs)
                else:
                    er = self.engs[ename].recs
                    qr = self.qs[ename].recs
                    own_sem = esems.get(ename)
                    self_edges = self.engs[ename].self_edges
                    prev_incs = 0
                    ei = qi = 0
                    while ei < len(er) or qi < len(qr):
                        take_e = qi >= len(qr) or (ei < len(er) and er[ei].seq < qr[qi].seq)
                        if take_e:
                            r = er[ei]; ei += 1
                            for w in r.waits:
                                sem, val = wait_val(w)
                                handle.wait_ge(sem, val)
                            if SELF_EDGE_WAITS and self_edges and prev_incs > 0:
                                handle.wait_ge(own_sem, prev_incs)
                            inst = getattr(handle, r.opname)(*r.args, **r.kwargs)
                            if r.inc_needed:
                                inst.then_inc(own_sem, 1)
                                prev_incs += 1
                        else:
                            r = qr[qi]; qi += 1
                            for w in r.waits:
                                sem, val = wait_val(w)
                                handle.wait_ge(sem, val)
                            if r.opname == "__fence__":
                                continue
                            kw = dict(r.kwargs)
                            if r.opname == "indirect_dma_start":
                                bnd = kw.pop("_bounds")
                                if bnd is not None:
                                    if bnd not in bounds_regs:
                                        bounds_regs[bnd] = handle.to_reg(bnd)
                                    kw["bounds_check"] = bounds_regs[bnd]
                            inst = getattr(handle, r.opname)(*r.args, **kw)
                            inst.then_inc(dsems[r.dma_sem], 16)
            return body

        used = set(self.engs) | set(self.qs)
        for ename in ("sync", "tensor", "scalar", "vector", "gpsimd"):
            if ename not in used:
                continue
            body = make_body(ename)
            getattr(blk, ename)(body)
        return esems, dsems


"""Megakernel builders for mk_prb_d2. Becomes kernel.py when complete.

Layouts: token-major (tokens on partitions) for LN/gathers; feature-major
("T") for matmuls (contraction on partitions). Host numpy = layout only.
Sync discipline: fw.py recording framework; self-edges on DVE/ACT/Pool;
<=1 outstanding DMA per semaphore; waits on exact recorded counts.
"""
import numpy as np
import ml_dtypes
from contextlib import ExitStack

import concourse.bass as bass
import concourse.mybir as mybir
from concourse.bass import IndirectOffsetOnAxis
from concourse.bass_utils import run_bass_kernel_spmd
from concourse.alu_op_type import AluOpType as OP


F32 = mybir.dt.float32
BF16 = mybir.dt.bfloat16
U32 = mybir.dt.uint32
AF = mybir.ActivationFunctionType
BF16NP = ml_dtypes.bfloat16

B = 8
T = 4096
D = 1024
TT_ALL = B * T
NE = 8
CAP0 = 1280
K0 = 8192
LN_EPS = 1e-5
BIS_ITERS = 28
BIG = 65536.0

NT = T // 128
NCH = TT_ALL // 512
NDT = D // 128
NTILES_ALL = TT_ALL // 128


class Prog:
    def __init__(self):
        self.nc = bass.Bass("TRN2", target_bir_lowering=False)
        self.ctx = ExitStack()

    def sem(self, name):
        return self.ctx.enter_context(self.nc.semaphore(name))

    def sb(self, name, shape, dtype):
        return self.ctx.enter_context(self.nc.sbuf_tensor(name, shape, dtype))

    def ps(self, name, shape, dtype=F32):
        return self.ctx.enter_context(self.nc.psum_tensor(name, shape, dtype))

    def din(self, name, shape, dtype):
        return self.nc.dram_tensor(name, shape, dtype, kind="ExternalInput").ap()

    def dout(self, name, shape, dtype):
        return self.nc.dram_tensor(name, shape, dtype, kind="ExternalOutput").ap()

    def block(self):
        return self.ctx.enter_context(self.nc.Block())


D_INNER = 2048
D_STATE = 16
DT_RANK = 64
NKI = D // 128
NDI = D_INNER // 128
MW = 33 * 128
NCHB = 8
CW = 512
FF = 4096
NFT = FF // 128
NTPE = CAP0 // 128
GRP = [4, 4, 2]
NH = 16
HD = 64
EXP_SHIFT = -10.0
CAP1 = 2048
NTPE1 = CAP1 // 128
RMS_EPS = 1e-8


def emit_stage_a(p, t, v, s, g, q, gq, bank, dr):
    nc = p.nc
    ctx = ExitStack()
    def sb(nm, shape, dt):
        return ctx.enter_context(nc.sbuf_tensor(f'a_'+nm, shape, dt))

    x_own = dr["x_own"]
    x_allT = dr["x_allT"]
    wg9 = dr["Wg9"]
    ones_col = dr["ones_col"]
    ones_n = dr["ones_n"]
    tokid = dr["tokid"]
    ecapbig_in = dr["ecapbig"]
    lists_init = dr["lists_init"]
    idn_in = dr["ident"]
    consts_in = dr["consts"]

    xn_out = dr["xn"]
    lists0 = dr["lists0"]
    thr_out = dr["thr"]

    xs = [sb(f"xs{i}", [128, NDT * 512], F32) for i in range(2)]
    sqs = [sb(f"sqs{i}", [128, NDT * 512], F32) for i in range(2)]
    wg_sb = sb("wg_sb", [128, NDT * 9], F32)
    onec = sb("onec", [128, 1], F32)
    onen = sb("onen", [1, 128], F32)
    crow = sb("crow", [1, 8], F32)
    cexp = sb("cexp", [128, 8], F32)
    tokids = sb("tokids", [128, NT], F32)
    ecapbig = sb("ecapbig_s", [8, 1], F32)
    li = sb("li", [128, NE * CAP0 * 2 // 128], F32)
    idn = sb("idn", [128, 128], F32)

    scoresTok = sb("scoresTok", [128, NE * NTILES_ALL], F32)
    rs_all = sb("rs_all", [128, NCH * 4], F32)
    muG = sb("muG", [128, NCH * 4], F32)
    var4 = [sb(f"var4_{i}", [128, 4], F32) for i in range(2)]
    rs4 = [sb(f"rs4_{i}", [128, 4], F32) for i in range(2)]
    mu2 = sb("mu2", [128, 1], F32)
    m2t = sb("m2t", [128, 8], F32)
    dlt = sb("dlt", [128, 8], F32)
    logit4 = [sb(f"logit4_{i}", [128, 32], F32) for i in range(2)]

    lo = sb("lo", [1, 8], F32)
    hi = sb("hi", [1, 8], F32)
    mid = sb("mid", [1, 8], F32)
    prd = sb("prd", [1, 8], F32)
    dmh = sb("dmh", [1, 8], F32)
    u8t = sb("u8t", [1, 8], F32)
    cntp = sb("cntp", [128, 8], F32)
    cjunk = sb("cjunk", [128, 256], F32)
    thr_c = sb("thr_c", [8, 1], F32)

    scoresT = sb("scoresT", [8, T], F32)
    maskT = sb("maskT", [8, T], F32)
    gateT = sb("gateT", [8, T], F32)
    ones8T = sb("ones8T", [8, T], F32)
    posT = sb("posT", [8, T], F32)
    offT = sb("offT", [8, T], F32)

    offu = [sb(f"offu{i}", [128, 8], U32) for i in range(2)]
    src2 = [sb(f"src2_{i}", [128, 16], F32) for i in range(2)]

    xt = [sb(f"xt{i}", [128, D], F32) for i in range(2)]
    xnb = [sb(f"xnb{i}", [128, D], BF16) for i in range(2)]

    psS = [bank[0][:, 0:40], bank[1][:, 0:40]]
    psQ = [bank[2][:, 0:4], bank[3][:, 0:4]]
    psT1 = bank[4][:, 0:128]
    psB = bank[5][:, 0:8]
    psC = bank[6][0:1, 0:8]
    psTh = bank[7][0:8, 0:1]


    pre = []
    pre.append(q.dma("pre", out=wg_sb[:].rearrange("p (a n) -> p a n", n=9),
                     in_=wg9.rearrange("(a p) n -> p a n", p=128)))
    pre.append(q.dma("pre", out=onec[:], in_=ones_col))
    pre.append(q.dma("pre", out=onen[:], in_=ones_n))
    pre.append(q.dma("pre", out=tokids[:], in_=tokid))
    pre.append(q.dma("pre", out=ecapbig[:], in_=ecapbig_in))
    pre.append(q.dma("pre", out=li[:], in_=lists_init))
    pre.append(q.dma("pre", out=idn[:], in_=idn_in))
    pre_done = pre[-1]

    q.wait(pre_done)
    r_li = q.dma("sli", out=lists0.rearrange("(a p) n -> p a n", p=128),
                 in_=li[:].rearrange("p (a n) -> p a n", n=2))

    t.wait(pre_done)
    for k in range(NDT):
        r_csum = t.matmul(psC, onec[:], wg_sb[:, 9 * k:9 * k + 8],
                          start=(k == 0), stop=(k == NDT - 1))
    v.wait(r_csum)
    r_crow = v.tensor_copy(out=crow[:], in_=psC)
    t.wait(r_crow)
    r_cb = t.matmul(psB, onen[:], crow[:], start=True, stop=True)
    v.wait(r_cb)
    r_cexp = v.tensor_copy(out=cexp[:], in_=psB)
    v.memset(ones8T[:], 1.0)

    te_ref = [None] * NCH
    v2_ref = [None] * NCH
    sg_ref = [None] * NCH
    sg_s_ref = [None] * NCH
    for c in range(NCH):
        b = c % 2
        if c >= 2:
            q.wait(te_ref[c - 2])
        r_lc = q.dma(f"lc{b}", out=xs[b][:].rearrange("p (a t) -> p a t", a=NDT),
                     in_=x_allT[:, c * 512:(c + 1) * 512].rearrange("(a p) t -> p a t", p=128))
        s.wait(r_lc)
        if c >= 2:
            s.wait(te_ref[c - 2])
        for k in range(NDT):
            r_sq = s.activation(out=sqs[b][:, k * 512:(k + 1) * 512],
                                in_=xs[b][:, k * 512:(k + 1) * 512], func=AF.Square)
        t.wait(r_lc)
        t.wait(r_sq)
        if c >= 2:
            t.wait(v2_ref[c - 2])
        for jj in range(4):
            for k in range(NDT):
                t.matmul(psS[b][:, 9 * jj:9 * jj + 9],
                         xs[b][:, k * 512 + jj * 128:k * 512 + (jj + 1) * 128],
                         wg_sb[:, 9 * k:9 * k + 9],
                         start=(k == 0), stop=(k == NDT - 1))
        for jj in range(4):
            for k in range(NDT):
                r_mm = t.matmul(psQ[b][:, jj:jj + 1],
                                sqs[b][:, k * 512 + jj * 128:k * 512 + (jj + 1) * 128],
                                onec[:], start=(k == 0), stop=(k == NDT - 1))
        te_ref[c] = r_mm
        v.wait(r_mm)
        if c >= 2:
            v.wait(sg_s_ref[c - 2])
        for jj in range(4):
            v.tensor_copy(out=muG[:, 4 * c + jj:4 * c + jj + 1],
                          in_=psS[b][:, 9 * jj + 8:9 * jj + 9])
            v.tensor_tensor(out=mu2[:], in0=muG[:, 4 * c + jj:4 * c + jj + 1],
                            in1=muG[:, 4 * c + jj:4 * c + jj + 1], op=OP.mult)
            r_v1 = v.scalar_tensor_tensor(out=var4[b][:, jj:jj + 1],
                                          in0=psQ[b][:, jj:jj + 1], scalar=1.0 / D,
                                          in1=mu2[:], op0=OP.mult, op1=OP.subtract)
        s.wait(r_v1)
        s.activation(out=rs4[b][:], in_=var4[b][:], func=AF.Ln, bias=LN_EPS)
        s.activation(out=rs4[b][:], in_=rs4[b][:], func=AF.Exp, scale=-0.5)
        r_rs = s.copy(out=rs_all[:, 4 * c:4 * c + 4], in_=rs4[b][:])
        v.wait(r_rs)
        for jj in range(4):
            v.tensor_scalar(out=m2t[:], in0=cexp[:],
                            scalar1=muG[:, 4 * c + jj:4 * c + jj + 1], scalar2=None,
                            op0=OP.mult)
            v.tensor_tensor(out=dlt[:], in0=psS[b][:, 9 * jj:9 * jj + 8],
                            in1=m2t[:], op=OP.subtract)
            r_v2 = v.tensor_scalar(out=logit4[b][:, 8 * jj:8 * jj + 8], in0=dlt[:],
                                   scalar1=rs_all[:, 4 * c + jj:4 * c + jj + 1],
                                   scalar2=None, op0=OP.mult)
        v2_ref[c] = r_v2
        s.wait(r_v2)
        s.activation(out=logit4[b][:], in_=logit4[b][:], func=AF.Exp, scale=-1.0)
        sg_s_ref[c] = s.add(out=logit4[b][:], in_=logit4[b][:], add=1.0)
        v.wait(sg_s_ref[c])
        dst = scoresTok[:].rearrange("p (e t) -> p t e", e=NE)[:, 4 * c:4 * c + 4, :]
        sg_ref[c] = v.reciprocal(out=dst,
                                 in_=logit4[b][:].rearrange("p (t e) -> p t e", e=NE))

    ev_ref = [None] * NT
    for j in range(NT):
        t.wait(sg_ref[j // 4])
        if j >= 1:
            t.wait(ev_ref[j - 1])
        src = scoresTok[:].rearrange("p (e t) -> p t e", e=NE)[:, j, :]
        r_tr = t.transpose(psT1[0:8, 0:128], src, idn[:])
        v.wait(r_tr)
        ev_ref[j] = v.tensor_copy(out=scoresT[:, j * 128:(j + 1) * 128],
                                  in_=psT1[0:8, 0:128])

    v.wait(sg_ref[NCH - 1])
    v.memset(lo[:], 0.0)
    v.memset(hi[:], 1.0)
    r_upd = v.memset(mid[:], 0.5)
    for i in range(BIS_ITERS):
        t.wait(r_upd)
        r_bc = t.matmul(psB, onen[:], mid[:], start=True, stop=True)
        v.wait(r_bc)
        for e in range(NE):
            r_cnt = v.tensor_scalar(out=cjunk[:],
                                    in0=scoresTok[:, e * NTILES_ALL:(e + 1) * NTILES_ALL],
                                    scalar1=psB[:, e:e + 1], scalar2=0.0,
                                    op0=OP.is_gt, op1=OP.add,
                                    accum_out=cntp[:, e:e + 1])
        t.wait(r_cnt)
        r_red = t.matmul(psC, onec[:], cntp[:], start=True, stop=True)
        v.wait(r_red)
        v.tensor_scalar(out=prd[:], in0=psC, scalar1=float(K0), scalar2=None,
                        op0=OP.is_ge)
        v.tensor_tensor(out=dmh[:], in0=mid[:], in1=lo[:], op=OP.subtract)
        v.tensor_tensor(out=u8t[:], in0=prd[:], in1=dmh[:], op=OP.mult)
        v.tensor_tensor(out=lo[:], in0=lo[:], in1=u8t[:], op=OP.add)
        v.tensor_tensor(out=dmh[:], in0=mid[:], in1=hi[:], op=OP.subtract)
        v.tensor_tensor(out=u8t[:], in0=prd[:], in1=dmh[:], op=OP.mult)
        v.tensor_tensor(out=hi[:], in0=hi[:], in1=dmh[:], op=OP.add)
        v.tensor_tensor(out=hi[:], in0=hi[:], in1=u8t[:], op=OP.subtract)
        v.tensor_tensor(out=mid[:], in0=lo[:], in1=hi[:], op=OP.add)
        r_upd = v.tensor_scalar(out=mid[:], in0=mid[:], scalar1=0.5, scalar2=None,
                                op0=OP.mult)

    t.wait(r_upd)
    r_th = t.matmul(psTh, lo[:], onen[0:1, 0:1], start=True, stop=True)
    v.wait(r_th)
    r_thr = v.tensor_copy(out=thr_c[:], in_=psTh)
    q.wait(r_thr)
    r_sth = q.dma("sth", out=thr_out, in_=thr_c[:])

    v.tensor_scalar(out=maskT[:], in0=scoresT[:], scalar1=thr_c[:], scalar2=None,
                    op0=OP.is_gt)
    v.tensor_tensor(out=gateT[:], in0=scoresT[:], in1=maskT[:], op=OP.mult)
    v.tensor_tensor_scan(out=posT[:], data0=ones8T[:], data1=maskT[:],
                         initial=0.0, op0=OP.mult, op1=OP.add)
    v.scalar_tensor_tensor(out=offT[:], in0=maskT[:], scalar=-BIG,
                           in1=posT[:], op0=OP.mult, op1=OP.add)
    r_mask = v.tensor_scalar(out=offT[:], in0=offT[:], scalar1=ecapbig[:],
                             scalar2=None, op0=OP.add)

    lb_ref = [None] * NT
    sc_refs = [[None] * NE for _ in range(NT)]
    for j in range(NT):
        bj = j % 2
        t.wait(r_mask)
        if j >= 1:
            t.wait(lb_ref[j - 1])
        tr1 = t.transpose(psT1[:, 0:8], offT[:, j * 128:(j + 1) * 128], idn[0:8, 0:8])
        tr2 = t.transpose(psT1[:, 8:16], gateT[:, j * 128:(j + 1) * 128], idn[0:8, 0:8])
        v.wait(tr1)
        v.wait(tr2)
        if j >= 2:
            for e in range(NE):
                v.wait(sc_refs[j - 2][e])
        v.tensor_copy(out=offu[bj][:], in_=psT1[:, 0:8])
        s2v = src2[bj][:].rearrange("p (a b) -> p a b", b=2)
        v.tensor_copy(out=s2v[:, :, 1], in_=psT1[:, 8:16])
        lb_ref[j] = v.tensor_copy(out=s2v[:, :, 0],
                                  in_=tokids[:, j:j + 1].to_broadcast([128, 8]))
        gq.wait(lb_ref[j])
        if j == 0:
            gq.wait(r_li)
        for e in range(NE):
            if j >= 1:
                gq.wait(sc_refs[j - 1][e])
            sc_refs[j][e] = gq.indirect(
                f"g{e}", out=lists0,
                out_offset=IndirectOffsetOnAxis(offu[bj][:, e:e + 1], 0),
                in_=src2[bj][:, 2 * e:2 * e + 2], in_offset=None,
                bounds=NE * CAP0 - 1, oob_is_err=False)

    xn_ref = [None] * NT
    st_ref = [None] * NT
    for j in range(NT):
        bj = j % 2
        if j >= 2:
            q.wait(xn_ref[j - 2])
        r_lx = q.dma(f"lx{bj}", out=xt[bj][:], in_=x_own[j * 128:(j + 1) * 128, :])
        v.wait(r_lx)
        if j >= 2:
            v.wait(st_ref[j - 2])
        xn_ref[j] = v.tensor_scalar(out=xnb[bj][:], in0=xt[bj][:],
                                    scalar1=muG[:, j:j + 1], scalar2=rs_all[:, j:j + 1],
                                    op0=OP.subtract, op1=OP.mult)
        q.wait(xn_ref[j])
        st_ref[j] = q.dma(f"sx{bj}", out=xn_out[j * 128:(j + 1) * 128, :], in_=xnb[bj][:])

    fence = [st_ref[NT - 2], st_ref[NT - 1], r_li, r_sth]
    fence += [sc_refs[NT - 1][e] for e in range(NE)]
    fence += [sc_refs[NT - 2][e] for e in range(NE)]
    q.fence(*fence)

    ctx.close()




def emit_stage_b(p, t, v, s, g, q, gq, bank, dr):
    nc = p.nc
    ctx = ExitStack()
    def sb(nm, shape, dt):
        return ctx.enter_context(nc.sbuf_tensor(f'b_'+nm, shape, dt))

    xn_in = dr["xn"]
    x_own = dr["x_own"]
    w_inT = dr["W_inT"]
    w_dtT = dr["W_dtT"]
    w_outT = dr["W_outT"]
    alog_r = dr["A_log_r"]
    bdt_r = dr["b_dt_r"]
    convw_r = dr["convw_r"]
    convb_r = dr["convb_r"]
    d_r = dr["D_r"]
    lng_r = dr["lng_r"]
    lnb_r = dr["lnb_r"]
    bsel_in = dr["bsel"]
    idnb_in = dr["ident_bf"]
    idnf_in = dr["ident_f"]
    onecb_in = dr["onec_bf"]
    onen_in = dr["ones_n"]
    consts_in = dr["consts"]

    x1a_out = dr["x1a"]

    winT = sb("winT", [128, NKI * MW], BF16)
    wdtT = sb("wdtT", [64, D_INNER], BF16)
    wo = [sb(f"wo{i}", [128, D], BF16) for i in range(2)]
    acols = sb("acols", [128, NDI * 16], F32)
    bdt = sb("bdt", [128, NDI], F32)
    cw_ = sb("cw_", [128, NDI * 4], F32)
    cb_ = sb("cb_", [128, NDI], F32)
    dr = sb("dr", [128, NDI], F32)
    lng = sb("lng", [128, NKI], F32)
    lnb = sb("lnb", [128, NKI], F32)
    bsel = sb("bsel_s", [16, 16 * 128], BF16)
    idnb = sb("idnb", [128, 128], BF16)
    idnf = sb("idnf", [128, 128], F32)
    onecb = sb("onecb", [128, 1], BF16)
    onen = sb("onen", [1, 128], F32)

    xnT = sb("xnT", [128, NKI * CW], BF16)
    browb = sb("browb", [16, CW], BF16)
    crowb = sb("crowb", [16, CW], BF16)
    dtlow = sb("dtlow", [64, CW], BF16)
    bb = sb("bb", [128, 16 * CW], BF16)
    cbr = sb("cbr", [128, 16 * CW], BF16)
    xc = sb("xc", [128, CW + 3], F32)
    tails = sb("tails", [128, NDI * 3], F32)
    uu = [sb(f"uu{i}", [128, CW], BF16) for i in range(2)]
    zs = [sb(f"zs{i}", [128, CW], BF16) for i in range(2)]
    dtt = [sb(f"dtt{i}", [128, CW], F32) for i in range(2)]
    sp_ab = sb("sp_ab", [128, CW], F32)
    dtu = sb("dtu", [128, CW], BF16)
    ud = sb("ud", [128, CW], BF16)
    da = [sb(f"da{i}", [128, CW], F32) for i in range(2)]
    bet2 = sb("bet2", [128, 2 * CW], BF16)
    hh2 = [sb(f"hh2_{i}", [128, 2 * CW], F32) for i in range(2)]
    hcm2 = [sb(f"hcm2_{i}", [128, 2 * CW], BF16) for i in range(2)]
    state = sb("state", [128, NDI * 16], F32)
    yz = sb("yz", [128, NDI * CW], BF16)
    att = yz[:, 0:NKI * CW]
    mpre = sb("mpre", [128, NKI * CW], BF16)
    sqt = [sb(f"sqt{i}", [128, CW], BF16) for i in range(2)]
    mu_row = sb("mu_row", [1, CW], F32)
    mu2r = sb("mu2r", [1, CW], F32)
    var_row = sb("var_row", [1, CW], F32)
    rstd_row = sb("rstd_row", [1, CW], F32)
    mr_row = sb("mr_row", [1, CW], F32)
    at1 = sb("at1", [128, CW], F32)
    xtok = sb("xtok", [128, D], F32)
    x1t = [sb(f"x1t{i}", [128, D], F32) for i in range(2)]



    pre = []
    for dst, src in [(acols, alog_r), (bdt, bdt_r), (cw_, convw_r), (cb_, convb_r),
                     (dr, d_r), (lng, lng_r), (lnb, lnb_r), (bsel, bsel_in),
                     (idnb, idnb_in), (idnf, idnf_in), (onecb, onecb_in),
                     (onen, onen_in)]:
        pre.append(q.dma("pre", out=dst[:], in_=src))
    pre_done = pre[-1]
    w1 = gq.dma("win", out=winT[:].rearrange("p (a m) -> p a m", a=NKI),
                in_=w_inT.rearrange("(a p) m -> p a m", p=128))
    w2 = gq.dma("wdt", out=wdtT[:], in_=w_dtT)

    cbn = sb("cbn", [128, NDI], F32)
    s.wait(pre_done)
    r_ae = s.activation(out=acols[:], in_=acols[:], func=AF.Exp)
    v.wait(r_ae)
    v.tensor_scalar(out=acols[:], in0=acols[:], scalar1=-1.0, scalar2=None, op0=OP.mult)
    v.tensor_scalar(out=cbn[:], in0=cb_[:], scalar1=-1.0, scalar2=None, op0=OP.mult)
    v.memset(tails[:], 0.0)
    r_init = v.memset(state[:], 0.0)

    yz_ref = [None] * NDI
    relu_ref = [None] * NDI
    uD_ref = [None] * NDI
    zs_ref = [None] * NDI
    add_bank = {6: None, 7: None}
    prev_mpre_ref = None
    prev_var_ref = None
    prev_add_last = None
    prev_last_hc = None
    x1st_ref = [None, None]
    wo_mm_ref = [None] * NDI
    add_ref = [None] * NKI
    prev_hcm_mm = [None, None]
    scan_hist = [None, None]
    hc_hist = [None, None]

    for c in range(NCHB):
        t0 = c * CW
        xnt_refs = []
        for a in range(NKI):
            if prev_mpre_ref is not None and a == 0:
                q.wait(prev_mpre_ref)
            xnt_refs.append(q.dma_transpose(f"xt{a % 2}",
                                            out=xnT[:, a * CW:(a + 1) * CW],
                                            in_=xn_in[t0:t0 + CW, a * 128:(a + 1) * 128]))

        t.wait(xnt_refs[NKI - 2])
        t.wait(xnt_refs[NKI - 1])
        if c == 0:
            t.wait(w1)
        else:
            t.wait(prev_add_last)
        for k in range(NKI):
            r_m32 = t.matmul(bank[0][:], winT[:, k * MW + 32 * 128:k * MW + 33 * 128],
                             xnT[:, k * CW:(k + 1) * CW],
                             start=(k == 0), stop=(k == NKI - 1))
        v.wait(r_m32)
        if c == 0:
            v.wait(r_init)
        v.tensor_copy(out=browb[:], in_=bank[0][0:16, :])
        v.tensor_copy(out=crowb[:], in_=bank[0][32:48, :])
        r_dtl = v.tensor_copy(out=dtlow[:], in_=bank[0][64:128, :])

        ev_b = ev_c = None
        for n in range(16):
            if n == 0:
                t.wait(r_dtl)
            else:
                t.wait(ev_b)
            r_b1 = t.matmul(bank[2][:], bsel[:, n * 128:(n + 1) * 128], browb[:],
                            start=True, stop=True)
            s.wait(r_b1)
            if n == 0 and prev_last_hc is not None:
                s.wait(prev_last_hc)
            ev_b = s.copy(out=bb[:, n * CW:(n + 1) * CW], in_=bank[2][:])
            if n > 0:
                t.wait(ev_c)
            r_b2 = t.matmul(bank[3][:], bsel[:, n * 128:(n + 1) * 128], crowb[:],
                            start=True, stop=True)
            s.wait(r_b2)
            ev_c = s.copy(out=cbr[:, n * CW:(n + 1) * CW], in_=bank[3][:])
        bc_done = ev_c

        for dt in range(NDI):
            pj = bank[dt % 2]
            py = bank[4 + dt % 2]
            if c > 0 or dt >= 2:
                t.wait(relu_ref[(dt - 2) % NDI])
            if dt == 0:
                t.wait(r_dtl)
            for k in range(NKI):
                r_xs = t.matmul(pj[:], winT[:, k * MW + dt * 128:k * MW + (dt + 1) * 128],
                                xnT[:, k * CW:(k + 1) * CW],
                                start=(k == 0), stop=(k == NKI - 1))
            v.wait(r_xs)
            v.tensor_copy(out=xc[:, 3:3 + CW], in_=pj[:])
            v.tensor_copy(out=xc[:, 0:3], in_=tails[:, dt * 3:dt * 3 + 3])
            v.tensor_copy(out=tails[:, dt * 3:dt * 3 + 3], in_=xc[:, CW:CW + 3])
            v.tensor_scalar(out=at1[:], in0=xc[:, 0:CW],
                            scalar1=cw_[:, dt * 4:dt * 4 + 1], scalar2=None, op0=OP.mult)
            for jj in range(1, 4):
                r_cv = v.scalar_tensor_tensor(out=at1[:], in0=xc[:, jj:jj + CW],
                                              scalar=cw_[:, dt * 4 + jj:dt * 4 + jj + 1],
                                              in1=at1[:], op0=OP.mult, op1=OP.add)
            euv = bet2[:].bitcast(F32)[:, 0:CW]
            s.wait(r_cv)
            if scan_hist[0] is not None:
                s.wait(scan_hist[0])
            if scan_hist[1] is not None:
                s.wait(scan_hist[1])
            s.activation(out=euv, in_=at1[:], func=AF.Exp, scale=-1.0,
                         bias=cbn[:, dt:dt + 1])
            r_s2 = s.add(out=euv, in_=euv, add=1.0)
            v.wait(r_s2)
            v.reciprocal(out=euv, in_=euv)
            v.tensor_scalar(out=ud[:], in0=at1[:], scalar1=cb_[:, dt:dt + 1],
                            scalar2=None, op0=OP.add)
            r_u = v.tensor_tensor(out=uu[dt % 2][:], in0=ud[:], in1=euv, op=OP.mult)
            t.wait(r_cv)
            for k in range(NKI):
                r_zp = t.matmul(pj[:], winT[:, k * MW + (16 + dt) * 128:k * MW + (17 + dt) * 128],
                                xnT[:, k * CW:(k + 1) * CW],
                                start=(k == 0), stop=(k == NKI - 1))
            s.wait(r_zp)
            s.activation(out=sp_ab[:], in_=pj[:], func=AF.Exp, scale=-1.0)
            r_ez2 = s.add(out=sp_ab[:], in_=sp_ab[:], add=1.0)
            v.wait(r_ez2)
            v.reciprocal(out=sp_ab[:], in_=sp_ab[:])
            if c > 0 or dt >= 2:
                v.wait(yz_ref[(dt - 2) % NDI])
            r_zs = v.tensor_tensor(out=zs[dt % 2][:], in0=pj[:], in1=sp_ab[:],
                                   op=OP.mult)
            zs_ref[dt] = r_zs
            t.wait(r_zs)
            if c == 0 and dt == 0:
                t.wait(w2)
            r_dtp = t.matmul(pj[:], wdtT[:, dt * 128:(dt + 1) * 128], dtlow[:],
                             start=True, stop=True)
            s.wait(r_dtp)
            s.activation(out=sp_ab[:], in_=pj[:], func=AF.Abs, bias=bdt[:, dt:dt + 1])
            s.activation(out=sp_ab[:], in_=sp_ab[:], func=AF.Exp, scale=-1.0)
            s.activation(out=sp_ab[:], in_=sp_ab[:], func=AF.Ln, bias=1.0)
            relu_ref[dt] = s.activation(out=at1[:], in_=pj[:], func=AF.Relu,
                                        bias=bdt[:, dt:dt + 1])
            v.wait(relu_ref[dt])
            r_dt2 = v.tensor_tensor(out=dtt[dt % 2][:], in0=sp_ab[:], in1=at1[:], op=OP.add)
            v.wait(r_u)
            v.tensor_tensor(out=dtu[:], in0=dtt[dt % 2][:], in1=uu[dt % 2][:], op=OP.mult)
            uD_ref[dt] = v.tensor_scalar(out=ud[:], in0=uu[dt % 2][:],
                                         scalar1=dr[:, dt:dt + 1], scalar2=None, op0=OP.mult)
            r_ya = [None] * 8
            dtu3 = dtu[:].unsqueeze(1).to_broadcast([128, 2, CW])
            for np_ in range(8):
                n0 = 2 * np_
                pp = np_ % 2
                if np_ == 0:
                    s.wait(r_dt2)
                if scan_hist[0] is not None:
                    s.wait(scan_hist[0])
                r_da0 = s.activation(out=da[0][:], in_=dtt[dt % 2][:], func=AF.Exp,
                                     scale=acols[:, dt * 16 + n0:dt * 16 + n0 + 1])
                if scan_hist[1] is not None:
                    s.wait(scan_hist[1])
                r_da1 = s.activation(out=da[1][:], in_=dtt[dt % 2][:], func=AF.Exp,
                                     scale=acols[:, dt * 16 + n0 + 1:dt * 16 + n0 + 2])
                if hc_hist[pp] is not None:
                    v.wait(hc_hist[pp])
                v.tensor_tensor(out=bet2[:].rearrange("p (a t) -> p a t", a=2),
                                in0=dtu3,
                                in1=bb[:, n0 * CW:(n0 + 2) * CW].rearrange("p (a t) -> p a t", a=2),
                                op=OP.mult)
                v.wait(r_da0)
                scan_hist[0] = v.tensor_tensor_scan(
                    out=hh2[pp][:, 0:CW], data0=da[0][:], data1=bet2[:, 0:CW],
                    initial=state[:, dt * 16 + n0:dt * 16 + n0 + 1],
                    op0=OP.mult, op1=OP.add)
                v.wait(r_da1)
                scan_hist[1] = v.tensor_tensor_scan(
                    out=hh2[pp][:, CW:2 * CW], data0=da[1][:], data1=bet2[:, CW:2 * CW],
                    initial=state[:, dt * 16 + n0 + 1:dt * 16 + n0 + 2],
                    op0=OP.mult, op1=OP.add)
                v.tensor_copy(out=state[:, dt * 16 + n0:dt * 16 + n0 + 2],
                              in_=bass.AP(hh2[pp].tensor, hh2[pp].offset + CW - 1,
                                          [[hh2[pp].stride(0), 128], [CW, 2]])
                              if False else hh2[pp][:].rearrange("p (a t) -> p a t", a=2)[:, :, CW - 1])
                if dt == 0 and np_ == 0:
                    v.wait(bc_done)
                if np_ >= 2:
                    v.wait(r_ya[np_ - 2])
                elif prev_hcm_mm[np_] is not None:
                    v.wait(prev_hcm_mm[np_])
                r_hc = v.tensor_tensor(out=hcm2[pp][:], in0=hh2[pp][:],
                                       in1=cbr[:, n0 * CW:(n0 + 2) * CW], op=OP.mult)
                hc_hist[pp] = r_hc
                t.wait(r_hc)
                if np_ == 0 and (c > 0 or dt >= 2):
                    t.wait(yz_ref[(dt - 2) % NDI])
                t.matmul(py[:], idnb[:], hcm2[pp][:, 0:CW], start=(np_ == 0), stop=False)
                r_ya[np_] = t.matmul(py[:], idnb[:], hcm2[pp][:, CW:2 * CW],
                                     start=False, stop=False)
            prev_hcm_mm = [r_ya[6], r_ya[7]]
            prev_last_hc = r_hc
            t.wait(uD_ref[dt])
            r_yacc = t.matmul(py[:], idnb[:], ud[:], start=False, stop=True)
            v.wait(r_yacc)
            yz_ref[dt] = v.tensor_tensor(out=yz[:, dt * CW:(dt + 1) * CW],
                                         in0=py[:], in1=zs[dt % 2][:], op=OP.mult)

        for k in range(NDI):
            kp = k % 2
            if c > 0 or k >= 2:
                gq.wait(wo_mm_ref[(k - 2) % NDI])
            r_wl = gq.dma(f"wo{kp}", out=wo[kp][:], in_=w_outT[k * 128:(k + 1) * 128, :])
            t.wait(r_wl)
            t.wait(yz_ref[k])
            if k == 0:
                t.wait(yz_ref[NDI - 1])
            for o in range(NKI):
                r_wo = t.matmul(bank[o][:], wo[kp][:, o * 128:(o + 1) * 128],
                                yz[:, k * CW:(k + 1) * CW],
                                start=(k == 0), stop=(k == NDI - 1))
            wo_mm_ref[k] = r_wo
        wout_done = r_wo

        v.wait(wout_done)
        for o in range(NKI):
            r_mp = v.tensor_tensor(out=mpre[:, o * CW:(o + 1) * CW],
                                   in0=bank[o][:], in1=xnT[:, o * CW:(o + 1) * CW],
                                   op=OP.add)
        prev_mpre_ref = r_mp

        ts2 = [None] * NKI
        for o in range(NKI):
            if o == 0:
                s.wait(r_mp)
            elif o >= 2:
                s.wait(ts2[o - 2])
            r_sq = s.activation(out=sqt[o % 2][:], in_=mpre[:, o * CW:(o + 1) * CW],
                                func=AF.Square)
            t.wait(r_sq)
            if o == 0:
                t.wait(r_mp)
            t.matmul(bank[0][0:1, :], onecb[:], mpre[:, o * CW:(o + 1) * CW],
                     start=(o == 0), stop=(o == NKI - 1))
            ts2[o] = t.matmul(bank[1][0:1, :], onecb[:], sqt[o % 2][:],
                              start=(o == 0), stop=(o == NKI - 1))
        v.wait(ts2[NKI - 1])
        v.tensor_scalar(out=mu_row[:], in0=bank[0][0:1, :], scalar1=1.0 / D,
                        scalar2=None, op0=OP.mult)
        v.tensor_tensor(out=mu2r[:], in0=mu_row[:], in1=mu_row[:], op=OP.mult)
        r_var = v.scalar_tensor_tensor(out=var_row[:], in0=bank[1][0:1, :],
                                       scalar=1.0 / D, in1=mu2r[:],
                                       op0=OP.mult, op1=OP.subtract)
        prev_var_ref = r_var
        s.wait(r_var)
        s.activation(out=rstd_row[:], in_=var_row[:], func=AF.Ln, bias=LN_EPS)
        r_rstd = s.activation(out=rstd_row[:], in_=rstd_row[:], func=AF.Exp, scale=-0.5)
        v.wait(r_rstd)
        r_mr = v.tensor_tensor(out=mr_row[:], in0=mu_row[:], in1=rstd_row[:], op=OP.mult)
        t.wait(r_mr)
        t.wait(r_var)
        r_bc1 = t.matmul(bank[2][:], onen[:], rstd_row[:], start=True, stop=True)
        r_bc2 = t.matmul(bank[3][:], onen[:], mr_row[:], start=True, stop=True)
        v.wait(r_bc2)
        for o in range(NKI):
            v.tensor_tensor(out=at1[:], in0=mpre[:, o * CW:(o + 1) * CW],
                            in1=bank[2][:], op=OP.mult)
            v.tensor_tensor(out=at1[:], in0=at1[:], in1=bank[3][:], op=OP.subtract)
            r_att = v.tensor_scalar(out=att[:, o * CW:(o + 1) * CW], in0=at1[:],
                                    scalar1=lng[:, o:o + 1], scalar2=lnb[:, o:o + 1],
                                    op0=OP.mult, op1=OP.add)
        att_done = r_att

        prev_lx = None
        for s4 in range(4):
            sp = s4 % 2
            if x1st_ref[sp] is not None:
                q.wait(x1st_ref[sp])
            if prev_lx is not None:
                q.wait(add_ref[NKI - 1])
            r_lx = q.dma(f"x{sp}", out=xtok[:],
                         in_=x_own[t0 + s4 * 128:t0 + (s4 + 1) * 128, :])
            prev_lx = r_lx
            for o in range(NKI):
                bi = 6 + o % 2
                pt = bank[bi]
                if o == 0:
                    t.wait(att_done)
                if add_bank[bi] is not None:
                    t.wait(add_bank[bi])
                ptb = pt[:].bitcast(BF16)
                r_tr = t.transpose(ptb[:, 0:128],
                                   att[:, o * CW + s4 * 128:o * CW + (s4 + 1) * 128],
                                   idnb[:])
                v.wait(r_tr)
                v.wait(r_lx)
                add_ref[o] = v.tensor_tensor(out=x1t[sp][:, o * 128:(o + 1) * 128],
                                             in0=ptb[:, 0:128],
                                             in1=xtok[:, o * 128:(o + 1) * 128], op=OP.add)
                add_bank[bi] = add_ref[o]
            q.wait(add_ref[NKI - 1])
            x1st_ref[sp] = q.dma(f"sx{sp}", out=x1a_out[t0 + s4 * 128:t0 + (s4 + 1) * 128, :],
                                 in_=x1t[sp][:])
        prev_add_last = add_ref[NKI - 1]

    q.fence(x1st_ref[0], x1st_ref[1])
    ctx.close()




def emit_stage_c(p, t, v, s, g, q, gq, bank, dr):
    nc = p.nc
    ctx = ExitStack()
    def sb(nm, shape, dt):
        return ctx.enter_context(nc.sbuf_tensor(f'c_'+nm, shape, dt))

    xn_in = dr["xn"]
    x1a_in = dr["x1a"]
    lists_in = dr["lists0"]
    w1_in = dr["W1"]
    b1_in = dr["b1_r"]
    w2_in = dr["W2"]
    b2_in = dr["b2_r"]
    idnb_in = dr["ident_bf"]

    x1_out = dr["x1"]

    idnb_c = sb("idnb_c", [128, 128], BF16)
    lraw = sb("lraw", [128, NTPE * NE * 2], F32)
    idsu = sb("idsu", [128, NTPE * NE], U32)
    gts = sb("gts", [128, NTPE * NE], F32)
    b1s = sb("b1s", [128, NE * NFT], F32)
    b2s = sb("b2s", [128, NE * NKI], F32)
    w1sb = sb("w1sb", [128, NKI * FF], BF16)
    w2sb = sb("w2sb", [128, NFT * D], BF16)
    gx = [sb(f"gx{i}", [128, D], BF16) for i in range(2)]
    exT = [sb(f"exT{i}", [128, NKI * 512], BF16) for i in range(2)]
    hidT = sb("hidT", [128, NFT * 512], BF16)
    outT = sb("outT", [128, NKI * 512], BF16)
    otok = [sb(f"otok{i}", [128, D], F32) for i in range(2)]



    xcp = []
    for i in range(4):
        xcp.append(q.dma(f"xc{i}", out=x1_out[i * T // 4:(i + 1) * T // 4, :],
                         in_=x1a_in[i * T // 4:(i + 1) * T // 4, :]))

    r_id = q.dma("pre", out=idnb_c[:], in_=idnb_in)
    r_lr = q.dma("pre", out=lraw[:].rearrange("p (a n) -> p a n", n=2),
                 in_=lists_in.rearrange("(a p) n -> p a n", p=128))
    q.dma("pre", out=b1s[:], in_=b1_in)
    pre_done = q.dma("pre", out=b2s[:], in_=b2_in)

    v.wait(pre_done)
    lv = lraw[:].rearrange("p (a n) -> p a n", n=2)
    r_ids = v.tensor_copy(out=idsu[:], in_=lv[:, :, 0])
    r_gts = v.tensor_copy(out=gts[:], in_=lv[:, :, 1])

    w1_ref = gq.dma("w1", out=w1sb[:].rearrange("p (a f) -> p a f", a=NKI),
                    in_=w1_in[0].rearrange("(a p) f -> p a f", p=128))
    w2_ref = gq.dma("w2", out=w2sb[:].rearrange("p (a o) -> p a o", a=NFT),
                    in_=w2_in[0].rearrange("(a p) o -> p a o", p=128))

    r_bc = None
    prev_scat = None
    prev_gather = {}
    prev_exT_use = [None, None]
    prev_w1_last = None
    prev_w2_last = None
    prev_otok_use = [None, None]
    tr_bank = {6: None, 7: None}
    otr_bank = {4: None, 5: None}
    r_hid_done_prev = None

    for e in range(NE):
        g0 = 0
        for gi, gnt in enumerate(GRP):
            gp = (e * 3 + gi) % 2
            nslots = gnt * 128
            ex_done = None
            for jj in range(gnt):
                a = e * NTPE + g0 + jj
                bj = jj % 2
                if prev_gather.get(bj) is not None:
                    v.wait(prev_gather[bj])
                r_ms = v.memset(gx[bj][:], 0.0)
                gq.wait(r_ms)
                gq.wait(r_ids)
                r_g = gq.indirect(f"ga{bj}", out=gx[bj][:], out_offset=None,
                                  in_=xn_in,
                                  in_offset=IndirectOffsetOnAxis(idsu[:, a:a + 1], 0),
                                  bounds=T - 1, oob_is_err=False)
                for k in range(NKI):
                    bi = 6 + k % 2
                    ptb = bank[bi][:].bitcast(BF16)
                    t.wait(r_g)
                    if e == 0 and gi == 0 and jj == 0 and k == 0:
                        t.wait(pre_done)
                    if tr_bank[bi] is not None:
                        t.wait(tr_bank[bi])
                    r_tr = t.transpose(ptb[:, 0:128], gx[bj][:, k * 128:(k + 1) * 128],
                                       idnb_c[:])
                    v.wait(r_tr)
                    if prev_exT_use[gp] is not None and jj == 0 and k == 0:
                        v.wait(prev_exT_use[gp])
                    r_ev = v.tensor_copy(out=exT[gp][:, k * 512 + jj * 128:k * 512 + jj * 128 + 128],
                                         in_=ptb[:, 0:128])
                    tr_bank[bi] = r_ev
                prev_gather[bj] = r_ev
                ex_done = r_ev
            r_gelu = None
            for ft in range(NFT):
                pb = bank[ft % 2]
                t.wait(ex_done)
                if ft == 0:
                    t.wait(w1_ref)
                    if r_hid_done_prev is not None:
                        t.wait(r_hid_done_prev)
                if ft >= 2:
                    t.wait(r_gelu_prev2)
                for k in range(NKI):
                    r_w1 = t.matmul(pb[:, 0:nslots],
                                    w1sb[:, k * FF + ft * 128:k * FF + (ft + 1) * 128],
                                    exT[gp][:, k * 512:k * 512 + nslots],
                                    start=(k == 0), stop=(k == NKI - 1))
                s.wait(r_w1)
                r_gelu_prev2 = r_gelu if ft >= 1 else None
                r_gelu = s.activation(out=hidT[:, ft * 512:ft * 512 + nslots],
                                      in_=pb[:, 0:nslots], func=AF.Gelu,
                                      bias=b1s[:, e * NFT + ft:e * NFT + ft + 1])
                if ft == NFT - 1:
                    prev_w1_last = r_w1
            prev_exT_use[gp] = prev_w1_last
            if gi == len(GRP) - 1 and e < NE - 1:
                gq.wait(prev_w1_last)
                w1_ref = gq.dma("w1", out=w1sb[:].rearrange("p (a f) -> p a f", a=NKI),
                                in_=w1_in[e + 1].rearrange("(a p) f -> p a f", p=128))
            r_ot = None
            for o in range(NKI):
                pb = bank[2 + o % 2]
                if o == 0:
                    t.wait(r_gelu)
                    t.wait(w2_ref)
                if o >= 2:
                    t.wait(r_ot_prev2)
                for k in range(NFT):
                    r_w2 = t.matmul(pb[:, 0:nslots],
                                    w2sb[:, k * D + o * 128:k * D + (o + 1) * 128],
                                    hidT[:, k * 512:k * 512 + nslots],
                                    start=(k == 0), stop=(k == NFT - 1))
                v.wait(r_w2)
                r_ot_prev2 = r_ot if o >= 1 else None
                r_ot = v.tensor_scalar(out=outT[:, o * 512:o * 512 + nslots],
                                       in0=pb[:, 0:nslots],
                                       scalar1=b2s[:, e * NKI + o:e * NKI + o + 1],
                                       scalar2=None, op0=OP.add)
                if o == NKI - 1:
                    prev_w2_last = r_w2
            r_hid_done_prev = prev_w2_last
            if gi == len(GRP) - 1 and e < NE - 1:
                gq.wait(prev_w2_last)
                w2_ref = gq.dma("w2", out=w2sb[:].rearrange("p (a o) -> p a o", a=NFT),
                                in_=w2_in[e + 1].rearrange("(a p) o -> p a o", p=128))
            for jj in range(gnt):
                a = e * NTPE + g0 + jj
                op_ = jj % 2
                for o in range(NKI):
                    bi = 4 + o % 2
                    ptb = bank[bi][:].bitcast(BF16)
                    t.wait(r_ot)
                    if otr_bank[bi] is not None:
                        t.wait(otr_bank[bi])
                    r_tr2 = t.transpose(ptb[:, 0:128],
                                        outT[:, o * 512 + jj * 128:o * 512 + jj * 128 + 128],
                                        idnb_c[:])
                    v.wait(r_tr2)
                    if o == 0 and prev_otok_use[op_] is not None:
                        v.wait(prev_otok_use[op_])
                    r_go = v.tensor_scalar(out=otok[op_][:, o * 128:(o + 1) * 128],
                                           in0=ptb[:, 0:128],
                                           scalar1=gts[:, a:a + 1], scalar2=None,
                                           op0=OP.mult)
                    otr_bank[bi] = r_go
                gq.wait(r_go)
                if prev_scat is not None:
                    gq.wait(prev_scat)
                else:
                    for r in xcp:
                        gq.wait(r)
                prev_scat = gq.indirect(
                    "sc0", out=x1_out, out_offset=IndirectOffsetOnAxis(idsu[:, a:a + 1], 0),
                    in_=otok[op_][:], in_offset=None,
                    bounds=T - 1, oob_is_err=False, compute_op=OP.add)
                prev_otok_use[op_] = prev_scat
            g0 += gnt

    q.fence(prev_scat)
    ctx.close()




def emit_stage_d(p, t, v, s, g, q, gq, bank, dr):
    nc = p.nc
    ctx = ExitStack()
    def sb(nm, shape, dt):
        return ctx.enter_context(nc.sbuf_tensor(f'd_'+nm, shape, dt))

    x1_in = dr["x1"]
    wqkv_in = dr["Wqkv"]
    wo_in = dr["Wo"]
    bo_in = dr["bo_r"]
    cosT_in = dr["cosT"]
    sinT_in = dr["sinT"]
    p2_in = dr["P2"]
    masks_in = dr["masks"]
    onen_in = dr["ones_n"]
    consts_in = dr["consts"]

    xn1_out = dr["xn1"]
    attn1T_out = dr["attn1T"]

    qT_d = dr["qT_d"]
    kT_d = dr["kT_d"]
    asc_d = dr["asc_d"]

    wq = [sb(f"wq{i}", [128, NKI * 128], BF16) for i in range(2)]
    wv = sb("wv", [128, NKI * D], BF16)
    bo = sb("bo", [128, NKI], F32)
    cosT = sb("cosT_s", [128, T], BF16)
    sinT = sb("sinT_s", [128, T], BF16)
    p2 = sb("p2", [128, 128], BF16)
    msks = sb("msks", [128, 4 * 512], BF16)
    onen = sb("onen", [1, 128], F32)

    xt = [sb(f"xt{i}", [128, D], F32) for i in range(2)]
    st6 = sb("st6", [128, 12], F32)
    mv2 = sb("mv2", [128, 2], F32)
    rs_c = sb("rs_c", [128, 1], F32)
    xnb = [sb(f"xnb{i}", [128, D], BF16) for i in range(2)]

    xnT = sb("xnTd", [128, NKI * 512], BF16)
    qktmp = sb("qktmp", [128, 512], BF16)
    ro1 = sb("ro1", [128, 512], BF16)
    ro2 = sb("ro2", [128, 512], BF16)
    qstr = sb("qstr", [128, NKI * 512], BF16)
    ah = qstr[0:64, 0:NKI * 512]
    kstr = sb("kstr", [128, NKI * 512], BF16)
    vab = sb("vab", [128, 32 * NH * (HD + 1)], BF16)
    kbuf = [sb(f"kbuf{i}", [64, T], BF16) for i in range(2)]
    qh = [sb(f"qh{i}", [64, T], BF16) for i in range(2)]
    eT = [sb(f"eT{i}", [128, 512], BF16) for i in range(4)]
    avs = sb("avs", [64, 512], F32)
    zr = sb("zr", [1, 512], F32)
    aTc = sb("aTc", [128, NKI * 512], BF16)
    a1ev = [sb(f"a1ev{i}", [128, 512], F32) for i in range(2)]

    wo2 = [sb(f"wo2_{i}", [128, NKI * 128], BF16) for i in range(2)]



    pre = []
    for dst, src in [(bo, bo_in), (cosT, cosT_in), (sinT, sinT_in), (p2, p2_in),
                     (msks, masks_in), (onen, onen_in)]:
        pre.append(q.dma("pre", out=dst[:], in_=src))
    pre_done = pre[-1]
    r_wv = gq.dma("wv", out=wv[:].rearrange("p (a m) -> p a m", a=NKI),
                  in_=wqkv_in[:, 2 * D:3 * D].rearrange("(a p) m -> p a m", p=128))

    r_vz = v.memset(vab[:], 1.0)

    st_ref = [None] * NT
    xn_ref = [None] * NT
    for j in range(NT):
        bj = j % 2
        if j >= 2:
            q.wait(xn_ref[j - 2])
        r_lx = q.dma(f"lx{bj}", out=xt[bj][:], in_=x1_in[j * 128:(j + 1) * 128, :])
        v.wait(r_lx)
        if j == 0:
            v.wait(pre_done)
        v.bn_stats(out=st6[:, 0:6], in_=xt[bj][:, 0:512])
        v.bn_stats(out=st6[:, 6:12], in_=xt[bj][:, 512:1024])
        r_mv = v.bn_aggr(out=mv2[:], in_=st6[:])
        s.wait(r_mv)
        s.activation(out=rs_c[:], in_=mv2[:, 1:2], func=AF.Ln, bias=LN_EPS)
        r_rs = s.activation(out=rs_c[:], in_=rs_c[:], func=AF.Exp, scale=-0.5)
        v.wait(r_rs)
        if j >= 2:
            v.wait(st_ref[j - 2])
        xn_ref[j] = v.tensor_scalar(out=xnb[bj][:], in0=xt[bj][:],
                                    scalar1=mv2[:, 0:1], scalar2=rs_c[:],
                                    op0=OP.subtract, op1=OP.mult)
        q.wait(xn_ref[j])
        st_ref[j] = q.dma(f"sx{bj}", out=xn1_out[j * 128:(j + 1) * 128, :], in_=xnb[bj][:])

    r_q_store = r_k_store = None
    wq_ref = [None, None]
    wq_mm = [None, None]
    r_rope = None
    krot_use = None
    vab_ev = None
    for c in range(NCHB):
        t0 = c * CW
        xnt_refs = []
        for a in range(NKI):
            if a == 0:
                q.wait(st_ref[4 * c + 2])
                q.wait(st_ref[4 * c + 3])
                if c > 0:
                    q.wait(vab_ev)
                    q.wait(r_k_store)
            xnt_refs.append(q.dma_transpose(f"xc{a % 2}",
                                            out=xnT[:, a * CW:(a + 1) * CW],
                                            in_=xn1_out[t0:t0 + CW, a * 128:(a + 1) * 128]))
        xnt_done2 = xnt_refs[NKI - 2]
        xnt_done = xnt_refs[NKI - 1]
        for mi in range(16):
            mp = mi % 2
            if wq_mm[mp] is not None:
                gq.wait(wq_mm[mp])
            wq_ref[mp] = gq.dma(f"wq{mp}",
                                out=wq[mp][:].rearrange("p (a m) -> p a m", a=NKI),
                                in_=wqkv_in[:, mi * 128:(mi + 1) * 128].rearrange("(a p) m -> p a m", p=128))
            pb = bank[mp]
            t.wait(wq_ref[mp])
            t.wait(xnt_done)
            t.wait(xnt_done2)
            if r_rope is not None:
                t.wait(r_rope)
            for k in range(NKI):
                r_qk = t.matmul(pb[:], wq[mp][:, k * 128:(k + 1) * 128],
                                xnT[:, k * CW:(k + 1) * CW],
                                start=(k == 0), stop=(k == NKI - 1))
            wq_mm[mp] = r_qk
            s.wait(r_qk)
            if mi < 8:
                r_ev = s.activation(out=qktmp[:], in_=pb[:], func=AF.Copy, scale=0.125)
            else:
                r_ev = s.activation(out=qktmp[:], in_=pb[:], func=AF.Copy)
            t.wait(r_ev)
            if krot_use is not None:
                t.wait(krot_use)
            r_kr = t.matmul(bank[2][:], p2[:], qktmp[:], start=True, stop=True)
            v.wait(r_kr)
            dst = qstr if mi < 8 else kstr
            blk8 = mi % 8
            if mi < 8 and c >= 0 and r_q_store is not None and mi == 0:
                v.wait(r_q_store)
            if mi == 8 and r_k_store is not None:
                v.wait(r_k_store)
            v.tensor_tensor(out=ro1[:], in0=qktmp[:], in1=cosT[:, t0:t0 + CW], op=OP.mult)
            v.tensor_tensor(out=ro2[:], in0=bank[2][:], in1=sinT[:, t0:t0 + CW], op=OP.mult)
            krot_use = r_rope = v.tensor_tensor(out=dst[:, blk8 * CW:(blk8 + 1) * CW],
                                                in0=ro1[:], in1=ro2[:], op=OP.add)
            if mi == 7:
                q.wait(r_rope)
                r_q_store = q.dma("qs", out=qT_d[:, t0:t0 + CW].rearrange("(a p) t -> p a t", p=128),
                                  in_=qstr[:].rearrange("p (a t) -> p a t", a=NKI))
            if mi == 15:
                q.wait(r_rope)
                r_k_store = q.dma("ks", out=kT_d[:, t0:t0 + CW].rearrange("(a p) t -> p a t", p=128),
                                  in_=kstr[:].rearrange("p (a t) -> p a t", a=NKI))
        for tt in range(4):
            for nch in range(2):
                pb = bank[4 + nch]
                if vab_ev is not None and not (c == 0 and tt == 0):
                    t.wait(vab_ev)
                t.wait(r_wv)
                for k in range(NKI):
                    r_vmm = t.matmul(pb[:], xnT[:, k * CW + tt * 128:k * CW + (tt + 1) * 128],
                                     wv[:, k * D + nch * 512:k * D + (nch + 1) * 512],
                                     start=(k == 0), stop=(k == NKI - 1))
                v.wait(r_vmm)
                if c == 0 and tt == 0 and nch == 0:
                    v.wait(r_vz)
                itile = 4 * c + tt
                for hh in range(8):
                    h = nch * 8 + hh
                    vab_ev = v.tensor_copy(
                        out=vab[:, itile * NH * (HD + 1) + h * (HD + 1):
                                itile * NH * (HD + 1) + h * (HD + 1) + HD],
                        in_=pb[:, hh * HD:(hh + 1) * HD])

    ah_store = [None] * NH
    prev_avev = None
    r_cp_prev = None
    last_smm = [None, None]
    av_hist = {}
    exp_hist = {}
    for h in range(NH):
        hp = h % 2
        if last_smm[hp] is not None:
            q.wait(last_smm[hp])
        if h == 0:
            q.wait(r_k_store)
            q.wait(r_q_store)
        r_kb = q.dma(f"kb{hp}", out=kbuf[hp][:], in_=kT_d[64 * h:64 * h + 64, :])
        r_qb = q.dma(f"qb{hp}", out=qh[hp][:], in_=qT_d[64 * h:64 * h + 64, :])
        for jq in range(8):
            nlive = 4 * jq + 4
            r_av = None
            r_e_q = [None] * nlive
            for i in range(nlive + 1):
                if i < nlive:
                    pb = bank[i % 4]
                    if h == 0 and jq == 0 and i == 0:
                        t.wait(r_ev)
                        t.wait(vab_ev)
                    if i < 4:
                        t.wait(r_kb)
                        t.wait(r_qb)
                    if exp_hist.get(i % 4) is not None:
                        t.wait(exp_hist[i % 4])
                    r_s = t.matmul(pb[:], kbuf[hp][:, i * 128:(i + 1) * 128],
                                   qh[hp][:, jq * 512:(jq + 1) * 512],
                                   start=True, stop=True)
                    s.wait(r_s)
                    if av_hist.get(i % 4) is not None:
                        s.wait(av_hist[i % 4])
                    r_e = s.activation(out=eT[i % 4][:], in_=pb[:], func=AF.Exp,
                                       bias=EXP_SHIFT)
                    exp_hist[i % 4] = r_e
                    if i >= 4 * jq:
                        v.wait(r_e)
                        r_e = v.tensor_tensor(out=eT[i % 4][:], in0=eT[i % 4][:],
                                              in1=msks[:, (i - 4 * jq) * 512:(i - 4 * jq + 1) * 512],
                                              op=OP.mult)
                        exp_hist[i % 4] = r_e
                    r_e_q[i] = r_e
                j2 = i - 1
                if j2 >= 0:
                    t.wait(r_e_q[j2])
                    if j2 == 0:
                        if r_cp_prev is not None:
                            t.wait(r_cp_prev)
                        if prev_avev is not None:
                            t.wait(prev_avev)
                    r_av = t.matmul(bank[4][0:65, :],
                                    vab[:, j2 * NH * (HD + 1) + h * (HD + 1):
                                        j2 * NH * (HD + 1) + (h + 1) * (HD + 1)],
                                    eT[j2 % 4][:],
                                    start=(j2 == 0), stop=(j2 == nlive - 1))
                    av_hist[j2 % 4] = r_av
            last_smm[hp] = r_s
            v.wait(r_av)
            if jq == 0 and h >= 1:
                v.wait(ah_store[h - 1])
            r_rc = v.reciprocal(out=zr[:], in_=bank[4][64:65, :])
            r_cp_prev = v.tensor_copy(out=avs[:], in_=bank[4][0:64, :])
            t.wait(r_rc)
            r_zb = t.matmul(bank[5][0:64, :], onen[0:1, 0:64], zr[:],
                            start=True, stop=True)
            v.wait(r_zb)
            if h == 0 and jq == 0:
                v.wait(r_q_store)
            prev_avev = v.tensor_tensor(out=ah[:, jq * CW:(jq + 1) * CW],
                                        in0=avs[:], in1=bank[5][0:64, :], op=OP.mult)
        q.wait(prev_avev)
        ah_store[h] = q.dma(f"ah{hp}", out=asc_d[64 * h:64 * h + 64, :], in_=ah)

    r_a1 = None
    a1_store = [None, None]
    prev_wo_mm = [None, None]
    prev_wo_ld = [None, None]
    for jq in range(8):
        t0 = jq * CW
        if jq == 0:
            for hh in range(NH):
                q.wait(ah_store[hh])
        if prev_wo_mm[0] is not None:
            q.wait(prev_wo_mm[0])
        r_ac = q.dma(f"ac{jq % 2}", out=aTc[:].rearrange("p (a t) -> p a t", a=NKI),
                     in_=asc_d[:, t0:t0 + CW].rearrange("(a p) t -> p a t", p=128))
        for o in range(NKI):
            pb = bank[o % 2]
            if prev_wo_ld[o % 2] is not None:
                gq.wait(prev_wo_ld[o % 2])
            r_wol = gq.dma(f"wol{o % 2}",
                           out=wo2[o % 2][:].rearrange("p (a m) -> p a m", a=NKI),
                           in_=wo_in[:, o * 128:(o + 1) * 128].rearrange("(a p) m -> p a m", p=128))
            t.wait(r_ac)
            t.wait(r_wol)
            if r_a1 is not None:
                t.wait(r_a1)
            for k in range(NKI):
                r_womm = t.matmul(pb[:], wo2[o % 2][:, k * 128:(k + 1) * 128],
                                  aTc[:, k * CW:(k + 1) * CW],
                                  start=(k == 0), stop=(k == NKI - 1))
            prev_wo_ld[o % 2] = r_womm
            v.wait(r_womm)
            if a1_store[o % 2] is not None:
                v.wait(a1_store[o % 2])
            r_a1 = v.tensor_scalar(out=a1ev[o % 2][:], in0=pb[:],
                                   scalar1=bo[:, o:o + 1], scalar2=None, op0=OP.add)
            q.wait(r_a1)
            a1_store[o % 2] = q.dma(f"a1{o % 2}",
                                    out=attn1T_out[o * 128:(o + 1) * 128, t0:t0 + CW],
                                    in_=a1ev[o % 2][:])
            if o == NKI - 1:
                prev_wo_mm[0] = r_womm


    q.fence(a1_store[0], a1_store[1], st_ref[NT - 2], st_ref[NT - 1])
    ctx.close()




def emit_stage_e(p, t, v, s, g, q, gq, bank, dr):
    nc = p.nc
    ctx = ExitStack()
    def sb(nm, shape, dt):
        return ctx.enter_context(nc.sbuf_tensor(f'e_'+nm, shape, dt))

    x1_in = dr["x1"]
    xn1_in = dr["xn1"]
    attn1T_in = dr["attn1T"]
    nwbc_in = dr["nw_bc"]
    rwT_in = dr["router_wT"]
    sw1_in = dr["sw1"]
    sb1_in = dr["sb1_r"]
    sw2_in = dr["sw2"]
    sb2_in = dr["sb2_r"]
    ew1_in = dr["ew1"]
    eb1_in = dr["eb1_r"]
    ew2_in = dr["ew2"]
    eb2_in = dr["eb2_r"]
    idnf_in = dr["ident_f"]
    tokid_in = dr["tokid"]
    ecapbig_in = dr["ecapbig1"]
    li_in = dr["lists_init1"]
    onen_in = dr["ones_n"]
    consts_in = dr["consts"]

    out_f = dr["out"]

    normed_d = dr["normed_d"]
    lists1 = dr["lists1"]

    nwbc = sb("nwbc", [128, D], BF16)
    rwT = sb("rwT", [128, NKI * NE], BF16)
    sb1 = sb("sb1", [128, NFT], F32)
    sb2 = sb("sb2", [128, NKI], F32)
    eb1 = sb("eb1", [128, NE * NFT], F32)
    eb2 = sb("eb2", [128, NE * NKI], F32)
    idnf = sb("idnf_e", [128, 128], F32)
    idnb = sb("idnb_e", [128, 128], BF16)
    tokids = sb("tokids_e", [128, NT], F32)
    ecapbig = sb("ecapbig_e", [8, 1], F32)

    sqa = sb("sqa", [128, 1], F32)
    rs_c = sb("rs_ce", [128, 1], F32)

    nT = sb("nT", [128, NKI * 512], BF16)
    exT = [nT] * 2
    wbig1 = sb("wbig1", [128, NKI * FF], BF16)
    wbig2 = sb("wbig2", [128, NFT * D], BF16)
    hidc = sb("hidc", [128, NFT * 512], BF16)
    a1c = [sb(f"a1c{i}", [128, 512], F32) for i in range(2)]
    comb = sb("comb", [128, NKI * 512], F32)
    combb = comb[:].bitcast(BF16)
    xt = [combb[:, 0:D], combb[:, D:2 * D]]
    nrm = [combb[:, 2 * D:3 * D], combb[:, 3 * D:4 * D]]

    xtok = sb("xtok_e", [128, D], F32)
    otok = [sb(f"otok_e{i}", [128, D], F32) for i in range(2)]
    hidcF = hidc[:].bitcast(F32)
    gateC = hidcF[0:8, 0:1024]
    maskC = hidcF[0:8, 1024:2048]
    posC = hidcF[0:8, 2048:3072]
    carry = sb("carry", [8, 1], F32)
    scTc = sb("scTc", [8, 512], F32)
    ones8c = sb("ones8c", [8, 1], F32)
    sctok = sb("sctok", [128, 8], F32)
    top8 = sb("top8", [128, 8], F32)
    den = sb("den", [128, 1], F32)
    msk8 = sb("msk8", [128, 8], F32)
    gtok = sb("gtok", [128, NT * 8], F32)
    offu = [sb("offu_e0", [128, 8], U32)] * 2
    src2 = [sb("src2_e0", [128, 16], F32)] * 2

    lraw = xtok[:, 0:NTPE1 * NE * 2]
    idsu = xtok[:, 256:256 + NTPE1 * NE].bitcast(U32)
    gts = xtok[:, 384:384 + NTPE1 * NE]
    gx = [a1c[0][:].bitcast(BF16), a1c[1][:].bitcast(BF16)]




    pre = []
    for dst, src in [(nwbc, nwbc_in), (sb1, sb1_in), (sb2, sb2_in), (eb1, eb1_in),
                     (eb2, eb2_in), (idnf, idnf_in), (tokids, tokid_in),
                     (ecapbig, ecapbig_in)]:
        pre.append(q.dma("pre", out=dst[:], in_=src))
    pre_done = pre[-1]
    r_rw = gq.dma("rw", out=rwT[:].rearrange("p (a e) -> p a e", a=NKI),
                  in_=rwT_in.rearrange("(a p) e -> p a e", p=128))
    r_sw1 = gq.dma("w1", out=wbig1[:].rearrange("p (a f) -> p a f", a=NKI),
                   in_=sw1_in.rearrange("(a p) f -> p a f", p=128))
    r_idb = gq.dma("idb", out=idnb[:], in_=idnf_in)
    r_li = q.dma("sli", out=lists1, in_=li_in)

    st_ref = [None] * NT
    nm_ref = [None] * NT
    for j in range(NT):
        bj = j % 2
        if j >= 2:
            q.wait(nm_ref[j - 2])
        r_lx = q.dma(f"lx{bj}", out=xt[bj], in_=xn1_in[j * 128:(j + 1) * 128, :])
        s.wait(r_lx)
        if j == 0:
            s.wait(pre_done)
        if j >= 2:
            s.wait(nm_ref[j - 2])
        s.activation(out=exT[0][:, 0:D], in_=xt[bj], func=AF.Square, accum_out=sqa[:])
        s.activation(out=rs_c[:], in_=sqa[:], func=AF.Ln, bias=RMS_EPS, scale=1.0 / D)
        r_rs = s.activation(out=rs_c[:], in_=rs_c[:], func=AF.Exp, scale=-0.5)
        v.wait(r_rs)
        if j >= 2:
            v.wait(st_ref[j - 2])
        v.tensor_scalar(out=nrm[bj], in0=xt[bj], scalar1=rs_c[:], scalar2=None,
                        op0=OP.mult)
        nm_ref[j] = v.tensor_tensor(out=nrm[bj], in0=nrm[bj], in1=nwbc[:],
                                    op=OP.mult)
        q.wait(nm_ref[j])
        st_ref[j] = q.dma(f"sn{bj}", out=normed_d[j * 128:(j + 1) * 128, :], in_=nrm[bj])

    sc_tr = [None] * NT
    r_comb_st = [None, None]
    r_hid_use = None
    r_sh_ev = None
    r_sc_ev = None
    r_trs_last = None
    prev_nT_use = None
    sw2_ld = [None, None]
    sw2_mm = [None, None]
    a1_used = [None, None]
    sh_ev_bank = {4: None, 5: None}
    r_relu = None
    r_hid_last_prev = None
    ctr_bank = {6: None, 7: None}
    add_ref = [None] * NKI
    for c in range(NCHB):
        t0 = c * CW
        nt_refs = []
        for a in range(NKI):
            if a == 0:
                q.wait(st_ref[NT - 2])
                q.wait(st_ref[NT - 1])
                if prev_nT_use is not None:
                    q.wait(prev_nT_use)
            nt_refs.append(q.dma_transpose(f"ntc{a % 2}",
                                           out=nT[:, a * CW:(a + 1) * CW],
                                           in_=normed_d[t0:t0 + CW, a * 128:(a + 1) * 128]))
        nt_done2, nt_done = nt_refs[NKI - 2], nt_refs[NKI - 1]
        t.wait(nt_done)
        t.wait(nt_done2)
        if c == 0:
            t.wait(r_rw)
        if r_sc_ev is not None:
            t.wait(r_sc_ev)
        for k in range(NKI):
            r_scm = t.matmul(bank[2][0:8, :], rwT[:, k * NE:(k + 1) * NE],
                             nT[:, k * CW:(k + 1) * CW],
                             start=(k == 0), stop=(k == NKI - 1))
        s.wait(r_scm)
        if r_trs_last is not None:
            s.wait(r_trs_last)
        r_sc_ev = s.activation(out=scTc[:], in_=bank[2][0:8, :],
                               func=AF.Sigmoid)
        for ft in range(NFT):
            pb = bank[ft % 2]
            if c == 0 and ft == 0:
                t.wait(r_sw1)
            wref = r_relu_prev2 if ft >= 2 else r_hid_last_prev
            if wref is not None:
                t.wait(wref)
            for k in range(NKI):
                r_h = t.matmul(pb[:], wbig1[:, k * FF + ft * 128:k * FF + (ft + 1) * 128],
                               nT[:, k * CW:(k + 1) * CW],
                               start=(k == 0), stop=(k == NKI - 1))
            s.wait(r_h)
            r_relu_prev2 = r_relu if ft >= 1 else None
            r_relu = s.activation(out=hidc[:, ft * CW:(ft + 1) * CW], in_=pb[:],
                                  func=AF.Relu, bias=sb1[:, ft:ft + 1])
        r_hid_last_prev = r_relu
        prev_nT_use = r_h

        for o in range(NKI):
            op_ = o % 2
            if sw2_mm[op_] is not None:
                gq.wait(sw2_mm[op_])
            sw2_ld[op_] = gq.dma(f"sw2{op_}",
                                 out=wbig2[:, 0:NFT * 128].rearrange("p (a m) -> p a m", a=NFT)
                                 if False else wbig2[:, op_ * NFT * 128:(op_ + 1) * NFT * 128].rearrange("p (a m) -> p a m", a=NFT),
                                 in_=sw2_in[:, o * 128:(o + 1) * 128].rearrange("(a p) m -> p a m", p=128))
            pb = bank[4 + op_]
            t.wait(sw2_ld[op_])
            t.wait(r_relu)
            if sh_ev_bank[4 + op_] is not None:
                t.wait(sh_ev_bank[4 + op_])
            for k in range(NFT):
                r_sh = t.matmul(pb[:], wbig2[:, op_ * NFT * 128 + k * 128:op_ * NFT * 128 + (k + 1) * 128],
                                hidc[:, k * CW:(k + 1) * CW],
                                start=(k == 0), stop=(k == NFT - 1))
            sw2_mm[op_] = r_sh
            if a1_used[op_] is not None:
                q.wait(a1_used[op_])
            r_aload = q.dma(f"a1l{op_}", out=a1c[op_][:],
                            in_=attn1T_in[o * 128:(o + 1) * 128, t0:t0 + CW])
            v.wait(r_sh)
            v.wait(r_aload)
            v.tensor_scalar(out=comb[:, o * CW:(o + 1) * CW], in0=pb[:],
                            scalar1=sb2[:, o:o + 1], scalar2=None, op0=OP.add)
            r_sh_ev = v.tensor_tensor(out=comb[:, o * CW:(o + 1) * CW],
                                      in0=comb[:, o * CW:(o + 1) * CW],
                                      in1=a1c[op_][:], op=OP.add)
            a1_used[op_] = r_sh_ev
            sh_ev_bank[4 + op_] = r_sh_ev
        for s4 in range(4):
            sp = s4 % 2
            if r_comb_st[sp] is not None:
                q.wait(r_comb_st[sp])
            r_lx1 = q.dma(f"x1l{sp}", out=xtok[:],
                          in_=x1_in[t0 + s4 * 128:t0 + (s4 + 1) * 128, :]) if s4 < 2 else \
                q.dma(f"x1l{sp}", out=xtok[:], in_=x1_in[t0 + s4 * 128:t0 + (s4 + 1) * 128, :])
            for o in range(NKI):
                bi = 6 + o % 2
                pt = bank[bi]
                t.wait(r_sh_ev)
                if ctr_bank[bi] is not None:
                    t.wait(ctr_bank[bi])
                r_tr = t.transpose(pt[:, 0:128],
                                   comb[:, o * CW + s4 * 128:o * CW + (s4 + 1) * 128],
                                   idnf[:])
                v.wait(r_tr)
                v.wait(r_lx1)
                add_ref[o] = v.tensor_tensor(out=otok[sp][:, o * 128:(o + 1) * 128],
                                             in0=pt[:, 0:128],
                                             in1=xtok[:, o * 128:(o + 1) * 128], op=OP.add)
                ctr_bank[bi] = add_ref[o]
            q.wait(add_ref[NKI - 1])
            r_comb_st[sp] = q.dma(f"so{sp}", out=out_f[t0 + s4 * 128:t0 + (s4 + 1) * 128, :],
                                  in_=otok[sp][:])
        for tt in range(4):
            j = 4 * c + tt
            t.wait(r_sc_ev)
            if j >= 1 and sc_tr[j - 1] is not None:
                t.wait(sc_tr[j - 1])
            r_trs = t.transpose(bank[3][0:128, 0:8], scTc[:, tt * 128:(tt + 1) * 128],
                                idnf[0:8, 0:8])
            r_trs_last = r_trs
            v.wait(r_trs)
            v.tensor_copy(out=sctok[:], in_=bank[3][0:128, 0:8])
            v.max(out=top8[:], in_=sctok[:])
            v.tensor_tensor(out=den[:], in0=top8[:, 0:1], in1=top8[:, 1:2], op=OP.add)
            v.tensor_scalar(out=den[:], in0=den[:], scalar1=1e-6, scalar2=None, op0=OP.add)
            v.reciprocal(out=den[:], in_=den[:])
            v.tensor_scalar(out=msk8[:], in0=sctok[:], scalar1=top8[:, 1:2], scalar2=None,
                            op0=OP.is_ge)
            v.tensor_tensor(out=msk8[:], in0=sctok[:], in1=msk8[:], op=OP.mult)
            sc_tr[j] = v.tensor_scalar(out=gtok[:, j * 8:(j + 1) * 8], in0=msk8[:],
                                       scalar1=den[:], scalar2=None, op0=OP.mult)

    v.memset(ones8c[:], 1.0)
    v.memset(carry[:], 0.0)
    lb_ref = [None] * NT
    sc_refs = [[None] * NE for _ in range(NT)]
    gt_ev = None
    for ch4 in range(4):
        for jj in range(8):
            j = 8 * ch4 + jj
            t.wait(sc_tr[NT - 1] if j == 0 else sc_tr[j])
            if gt_ev is not None:
                t.wait(gt_ev)
            if ch4 > 0 and jj == 0:
                t.wait(lb_ref[8 * ch4 - 1])
            r_trg = t.transpose(bank[3][0:8, 0:128], gtok[:, j * 8:(j + 1) * 8], idnf[:])
            v.wait(r_trg)
            if j == 0:
                v.wait(sw2_mm[0])
                v.wait(sw2_mm[1])
            gt_ev = v.tensor_copy(out=gateC[:, jj * 128:(jj + 1) * 128],
                                  in_=bank[3][0:8, 0:128])
        v.tensor_scalar(out=maskC, in0=gateC, scalar1=0.0, scalar2=None,
                        op0=OP.is_gt)
        v.tensor_tensor_scan(out=posC, data0=ones8c[:].to_broadcast([8, 1024]),
                             data1=maskC, initial=carry[:, 0:1],
                             op0=OP.mult, op1=OP.add)
        v.tensor_copy(out=carry[:], in_=posC[:, 1023:1024])
        v.scalar_tensor_tensor(out=posC, in0=maskC, scalar=-BIG,
                               in1=posC, op0=OP.mult, op1=OP.add)
        r_mask = v.tensor_scalar(out=posC, in0=posC, scalar1=ecapbig[:],
                                 scalar2=None, op0=OP.add)
        for jj in range(8):
            j = 8 * ch4 + jj
            bj = j % 2
            t.wait(r_mask)
            if j >= 1:
                t.wait(lb_ref[j - 1])
            tr1 = t.transpose(bank[3][:, 0:8], posC[:, jj * 128:(jj + 1) * 128],
                              idnf[0:8, 0:8])
            tr2 = t.transpose(bank[3][:, 8:16], gateC[:, jj * 128:(jj + 1) * 128],
                              idnf[0:8, 0:8])
            v.wait(tr1)
            v.wait(tr2)
            if j >= 1:
                for e in range(NE):
                    v.wait(sc_refs[j - 1][e])
            v.tensor_copy(out=offu[bj][:], in_=bank[3][:, 0:8])
            s2v = src2[bj][:].rearrange("p (a b) -> p a b", b=2)
            v.tensor_copy(out=s2v[:, :, 1], in_=bank[3][:, 8:16])
            lb_ref[j] = v.tensor_copy(out=s2v[:, :, 0],
                                      in_=tokids[:, j:j + 1].to_broadcast([128, 8]))
            gq.wait(lb_ref[j])
            if j == 0:
                gq.wait(r_li)
            for e in range(NE):
                if j >= 1:
                    gq.wait(sc_refs[j - 1][e])
                sc_refs[j][e] = gq.indirect(
                    f"g{e}", out=lists1,
                    out_offset=IndirectOffsetOnAxis(offu[bj][:, e:e + 1], 0),
                    in_=src2[bj][:, 2 * e:2 * e + 2], in_offset=None,
                    bounds=NE * CAP1 - 1, oob_is_err=False)

    for e in range(NE):
        q.wait(sc_refs[NT - 1][e])
    r_ll = q.dma("llr", out=lraw.rearrange("p (a n) -> p a n", n=2),
                 in_=lists1.rearrange("(a p) n -> p a n", p=128))
    v.wait(r_ll)
    lv = lraw.rearrange("p (a n) -> p a n", n=2)
    r_ids = v.tensor_copy(out=idsu, in_=lv[:, :, 0])
    r_gts = v.tensor_copy(out=gts, in_=lv[:, :, 1])

    ew1_ref = gq.dma("w1", out=wbig1[:].rearrange("p (a f) -> p a f", a=NKI),
                     in_=ew1_in[0].rearrange("(a p) f -> p a f", p=128))
    ew2_ref = gq.dma("w2e", out=wbig2[:].rearrange("p (a o) -> p a o", a=NFT),
                     in_=ew2_in[0].rearrange("(a p) o -> p a o", p=128))

    prev_scat = None
    prev_gather = {}
    prev_exT_use = [None, None]
    prev_otok_use = [None, None]
    tr_bank = {6: None, 7: None}
    otr_bank = {4: None, 5: None}
    r_hid_done_prev = None
    GRP1 = [4, 4, 4, 4]
    for e in range(NE):
        g0 = 0
        for gi, gnt in enumerate(GRP1):
            gp = (e * len(GRP1) + gi) % 2
            nslots = gnt * 128
            ex_done = None
            for jj in range(gnt):
                a = e * NTPE1 + g0 + jj
                bj = jj % 2
                if prev_gather.get(bj) is not None:
                    v.wait(prev_gather[bj])
                r_ms = v.memset(gx[bj][:, 0:D], 0.0)
                gq.wait(r_ms)
                gq.wait(r_ids)
                r_g = gq.indirect(f"ga{bj}", out=gx[bj][:, 0:D], out_offset=None,
                                  in_=normed_d,
                                  in_offset=IndirectOffsetOnAxis(idsu[:, a:a + 1], 0),
                                  bounds=T - 1, oob_is_err=False)
                for k in range(NKI):
                    bi = 6 + k % 2
                    ptb = bank[bi][:].bitcast(BF16)
                    t.wait(r_g)
                    if e == 0 and gi == 0 and jj == 0 and k == 0:
                        t.wait(r_idb)
                        t.wait(r_comb_st[0])
                        t.wait(r_comb_st[1])
                    if tr_bank[bi] is not None:
                        t.wait(tr_bank[bi])
                    r_tr = t.transpose(ptb[:, 0:128], gx[bj][:, k * 128:(k + 1) * 128],
                                       idnb[:])
                    v.wait(r_tr)
                    if prev_exT_use[gp] is not None and jj == 0 and k == 0:
                        v.wait(prev_exT_use[gp])
                    r_ev = v.tensor_copy(out=exT[gp][:, k * 512 + jj * 128:k * 512 + jj * 128 + 128],
                                         in_=ptb[:, 0:128])
                    tr_bank[bi] = r_ev
                prev_gather[bj] = r_ev
                ex_done = r_ev
            r_gelu = None
            for ft in range(NFT):
                pb = bank[ft % 2]
                t.wait(ex_done)
                if ft == 0:
                    t.wait(ew1_ref)
                    if r_hid_done_prev is not None:
                        t.wait(r_hid_done_prev)
                if ft >= 2:
                    t.wait(r_gelu_prev2)
                for k in range(NKI):
                    r_w1 = t.matmul(pb[:, 0:nslots],
                                    wbig1[:, k * FF + ft * 128:k * FF + (ft + 1) * 128],
                                    exT[gp][:, k * 512:k * 512 + nslots],
                                    start=(k == 0), stop=(k == NKI - 1))
                s.wait(r_w1)
                r_gelu_prev2 = r_gelu if ft >= 1 else None
                r_gelu = s.activation(out=hidc[:, ft * 512:ft * 512 + nslots],
                                      in_=pb[:, 0:nslots], func=AF.Relu,
                                      bias=eb1[:, e * NFT + ft:e * NFT + ft + 1])
                if ft == NFT - 1:
                    prev_w1_last = r_w1
            prev_exT_use[gp] = prev_w1_last
            if gi == len(GRP1) - 1 and e < NE - 1:
                gq.wait(prev_w1_last)
                ew1_ref = gq.dma("w1", out=wbig1[:].rearrange("p (a f) -> p a f", a=NKI),
                                 in_=ew1_in[e + 1].rearrange("(a p) f -> p a f", p=128))
            r_ot = None
            for o in range(NKI):
                pb = bank[2 + o % 2]
                if o == 0:
                    t.wait(r_gelu)
                    t.wait(ew2_ref)
                if o >= 2:
                    t.wait(r_ot_prev2)
                for k in range(NFT):
                    r_w2 = t.matmul(pb[:, 0:nslots],
                                    wbig2[:, k * D + o * 128:k * D + (o + 1) * 128],
                                    hidc[:, k * 512:k * 512 + nslots],
                                    start=(k == 0), stop=(k == NFT - 1))
                v.wait(r_w2)
                r_ot_prev2 = r_ot if o >= 1 else None
                r_ot = v.tensor_scalar(out=comb[:, o * 512:o * 512 + nslots],
                                       in0=pb[:, 0:nslots],
                                       scalar1=eb2[:, e * NKI + o:e * NKI + o + 1],
                                       scalar2=None, op0=OP.add)
                if o == NKI - 1:
                    prev_w2_last = r_w2
            r_hid_done_prev = prev_w2_last
            if gi == len(GRP1) - 1 and e < NE - 1:
                gq.wait(prev_w2_last)
                ew2_ref = gq.dma("w2e", out=wbig2[:].rearrange("p (a o) -> p a o", a=NFT),
                                 in_=ew2_in[e + 1].rearrange("(a p) o -> p a o", p=128))
            for jj in range(gnt):
                a = e * NTPE1 + g0 + jj
                op_ = jj % 2
                for o in range(NKI):
                    bi = 4 + o % 2
                    pt = bank[bi]
                    t.wait(r_ot)
                    if otr_bank[bi] is not None:
                        t.wait(otr_bank[bi])
                    r_tr2 = t.transpose(pt[:, 0:128],
                                        comb[:, o * 512 + jj * 128:o * 512 + jj * 128 + 128],
                                        idnf[:])
                    v.wait(r_tr2)
                    if o == 0 and prev_otok_use[op_] is not None:
                        v.wait(prev_otok_use[op_])
                    r_go = v.tensor_scalar(out=otok[op_][:, o * 128:(o + 1) * 128],
                                           in0=pt[:, 0:128],
                                           scalar1=gts[:, a:a + 1], scalar2=None,
                                           op0=OP.mult)
                    otr_bank[bi] = r_go
                gq.wait(r_go)
                if prev_scat is not None:
                    gq.wait(prev_scat)
                else:
                    gq.wait(r_comb_st[0])
                    gq.wait(r_comb_st[1])
                prev_scat = gq.indirect(
                    "sc0", out=out_f, out_offset=IndirectOffsetOnAxis(idsu[:, a:a + 1], 0),
                    in_=otok[op_][:], in_offset=None,
                    bounds=T - 1, oob_is_err=False, compute_op=OP.add)
                prev_otok_use[op_] = prev_scat
            g0 += gnt

    q.fence(prev_scat)
    ctx.close()




def build_fused():
    p = Prog()
    nc = p.nc
    f = np.float32
    dr = {}
    def din(nm, shape, dt):
        dr[nm] = p.din(nm, shape, dt)
    def dint(nm, shape, dt):
        dr[nm] = nc.dram_tensor(nm, shape, dt, kind="Internal").ap()

    din("x_own", [T, D], F32)
    din("x_allT", [D, TT_ALL], F32)
    din("Wg9", [D, 9], F32)
    din("ones_col", [128, 1], F32)
    din("ones_n", [1, 128], F32)
    din("tokid", [128, NT], F32)
    din("ecapbig", [8, 1], F32)
    din("lists_init", [128, NE * CAP0 * 2 // 128], F32)
    din("ident", [128, 128], F32)
    din("consts", [128, 8], F32)
    din("W_inT", [D, MW], F32)
    din("W_dtT", [DT_RANK, D_INNER], F32)
    din("W_outT", [D_INNER, D], F32)
    din("A_log_r", [128, NDI * 16], F32)
    din("b_dt_r", [128, NDI], F32)
    din("convw_r", [128, NDI * 4], F32)
    din("convb_r", [128, NDI], F32)
    din("D_r", [128, NDI], F32)
    din("lng_r", [128, NKI], F32)
    din("lnb_r", [128, NKI], F32)
    din("bsel", [16, 16 * 128], BF16)
    din("ident_bf", [128, 128], BF16)
    din("ident_f", [128, 128], F32)
    din("onec_bf", [128, 1], BF16)
    din("W1", [NE, D, FF], F32)
    din("b1_r", [128, NE * NFT], F32)
    din("W2", [NE, FF, D], F32)
    din("b2_r", [128, NE * NKI], F32)
    din("Wqkv", [D, 3 * D], F32)
    din("Wo", [D, D], F32)
    din("bo_r", [128, NKI], F32)
    din("cosT", [128, T], BF16)
    din("sinT", [128, T], BF16)
    din("P2", [128, 128], BF16)
    din("masks", [128, 4 * 512], BF16)
    din("nw_bc", [128, D], BF16)
    din("router_wT", [D, NE], F32)
    din("sw1", [D, FF], F32)
    din("sb1_r", [128, NFT], F32)
    din("sw2", [FF, D], F32)
    din("sb2_r", [128, NKI], F32)
    din("ew1", [NE, D, FF], F32)
    din("eb1_r", [128, NE * NFT], F32)
    din("ew2", [NE, FF, D], F32)
    din("eb2_r", [128, NE * NKI], F32)
    din("ecapbig1", [8, 1], F32)
    din("lists_init1", [NE * CAP1, 2], F32)

    dint("xn", [T, D], BF16)
    dint("lists0", [NE * CAP0, 2], F32)
    dint("thr", [8, 1], F32)
    dint("x1a", [T, D], F32)
    dint("x1", [T, D], F32)
    dint("xn1", [T, D], BF16)
    dint("attn1T", [D, T], F32)
    dint("qT_d", [D, T], BF16)
    dint("kT_d", [D, T], BF16)
    dint("asc_d", [D, T], BF16)
    dint("normed_d", [T, D], BF16)
    dint("lists1", [NE * CAP1, 2], F32)

    dr["out"] = p.dout("out", [T, D], F32)

    csts_g = p.sb("csts_g", [128, 8], F32)
    nc.const_aps.register(F32, LN_EPS, csts_g[:, 0:1])
    nc.const_aps.register(F32, -0.5, csts_g[:, 1:2])
    nc.const_aps.register(F32, -1.0, csts_g[:, 2:3])
    nc.const_aps.register(F32, 1.0, csts_g[:, 3:4])
    nc.const_aps.register(F32, 0.125, csts_g[:, 4:5])
    nc.const_aps.register(F32, EXP_SHIFT, csts_g[:, 5:6])
    nc.const_aps.register(F32, RMS_EPS, csts_g[:, 6:7])

    bank = [p.ps(f"bank{i}", [128, 512]) for i in range(8)]

    mkp = MKProg()
    t = mkp.eng("tensor")
    v = mkp.eng("vector")
    s = mkp.eng("scalar")
    g = mkp.eng("gpsimd")
    q = mkp.q("sync")
    gq = mkp.q("gpsimd")

    r_cst = q.dma("cst", out=csts_g[:], in_=dr["consts"])
    for eng in (t, v, s):
        eng.wait(r_cst)
    gq.wait(r_cst)

    emit_stage_a(p, t, v, s, g, q, gq, bank, dr)
    mkp.barrier()
    emit_stage_b(p, t, v, s, g, q, gq, bank, dr)
    mkp.barrier()
    emit_stage_c(p, t, v, s, g, q, gq, bank, dr)
    mkp.barrier()
    emit_stage_d(p, t, v, s, g, q, gq, bank, dr)
    mkp.barrier()
    emit_stage_e(p, t, v, s, g, q, gq, bank, dr)
    mkp.emit(p)
    return p


def stage_a_inputs(x_full, Wg, core):
    x_own = np.ascontiguousarray(x_full[core]).astype(np.float32)
    order = [core] + [b for b in range(B) if b != core]
    x_ord = x_full[order]
    x_allT = np.ascontiguousarray(x_ord.reshape(TT_ALL, D).T).astype(np.float32)
    wg9 = np.concatenate([Wg.astype(np.float32),
                          np.full((D, 1), 1.0 / D, np.float32)], axis=1)
    ecapbig = (np.arange(NE, dtype=np.float32) * CAP0 - 1.0 + BIG).reshape(NE, 1)
    li = np.zeros((NE * CAP0, 2), np.float32)
    li[:, 0] = BIG
    return {
        "x_own": x_own,
        "x_allT": x_allT,
        "Wg9": wg9,
        "ones_col": np.ones((128, 1), np.float32),
        "ones_n": np.ones((1, 128), np.float32),
        "tokid": np.arange(T, dtype=np.float32).reshape(NT, 128).T.copy(),
        "ecapbig": ecapbig,
        "lists_init": np.ascontiguousarray(
            li.reshape(-1, 128, 2).transpose(1, 0, 2).reshape(128, -1)),
        "ident": np.eye(128, dtype=np.float32),
        "consts": np.broadcast_to(np.array([LN_EPS, -0.5], np.float32), (128, 2)).copy(),
    }




def stage_b_inputs(xn, x_core, W_in, W_dt, b_dt, A_log, Dv, conv_w, conv_b, W_out,
                   ln_g, ln_b):
    f = np.float32
    Wp = np.zeros((MW, D), f)
    Wp[0:2048] = W_in[0:2048]
    Wp[2048:4096] = W_in[2048:4096]
    m32 = 32 * 128
    Wp[m32 + 0:m32 + 16] = W_in[4096:4112]
    Wp[m32 + 32:m32 + 48] = W_in[4112:4128]
    Wp[m32 + 64:m32 + 128] = W_in[4128:4192]
    bsel = np.zeros((16, 16 * 128), BF16NP)
    for n in range(16):
        bsel[n, n * 128:(n + 1) * 128] = 1
    return {
        "xn": xn,
        "x_own": None if x_core is None else x_core.astype(f),
        "W_inT": np.ascontiguousarray(Wp.T),
        "W_dtT": np.ascontiguousarray(W_dt.astype(f).T),
        "W_outT": np.ascontiguousarray(W_out.astype(f).T),
        "A_log_r": np.ascontiguousarray(
            A_log.astype(f).reshape(16, 128, 16).transpose(1, 0, 2).reshape(128, -1)),
        "b_dt_r": np.ascontiguousarray(b_dt.astype(f).reshape(16, 128).T),
        "convw_r": np.ascontiguousarray(
            conv_w.astype(f).reshape(16, 128, 4).transpose(1, 0, 2).reshape(128, -1)),
        "convb_r": np.ascontiguousarray(conv_b.astype(f).reshape(16, 128).T),
        "D_r": np.ascontiguousarray(Dv.astype(f).reshape(16, 128).T),
        "lng_r": np.ascontiguousarray(ln_g.astype(f).reshape(8, 128).T),
        "lnb_r": np.ascontiguousarray(ln_b.astype(f).reshape(8, 128).T),
        "bsel": bsel,
        "ident_bf": np.eye(128, dtype=BF16NP),
        "ident_f": np.eye(128, dtype=np.float32),
        "onec_bf": np.ones((128, 1), BF16NP),
        "ones_n": np.ones((1, 128), np.float32),
        "consts": np.broadcast_to(np.array([LN_EPS, -0.5, -1.0, 1.0], f), (128, 4)).copy(),
    }




def stage_c_inputs(xn, x1a, lists0, W1, b1, W2, b2):
    f = np.float32
    return {
        "xn": xn,
        "x1a": None if x1a is None else x1a.astype(f),
        "lists0": None if lists0 is None else lists0.astype(f),
        "W1": W1.astype(f),
        "b1_r": np.ascontiguousarray(
            b1.astype(f).reshape(NE, NFT, 128).transpose(2, 0, 1).reshape(128, -1)),
        "W2": W2.astype(f),
        "b2_r": np.ascontiguousarray(
            b2.astype(f).reshape(NE, NKI, 128).transpose(2, 0, 1).reshape(128, -1)),
        "ident_bf": np.eye(128, dtype=BF16NP),
    }




def stage_d_inputs(x1, Wqkv, Wo, bo):
    f = np.float32
    half = HD // 2
    inv_freq = 1.0 / 10000.0 ** (np.arange(half, dtype=f) / half)
    tv = np.arange(T, dtype=f)
    freqs = np.outer(tv, inv_freq)
    cos = np.repeat(np.cos(freqs), 2, axis=1)
    sin = np.repeat(np.sin(freqs), 2, axis=1)
    cosT = np.ascontiguousarray(np.vstack([cos.T, cos.T])).astype(BF16NP)
    sinT = np.ascontiguousarray(np.vstack([sin.T, sin.T])).astype(BF16NP)
    P64 = np.zeros((HD, HD), f)
    for dpr in range(HD):
        if dpr % 2 == 0:
            P64[dpr + 1, dpr] = -1.0
        else:
            P64[dpr - 1, dpr] = 1.0
    P2v = np.zeros((128, 128), f)
    P2v[:HD, :HD] = P64
    P2v[HD:, HD:] = P64
    masks = np.zeros((128, 4, 512), f)
    for r in range(4):
        for pp in range(128):
            masks[pp, r, r * 128 + pp:] = 1.0
    return {
        "x1": None if x1 is None else x1.astype(f),
        "Wqkv": Wqkv.astype(f),
        "Wo": Wo.astype(f),
        "bo_r": np.ascontiguousarray(bo.astype(f).reshape(NKI, 128).T),
        "cosT": cosT,
        "sinT": sinT,
        "P2": P2v.astype(BF16NP),
        "masks": np.ascontiguousarray(masks.reshape(128, -1)).astype(BF16NP),
        "ones_n": np.ones((1, 128), f),
        "consts": np.broadcast_to(np.array([LN_EPS, -0.5, 0.125, EXP_SHIFT], f),
                                  (128, 4)).copy(),
    }




def stage_e_inputs(x1, xn1, attn1T, nw, router_w, sw1, sb1, sw2, sb2,
                   ew1, eb1, ew2, eb2):
    f = np.float32
    ecapbig = (np.arange(NE, dtype=f) * CAP1 - 1.0 + BIG).reshape(NE, 1)
    li = np.zeros((NE * CAP1, 2), f)
    li[:, 0] = BIG
    return {
        "x1": None if x1 is None else x1.astype(f),
        "xn1": xn1,
        "attn1T": None if attn1T is None else attn1T.astype(f),
        "nw_bc": np.broadcast_to(nw.astype(f), (128, D)).astype(BF16NP).copy(),
        "router_wT": np.ascontiguousarray(router_w.astype(f).T),
        "sw1": sw1.astype(f),
        "sb1_r": np.ascontiguousarray(sb1.astype(f).reshape(NFT, 128).T),
        "sw2": sw2.astype(f),
        "sb2_r": np.ascontiguousarray(sb2.astype(f).reshape(NKI, 128).T),
        "ew1": ew1.astype(f),
        "eb1_r": np.ascontiguousarray(
            eb1.astype(f).reshape(NE, NFT, 128).transpose(2, 0, 1).reshape(128, -1)),
        "ew2": ew2.astype(f),
        "eb2_r": np.ascontiguousarray(
            eb2.astype(f).reshape(NE, NKI, 128).transpose(2, 0, 1).reshape(128, -1)),
        "ident_f": np.eye(128, dtype=f),
        "tokid": np.arange(T, dtype=f).reshape(NT, 128).T.copy(),
        "ecapbig1": ecapbig,
        "lists_init1": li,
        "ones_n": np.ones((1, 128), f),
        "consts": np.broadcast_to(np.array([RMS_EPS, -0.5, 0.0, 0.0], f), (128, 4)).copy(),
    }


def fused_inputs(ins, core):
    f = np.float32
    x = np.asarray(ins["L0_attn_x"], f)
    im = {}
    im.update(stage_a_inputs(x, np.asarray(ins["L0_ffn_Wg"], f), core))
    b = stage_b_inputs(None, x[core],
                       np.asarray(ins["L0_attn_W_in"], f),
                       np.asarray(ins["L0_attn_W_dt"], f),
                       np.asarray(ins["L0_attn_b_dt"], f),
                       np.asarray(ins["L0_attn_A_log"], f),
                       np.asarray(ins["L0_attn_D"], f),
                       np.asarray(ins["L0_attn_conv_w"], f),
                       np.asarray(ins["L0_attn_conv_b"], f),
                       np.asarray(ins["L0_attn_W_out"], f),
                       np.asarray(ins["L0_attn_ln_g"], f),
                       np.asarray(ins["L0_attn_ln_b"], f))
    b.pop("xn"); b.pop("x_own")
    im.update(b)
    c = stage_c_inputs(None, None, None,
                       np.asarray(ins["L0_ffn_W1"], f),
                       np.asarray(ins["L0_ffn_b1"], f),
                       np.asarray(ins["L0_ffn_W2"], f),
                       np.asarray(ins["L0_ffn_b2"], f))
    for k in ("xn", "x1a", "lists0"):
        c.pop(k)
    im.update(c)
    d = stage_d_inputs(None,
                       np.asarray(ins["L1_attn_Wqkv"], f),
                       np.asarray(ins["L1_attn_Wo"], f),
                       np.asarray(ins["L1_attn_bo"], f))
    d.pop("x1")
    im.update(d)
    e = stage_e_inputs(None, None, None,
                       np.asarray(ins["L1_ffn_norm_weight"], f),
                       np.asarray(ins["L1_ffn_router_w"], f),
                       np.asarray(ins["L1_ffn_sw1"], f),
                       np.asarray(ins["L1_ffn_sb1"], f),
                       np.asarray(ins["L1_ffn_sw2"], f),
                       np.asarray(ins["L1_ffn_sb2"], f),
                       np.asarray(ins["L1_ffn_ew1"], f),
                       np.asarray(ins["L1_ffn_eb1"], f),
                       np.asarray(ins["L1_ffn_ew2"], f),
                       np.asarray(ins["L1_ffn_eb2"], f))
    for k in ("x1", "xn1", "attn1T"):
        e.pop(k)
    im.update(e)
    im["consts"] = np.broadcast_to(np.array(
        [LN_EPS, -0.5, -1.0, 1.0, 0.125, EXP_SHIFT, RMS_EPS, 0.0], f),
        (128, 8)).copy()
    return im


_FPROG = []

def kernel(**ins):
    if not _FPROG:
        _FPROG.append(build_fused())
    p = _FPROG[0]
    in_maps = [fused_inputs(ins, c) for c in range(B)]
    res = run_bass_kernel_spmd(p.nc, in_maps, core_ids=list(range(B)))
    return np.stack([np.asarray(r["out"], np.float32) for r in res.results], 0)


# version 10
# speedup vs baseline: 1.0116x; 1.0116x over previous
import os
import concourse.bass as bass
import concourse.mybir as mybir

SELF_EDGE_WAITS = os.environ.get("FW_SELF_WAITS", "1") == "1"


class OpRef:
    __slots__ = ("eng", "idx")
    def __init__(self, eng, idx):
        self.eng = eng
        self.idx = idx


class DmaRef:
    __slots__ = ("q", "sem_name", "value")
    def __init__(self, q, sem_name, value):
        self.q = q
        self.sem_name = sem_name
        self.value = value


_SEQ = [0]

class _Rec:
    __slots__ = ("opname", "args", "kwargs", "waits", "inc_needed", "is_dma",
                 "dma_sem", "seq")
    def __init__(self, opname, args, kwargs, is_dma=False, dma_sem=None):
        self.opname = opname
        self.args = args
        self.kwargs = kwargs
        self.waits = []
        self.inc_needed = False
        self.is_dma = is_dma
        self.dma_sem = dma_sem
        _SEQ[0] += 1
        self.seq = _SEQ[0]


class Eng:
    def __init__(self, mk, name, self_edges=True):
        self.mk = mk
        self.name = name
        self.self_edges = self_edges
        self.recs = []
        self.pending_waits = []

    def wait(self, ref):
        assert ref is not None
        if isinstance(ref, OpRef) and not ref.eng.self_edges:
            ref.eng.mark(ref)
        self.pending_waits.append(ref)

    def __getattr__(self, opname):
        if opname.startswith("_"):
            raise AttributeError(opname)
        def rec(*args, **kwargs):
            r = _Rec(opname, args, kwargs)
            r.waits = self.pending_waits
            self.pending_waits = []
            self.recs.append(r)
            ref = OpRef(self, len(self.recs))
            if self.self_edges:
                r.inc_needed = True
            return ref
        return rec

    def mark(self, ref):
        self.recs[ref.idx - 1].inc_needed = True


class Q:
    def __init__(self, mk, name):
        self.mk = mk
        self.name = name
        self.recs = []
        self.pending_waits = []
        self.sem_counts = {}

    def wait(self, ref):
        if isinstance(ref, OpRef) and not ref.eng.self_edges:
            ref.eng.mark(ref)
        self.pending_waits.append(ref)

    def dma(self, sem_name, out, in_, **kwargs):
        key = (self.name, sem_name)
        self.mk.dma_sems.setdefault(sem_name, 0)
        self.mk.dma_sems[sem_name] += 16
        r = _Rec("dma_start", (), dict(out=out, in_=in_, **kwargs),
                 is_dma=True, dma_sem=sem_name)
        r.waits = self.pending_waits
        self.pending_waits = []
        self.recs.append(r)
        return DmaRef(self, sem_name, self.mk.dma_sems[sem_name])

    def dma_transpose(self, sem_name, out, in_):
        self.mk.dma_sems.setdefault(sem_name, 0)
        self.mk.dma_sems[sem_name] += 16
        r = _Rec("dma_start_transpose", (), dict(out=out, in_=in_),
                 is_dma=True, dma_sem=sem_name)
        r.waits = self.pending_waits
        self.pending_waits = []
        self.recs.append(r)
        return DmaRef(self, sem_name, self.mk.dma_sems[sem_name])

    def indirect(self, sem_name, out, out_offset, in_, in_offset,
                 bounds=None, **kwargs):
        self.mk.dma_sems.setdefault(sem_name, 0)
        self.mk.dma_sems[sem_name] += 16
        r = _Rec("indirect_dma_start", (),
                 dict(out=out, out_offset=out_offset, in_=in_,
                      in_offset=in_offset, _bounds=bounds, **kwargs),
                 is_dma=True, dma_sem=sem_name)
        r.waits = self.pending_waits
        self.pending_waits = []
        self.recs.append(r)
        return DmaRef(self, sem_name, self.mk.dma_sems[sem_name])

    def fence(self, *refs):
        r = _Rec("__fence__", (), {})
        r.waits = self.pending_waits + list(refs)
        self.pending_waits = []
        for ref in r.waits:
            if isinstance(ref, OpRef) and not ref.eng.self_edges:
                ref.eng.mark(ref)
        self.recs.append(r)

    def collective(self, sem_name, *args, **kwargs):
        self.mk.dma_sems.setdefault(sem_name, 0)
        self.mk.dma_sems[sem_name] += 16
        r = _Rec("collective_compute", args, kwargs, is_dma=True,
                 dma_sem=sem_name)
        r.waits = self.pending_waits
        self.pending_waits = []
        self.recs.append(r)
        return DmaRef(self, sem_name, self.mk.dma_sems[sem_name])


class MKProg:
    ENGINES = ("tensor", "vector", "scalar", "gpsimd")
    QUEUES = ("sync", "gpsimd", "scalar")

    def __init__(self):
        self.engs = {}
        self.qs = {}
        self.dma_sems = {}

    def eng(self, name):
        if name not in self.engs:
            self.engs[name] = Eng(self, name, self_edges=(name != "tensor"))
        return self.engs[name]

    def q(self, name):
        if name not in self.qs:
            self.qs[name] = Q(self, name)
        return self.qs[name]

    def barrier(self):
        refs = []
        for e in self.engs.values():
            if e.recs:
                refs.append(OpRef(e, len(e.recs)))
        for semname, val in self.dma_sems.items():
            refs.append(DmaRef(None, semname, val))
        for e in self.engs.values():
            for r in refs:
                if isinstance(r, OpRef) and r.eng is e:
                    continue
                e.wait(r)
        for qq in self.qs.values():
            for r in refs:
                if isinstance(r, OpRef):
                    qq.wait(r)
                else:
                    qq.wait(r)

    def emit(self, p):
        nc = p.nc
        esems = {n: p.sem(f"E_{n}") for n in self.engs}
        dsems = {n: p.sem(f"D_{n}") for n in self.dma_sems}

        inc_prefix = {}
        for n, e in self.engs.items():
            pref = []
            c = 0
            for r in e.recs:
                if r.inc_needed:
                    c += 1
                pref.append(c)
            inc_prefix[n] = pref

        def wait_val(ref):
            if isinstance(ref, OpRef):
                return esems[ref.eng.name], inc_prefix[ref.eng.name][ref.idx - 1]
            else:
                return dsems[ref.sem_name], ref.value

        blk = p.block()

        def emit_engine_ops(handle, ename, recs):
            own_sem = esems.get(ename)
            self_edges = ename in self.engs and self.engs[ename].self_edges
            prev_incs = 0
            for i, r in enumerate(recs):
                for w in r.waits:
                    sem, val = wait_val(w)
                    handle.wait_ge(sem, val)
                if SELF_EDGE_WAITS and self_edges and prev_incs > 0:
                    handle.wait_ge(own_sem, prev_incs)
                try:
                    inst = getattr(handle, r.opname)(*r.args, **r.kwargs)
                except Exception as ex:
                    print('FAIL op', ename, i, r.opname,
                          [getattr(a, 'shape', a) for a in r.args],
                          {k: getattr(vv, 'shape', None) for k, vv in r.kwargs.items()})
                    raise
                if r.inc_needed:
                    inst.then_inc(own_sem, 1)
                    prev_incs += 1

        def emit_queue_ops(handle, q, bounds_regs):
            for r in q.recs:
                for w in r.waits:
                    sem, val = wait_val(w)
                    handle.wait_ge(sem, val)
                if r.opname == "__fence__":
                    continue
                kw = dict(r.kwargs)
                if r.opname == "indirect_dma_start":
                    b = kw.pop("_bounds")
                    if b is not None:
                        if b not in bounds_regs:
                            bounds_regs[b] = handle.to_reg(b)
                        kw["bounds_check"] = bounds_regs[b]
                inst = getattr(handle, r.opname)(*r.args, **kw)
                inst.then_inc(dsems[r.dma_sem], 16)

        sections = {}
        for n in self.engs:
            sections[n] = ("eng", n)
        def make_body(ename):
            def body(handle):
                bounds_regs = {}
                has_e = ename in self.engs
                has_q = ename in self.qs
                if has_e and not has_q:
                    emit_engine_ops(handle, ename, self.engs[ename].recs)
                elif has_q and not has_e:
                    emit_queue_ops(handle, self.qs[ename], bounds_regs)
                else:
                    er = self.engs[ename].recs
                    qr = self.qs[ename].recs
                    own_sem = esems.get(ename)
                    self_edges = self.engs[ename].self_edges
                    prev_incs = 0
                    ei = qi = 0
                    while ei < len(er) or qi < len(qr):
                        take_e = qi >= len(qr) or (ei < len(er) and er[ei].seq < qr[qi].seq)
                        if take_e:
                            r = er[ei]; ei += 1
                            for w in r.waits:
                                sem, val = wait_val(w)
                                handle.wait_ge(sem, val)
                            if SELF_EDGE_WAITS and self_edges and prev_incs > 0:
                                handle.wait_ge(own_sem, prev_incs)
                            inst = getattr(handle, r.opname)(*r.args, **r.kwargs)
                            if r.inc_needed:
                                inst.then_inc(own_sem, 1)
                                prev_incs += 1
                        else:
                            r = qr[qi]; qi += 1
                            for w in r.waits:
                                sem, val = wait_val(w)
                                handle.wait_ge(sem, val)
                            if r.opname == "__fence__":
                                continue
                            kw = dict(r.kwargs)
                            if r.opname == "indirect_dma_start":
                                bnd = kw.pop("_bounds")
                                if bnd is not None:
                                    if bnd not in bounds_regs:
                                        bounds_regs[bnd] = handle.to_reg(bnd)
                                    kw["bounds_check"] = bounds_regs[bnd]
                            inst = getattr(handle, r.opname)(*r.args, **kw)
                            inst.then_inc(dsems[r.dma_sem], 16)
            return body

        used = set(self.engs) | set(self.qs)
        for ename in ("sync", "tensor", "scalar", "vector", "gpsimd"):
            if ename not in used:
                continue
            body = make_body(ename)
            getattr(blk, ename)(body)
        return esems, dsems


"""Megakernel builders for mk_prb_d2. Becomes kernel.py when complete.

Layouts: token-major (tokens on partitions) for LN/gathers; feature-major
("T") for matmuls (contraction on partitions). Host numpy = layout only.
Sync discipline: fw.py recording framework; self-edges on DVE/ACT/Pool;
<=1 outstanding DMA per semaphore; waits on exact recorded counts.
"""
import numpy as np
import ml_dtypes
from contextlib import ExitStack

import concourse.bass as bass
import concourse.mybir as mybir
from concourse.bass import IndirectOffsetOnAxis
from concourse.bass_utils import run_bass_kernel_spmd
from concourse.alu_op_type import AluOpType as OP


F32 = mybir.dt.float32
BF16 = mybir.dt.bfloat16
U32 = mybir.dt.uint32
AF = mybir.ActivationFunctionType
BF16NP = ml_dtypes.bfloat16

B = 8
T = 4096
D = 1024
TT_ALL = B * T
NE = 8
CAP0 = 1280
K0 = 8192
LN_EPS = 1e-5
BIS_ITERS = 28
BIG = 65536.0

NT = T // 128
NCH = TT_ALL // 512
NDT = D // 128
NTILES_ALL = TT_ALL // 128


class Prog:
    def __init__(self):
        self.nc = bass.Bass("TRN2", target_bir_lowering=False)
        self.ctx = ExitStack()

    def sem(self, name):
        return self.ctx.enter_context(self.nc.semaphore(name))

    def sb(self, name, shape, dtype):
        return self.ctx.enter_context(self.nc.sbuf_tensor(name, shape, dtype))

    def ps(self, name, shape, dtype=F32):
        return self.ctx.enter_context(self.nc.psum_tensor(name, shape, dtype))

    def din(self, name, shape, dtype):
        return self.nc.dram_tensor(name, shape, dtype, kind="ExternalInput").ap()

    def dout(self, name, shape, dtype):
        return self.nc.dram_tensor(name, shape, dtype, kind="ExternalOutput").ap()

    def block(self):
        return self.ctx.enter_context(self.nc.Block())


D_INNER = 2048
D_STATE = 16
DT_RANK = 64
NKI = D // 128
NDI = D_INNER // 128
MW = 33 * 128
NCHB = 8
CW = 512
FF = 4096
NFT = FF // 128
NTPE = CAP0 // 128
GRP = [4, 4, 2]
NH = 16
HD = 64
EXP_SHIFT = -10.0
CAP1 = 1536
NTPE1 = CAP1 // 128
RMS_EPS = 1e-8


def emit_stage_a(p, t, v, s, g, q, gq, bank, dr):
    nc = p.nc
    ctx = ExitStack()
    def sb(nm, shape, dt):
        return ctx.enter_context(nc.sbuf_tensor(f'a_'+nm, shape, dt))

    x_own = dr["x_own"]
    x_allT = dr["x_allT"]
    wg9 = dr["Wg9"]
    ones_col = dr["ones_col"]
    ones_n = dr["ones_n"]
    tokid = dr["tokid"]
    ecapbig_in = dr["ecapbig"]
    lists_init = dr["lists_init"]
    idn_in = dr["ident"]
    consts_in = dr["consts"]

    xn_out = dr["xn"]
    lists0 = dr["lists0"]
    thr_out = dr["thr"]

    xs = [sb(f"xs{i}", [128, NDT * 512], F32) for i in range(2)]
    sqs = [sb(f"sqs{i}", [128, NDT * 512], F32) for i in range(2)]
    wg_sb = sb("wg_sb", [128, NDT * 9], F32)
    onec = sb("onec", [128, 1], F32)
    onen = sb("onen", [1, 128], F32)
    crow = sb("crow", [1, 8], F32)
    cexp = sb("cexp", [128, 8], F32)
    tokids = sb("tokids", [128, NT], F32)
    ecapbig = sb("ecapbig_s", [8, 1], F32)
    li = sb("li", [128, NE * CAP0 * 2 // 128], F32)
    idn = sb("idn", [128, 128], F32)

    scoresTok = sb("scoresTok", [128, NE * NTILES_ALL], F32)
    rs_all = sb("rs_all", [128, NCH * 4], F32)
    muG = sb("muG", [128, NCH * 4], F32)
    var4 = [sb(f"var4_{i}", [128, 4], F32) for i in range(2)]
    rs4 = [sb(f"rs4_{i}", [128, 4], F32) for i in range(2)]
    mu2 = sb("mu2", [128, 1], F32)
    m2t = sb("m2t", [128, 8], F32)
    dlt = sb("dlt", [128, 8], F32)
    logit4 = [sb(f"logit4_{i}", [128, 32], F32) for i in range(2)]

    lo = sb("lo", [1, 8], F32)
    hi = sb("hi", [1, 8], F32)
    mid = sb("mid", [1, 8], F32)
    prd = sb("prd", [1, 8], F32)
    dmh = sb("dmh", [1, 8], F32)
    u8t = sb("u8t", [1, 8], F32)
    cntp = sb("cntp", [128, 8], F32)
    cjunk = sb("cjunk", [128, 256], F32)
    thr_c = sb("thr_c", [8, 1], F32)

    scoresT = sb("scoresT", [8, T], F32)
    maskT = sb("maskT", [8, T], F32)
    gateT = sb("gateT", [8, T], F32)
    ones8T = sb("ones8T", [8, T], F32)
    posT = sb("posT", [8, T], F32)
    offT = sb("offT", [8, T], F32)

    offu = [sb(f"offu{i}", [128, 8], U32) for i in range(2)]
    src2 = [sb(f"src2_{i}", [128, 16], F32) for i in range(2)]

    xt = [sb(f"xt{i}", [128, D], F32) for i in range(2)]
    xnb = [sb(f"xnb{i}", [128, D], BF16) for i in range(2)]

    psS = [bank[0][:, 0:40], bank[1][:, 0:40]]
    psQ = [bank[2][:, 0:4], bank[3][:, 0:4]]
    psT1 = bank[4][:, 0:128]
    psB = bank[5][:, 0:8]
    psC = bank[6][0:1, 0:8]
    psTh = bank[7][0:8, 0:1]


    pre = []
    pre.append(q.dma("pre", out=wg_sb[:].rearrange("p (a n) -> p a n", n=9),
                     in_=wg9.rearrange("(a p) n -> p a n", p=128)))
    pre.append(q.dma("pre", out=onec[:], in_=ones_col))
    pre.append(q.dma("pre", out=onen[:], in_=ones_n))
    pre.append(q.dma("pre", out=tokids[:], in_=tokid))
    pre.append(q.dma("pre", out=ecapbig[:], in_=ecapbig_in))
    pre.append(q.dma("pre", out=li[:], in_=lists_init))
    pre.append(q.dma("pre", out=idn[:], in_=idn_in))
    pre_done = pre[-1]

    q.wait(pre_done)
    r_li = q.dma("sli", out=lists0.rearrange("(a p) n -> p a n", p=128),
                 in_=li[:].rearrange("p (a n) -> p a n", n=2))

    t.wait(pre_done)
    for k in range(NDT):
        r_csum = t.matmul(psC, onec[:], wg_sb[:, 9 * k:9 * k + 8],
                          start=(k == 0), stop=(k == NDT - 1))
    v.wait(r_csum)
    r_crow = v.tensor_copy(out=crow[:], in_=psC)
    t.wait(r_crow)
    r_cb = t.matmul(psB, onen[:], crow[:], start=True, stop=True)
    v.wait(r_cb)
    r_cexp = v.tensor_copy(out=cexp[:], in_=psB)
    v.memset(ones8T[:], 1.0)

    te_ref = [None] * NCH
    v2_ref = [None] * NCH
    sg_ref = [None] * NCH
    sg_s_ref = [None] * NCH
    for c in range(NCH):
        b = c % 2
        if c >= 2:
            q.wait(te_ref[c - 2])
        r_lc = q.dma(f"lc{b}", out=xs[b][:].rearrange("p (a t) -> p a t", a=NDT),
                     in_=x_allT[:, c * 512:(c + 1) * 512].rearrange("(a p) t -> p a t", p=128))
        s.wait(r_lc)
        if c >= 2:
            s.wait(te_ref[c - 2])
        for k in range(NDT):
            r_sq = s.activation(out=sqs[b][:, k * 512:(k + 1) * 512],
                                in_=xs[b][:, k * 512:(k + 1) * 512], func=AF.Square)
        t.wait(r_lc)
        t.wait(r_sq)
        if c >= 2:
            t.wait(v2_ref[c - 2])
        for jj in range(4):
            for k in range(NDT):
                t.matmul(psS[b][:, 9 * jj:9 * jj + 9],
                         xs[b][:, k * 512 + jj * 128:k * 512 + (jj + 1) * 128],
                         wg_sb[:, 9 * k:9 * k + 9],
                         start=(k == 0), stop=(k == NDT - 1))
        for jj in range(4):
            for k in range(NDT):
                r_mm = t.matmul(psQ[b][:, jj:jj + 1],
                                sqs[b][:, k * 512 + jj * 128:k * 512 + (jj + 1) * 128],
                                onec[:], start=(k == 0), stop=(k == NDT - 1))
        te_ref[c] = r_mm
        v.wait(r_mm)
        if c >= 2:
            v.wait(sg_s_ref[c - 2])
        for jj in range(4):
            v.tensor_copy(out=muG[:, 4 * c + jj:4 * c + jj + 1],
                          in_=psS[b][:, 9 * jj + 8:9 * jj + 9])
            v.tensor_tensor(out=mu2[:], in0=muG[:, 4 * c + jj:4 * c + jj + 1],
                            in1=muG[:, 4 * c + jj:4 * c + jj + 1], op=OP.mult)
            r_v1 = v.scalar_tensor_tensor(out=var4[b][:, jj:jj + 1],
                                          in0=psQ[b][:, jj:jj + 1], scalar=1.0 / D,
                                          in1=mu2[:], op0=OP.mult, op1=OP.subtract)
        s.wait(r_v1)
        s.activation(out=rs4[b][:], in_=var4[b][:], func=AF.Ln, bias=LN_EPS)
        s.activation(out=rs4[b][:], in_=rs4[b][:], func=AF.Exp, scale=-0.5)
        r_rs = s.copy(out=rs_all[:, 4 * c:4 * c + 4], in_=rs4[b][:])
        v.wait(r_rs)
        for jj in range(4):
            v.tensor_scalar(out=m2t[:], in0=cexp[:],
                            scalar1=muG[:, 4 * c + jj:4 * c + jj + 1], scalar2=None,
                            op0=OP.mult)
            v.tensor_tensor(out=dlt[:], in0=psS[b][:, 9 * jj:9 * jj + 8],
                            in1=m2t[:], op=OP.subtract)
            r_v2 = v.tensor_scalar(out=logit4[b][:, 8 * jj:8 * jj + 8], in0=dlt[:],
                                   scalar1=rs_all[:, 4 * c + jj:4 * c + jj + 1],
                                   scalar2=None, op0=OP.mult)
        v2_ref[c] = r_v2
        s.wait(r_v2)
        s.activation(out=logit4[b][:], in_=logit4[b][:], func=AF.Exp, scale=-1.0)
        sg_s_ref[c] = s.add(out=logit4[b][:], in_=logit4[b][:], add=1.0)
        v.wait(sg_s_ref[c])
        dst = scoresTok[:].rearrange("p (e t) -> p t e", e=NE)[:, 4 * c:4 * c + 4, :]
        sg_ref[c] = v.reciprocal(out=dst,
                                 in_=logit4[b][:].rearrange("p (t e) -> p t e", e=NE))

    ev_ref = [None] * NT
    for j in range(NT):
        t.wait(sg_ref[j // 4])
        if j >= 1:
            t.wait(ev_ref[j - 1])
        src = scoresTok[:].rearrange("p (e t) -> p t e", e=NE)[:, j, :]
        r_tr = t.transpose(psT1[0:8, 0:128], src, idn[:])
        v.wait(r_tr)
        ev_ref[j] = v.tensor_copy(out=scoresT[:, j * 128:(j + 1) * 128],
                                  in_=psT1[0:8, 0:128])

    v.wait(sg_ref[NCH - 1])
    v.memset(lo[:], 0.0)
    v.memset(hi[:], 1.0)
    r_upd = v.memset(mid[:], 0.5)
    for i in range(BIS_ITERS):
        t.wait(r_upd)
        r_bc = t.matmul(psB, onen[:], mid[:], start=True, stop=True)
        v.wait(r_bc)
        for e in range(NE):
            r_cnt = v.tensor_scalar(out=cjunk[:],
                                    in0=scoresTok[:, e * NTILES_ALL:(e + 1) * NTILES_ALL],
                                    scalar1=psB[:, e:e + 1], scalar2=0.0,
                                    op0=OP.is_gt, op1=OP.add,
                                    accum_out=cntp[:, e:e + 1])
        t.wait(r_cnt)
        r_red = t.matmul(psC, onec[:], cntp[:], start=True, stop=True)
        v.wait(r_red)
        v.tensor_scalar(out=prd[:], in0=psC, scalar1=float(K0), scalar2=None,
                        op0=OP.is_ge)
        v.tensor_tensor(out=dmh[:], in0=mid[:], in1=lo[:], op=OP.subtract)
        v.tensor_tensor(out=u8t[:], in0=prd[:], in1=dmh[:], op=OP.mult)
        v.tensor_tensor(out=lo[:], in0=lo[:], in1=u8t[:], op=OP.add)
        v.tensor_tensor(out=dmh[:], in0=mid[:], in1=hi[:], op=OP.subtract)
        v.tensor_tensor(out=u8t[:], in0=prd[:], in1=dmh[:], op=OP.mult)
        v.tensor_tensor(out=hi[:], in0=hi[:], in1=dmh[:], op=OP.add)
        v.tensor_tensor(out=hi[:], in0=hi[:], in1=u8t[:], op=OP.subtract)
        v.tensor_tensor(out=mid[:], in0=lo[:], in1=hi[:], op=OP.add)
        r_upd = v.tensor_scalar(out=mid[:], in0=mid[:], scalar1=0.5, scalar2=None,
                                op0=OP.mult)

    t.wait(r_upd)
    r_th = t.matmul(psTh, lo[:], onen[0:1, 0:1], start=True, stop=True)
    v.wait(r_th)
    r_thr = v.tensor_copy(out=thr_c[:], in_=psTh)
    q.wait(r_thr)
    r_sth = q.dma("sth", out=thr_out, in_=thr_c[:])

    v.tensor_scalar(out=maskT[:], in0=scoresT[:], scalar1=thr_c[:], scalar2=None,
                    op0=OP.is_gt)
    v.tensor_tensor(out=gateT[:], in0=scoresT[:], in1=maskT[:], op=OP.mult)
    v.tensor_tensor_scan(out=posT[:], data0=ones8T[:], data1=maskT[:],
                         initial=0.0, op0=OP.mult, op1=OP.add)
    v.scalar_tensor_tensor(out=offT[:], in0=maskT[:], scalar=-BIG,
                           in1=posT[:], op0=OP.mult, op1=OP.add)
    r_mask = v.tensor_scalar(out=offT[:], in0=offT[:], scalar1=ecapbig[:],
                             scalar2=None, op0=OP.add)

    lb_ref = [None] * NT
    sc_refs = [[None] * NE for _ in range(NT)]
    for j in range(NT):
        bj = j % 2
        t.wait(r_mask)
        if j >= 1:
            t.wait(lb_ref[j - 1])
        tr1 = t.transpose(psT1[:, 0:8], offT[:, j * 128:(j + 1) * 128], idn[0:8, 0:8])
        tr2 = t.transpose(psT1[:, 8:16], gateT[:, j * 128:(j + 1) * 128], idn[0:8, 0:8])
        v.wait(tr1)
        v.wait(tr2)
        if j >= 2:
            for e in range(NE):
                v.wait(sc_refs[j - 2][e])
        v.tensor_copy(out=offu[bj][:], in_=psT1[:, 0:8])
        s2v = src2[bj][:].rearrange("p (a b) -> p a b", b=2)
        v.tensor_copy(out=s2v[:, :, 1], in_=psT1[:, 8:16])
        lb_ref[j] = v.tensor_copy(out=s2v[:, :, 0],
                                  in_=tokids[:, j:j + 1].to_broadcast([128, 8]))
        gq.wait(lb_ref[j])
        if j == 0:
            gq.wait(r_li)
        for e in range(NE):
            if j >= 1:
                gq.wait(sc_refs[j - 1][e])
            sc_refs[j][e] = gq.indirect(
                f"g{e}", out=lists0,
                out_offset=IndirectOffsetOnAxis(offu[bj][:, e:e + 1], 0),
                in_=src2[bj][:, 2 * e:2 * e + 2], in_offset=None,
                bounds=NE * CAP0 - 1, oob_is_err=False)

    xn_ref = [None] * NT
    st_ref = [None] * NT
    for j in range(NT):
        bj = j % 2
        if j >= 2:
            q.wait(xn_ref[j - 2])
        r_lx = q.dma(f"lx{bj}", out=xt[bj][:], in_=x_own[j * 128:(j + 1) * 128, :])
        v.wait(r_lx)
        if j >= 2:
            v.wait(st_ref[j - 2])
        xn_ref[j] = v.tensor_scalar(out=xnb[bj][:], in0=xt[bj][:],
                                    scalar1=muG[:, j:j + 1], scalar2=rs_all[:, j:j + 1],
                                    op0=OP.subtract, op1=OP.mult)
        q.wait(xn_ref[j])
        st_ref[j] = q.dma(f"sx{bj}", out=xn_out[j * 128:(j + 1) * 128, :], in_=xnb[bj][:])

    fence = [st_ref[NT - 2], st_ref[NT - 1], r_li, r_sth]
    fence += [sc_refs[NT - 1][e] for e in range(NE)]
    fence += [sc_refs[NT - 2][e] for e in range(NE)]
    q.fence(*fence)

    ctx.close()




def emit_stage_b(p, t, v, s, g, q, gq, bank, dr):
    nc = p.nc
    ctx = ExitStack()
    def sb(nm, shape, dt):
        return ctx.enter_context(nc.sbuf_tensor(f'b_'+nm, shape, dt))

    xn_in = dr["xn"]
    x_own = dr["x_own"]
    w_inT = dr["W_inT"]
    w_dtT = dr["W_dtT"]
    w_outT = dr["W_outT"]
    alog_r = dr["A_log_r"]
    bdt_r = dr["b_dt_r"]
    convw_r = dr["convw_r"]
    convb_r = dr["convb_r"]
    d_r = dr["D_r"]
    lng_r = dr["lng_r"]
    lnb_r = dr["lnb_r"]
    bsel_in = dr["bsel"]
    idnb_in = dr["ident_bf"]
    idnf_in = dr["ident_f"]
    onecb_in = dr["onec_bf"]
    onen_in = dr["ones_n"]
    consts_in = dr["consts"]

    x1a_out = dr["x1a"]

    winT = sb("winT", [128, NKI * MW], BF16)
    wdtT = sb("wdtT", [64, D_INNER], BF16)
    wo = [sb(f"wo{i}", [128, D], BF16) for i in range(2)]
    acols = sb("acols", [128, NDI * 16], F32)
    bdt = sb("bdt", [128, NDI], F32)
    cw_ = sb("cw_", [128, NDI * 4], F32)
    cb_ = sb("cb_", [128, NDI], F32)
    dr = sb("dr", [128, NDI], F32)
    lng = sb("lng", [128, NKI], F32)
    lnb = sb("lnb", [128, NKI], F32)
    bsel = sb("bsel_s", [16, 16 * 128], BF16)
    idnb = sb("idnb", [128, 128], BF16)
    idnf = sb("idnf", [128, 128], F32)
    onecb = sb("onecb", [128, 1], BF16)
    onen = sb("onen", [1, 128], F32)

    xnT = sb("xnT", [128, NKI * CW], BF16)
    browb = sb("browb", [16, CW], BF16)
    crowb = sb("crowb", [16, CW], BF16)
    dtlow = sb("dtlow", [64, CW], BF16)
    bb = sb("bb", [128, 16 * CW], BF16)
    mpre = bb[:, 0:NKI * CW]
    cbr = sb("cbr", [128, 16 * CW], BF16)
    xc = sb("xc", [128, CW + 3], F32)
    tails = sb("tails", [128, NDI * 3], F32)
    uu = [sb(f"uu{i}", [128, CW], BF16) for i in range(2)]
    zs = [sb(f"zs{i}", [128, CW], BF16) for i in range(2)]
    dtt = [sb(f"dtt{i}", [128, CW], F32) for i in range(2)]
    sp_ab = sb("sp_ab", [128, CW], F32)
    dtu = sb("dtu", [128, CW], BF16)
    ud = sb("ud", [128, CW], BF16)
    da = [sb(f"da{i}", [128, CW], F32) for i in range(2)]
    bet2 = [sb(f"bet2_{i}", [128, 2 * CW], BF16) for i in range(2)]
    hh2 = [sb(f"hh2_{i}", [128, 2 * CW], F32) for i in range(2)]
    hcm2 = [sb(f"hcm2_{i}", [128, 2 * CW], BF16) for i in range(2)]
    state = sb("state", [128, NDI * 16], F32)
    yz = sb("yz", [128, NDI * CW], BF16)
    att = yz[:, 0:NKI * CW]
    sqt = [sb(f"sqt{i}", [128, CW], BF16) for i in range(2)]
    mu_row = sb("mu_row", [1, CW], F32)
    mu2r = sb("mu2r", [1, CW], F32)
    var_row = sb("var_row", [1, CW], F32)
    rstd_row = sb("rstd_row", [1, CW], F32)
    mr_row = sb("mr_row", [1, CW], F32)
    at1 = sb("at1", [128, CW], F32)
    xtok = sb("xtok", [128, D], F32)
    x1t = [sb(f"x1t{i}", [128, D], F32) for i in range(2)]



    pre = []
    for dst, src in [(acols, alog_r), (bdt, bdt_r), (cw_, convw_r), (cb_, convb_r),
                     (dr, d_r), (lng, lng_r), (lnb, lnb_r), (bsel, bsel_in),
                     (idnb, idnb_in), (idnf, idnf_in), (onecb, onecb_in),
                     (onen, onen_in)]:
        pre.append(q.dma("pre", out=dst[:], in_=src))
    pre_done = pre[-1]
    w1 = gq.dma("win", out=winT[:].rearrange("p (a m) -> p a m", a=NKI),
                in_=w_inT.rearrange("(a p) m -> p a m", p=128))
    w2 = gq.dma("wdt", out=wdtT[:], in_=w_dtT)

    cbn = sb("cbn", [128, NDI], F32)
    s.wait(pre_done)
    r_ae = s.activation(out=acols[:], in_=acols[:], func=AF.Exp)
    v.wait(r_ae)
    v.tensor_scalar(out=acols[:], in0=acols[:], scalar1=-1.0, scalar2=None, op0=OP.mult)
    v.tensor_scalar(out=cbn[:], in0=cb_[:], scalar1=-1.0, scalar2=None, op0=OP.mult)
    v.memset(tails[:], 0.0)
    r_init = v.memset(state[:], 0.0)

    yz_ref = [None] * NDI
    relu_ref = [None] * NDI
    uD_ref = [None] * NDI
    zs_ref = [None] * NDI
    add_bank = {6: None, 7: None}
    prev_mpre_ref = None
    prev_var_ref = None
    prev_add_last = None
    prev_last_hc = None
    x1st_ref = [None, None]
    wo_mm_ref = [None] * NDI
    add_ref = [None] * NKI
    prev_hcm_mm = [None, None]
    scan_hist = [None, None]
    scan2_hist = [None, None]
    hc_hist = [None, None]

    for c in range(NCHB):
        t0 = c * CW
        xnt_refs = []
        for a in range(NKI):
            if prev_mpre_ref is not None and a == 0:
                q.wait(prev_mpre_ref)
            xnt_refs.append(q.dma_transpose(f"xt{a % 2}",
                                            out=xnT[:, a * CW:(a + 1) * CW],
                                            in_=xn_in[t0:t0 + CW, a * 128:(a + 1) * 128]))

        t.wait(xnt_refs[NKI - 2])
        t.wait(xnt_refs[NKI - 1])
        if c == 0:
            t.wait(w1)
        else:
            t.wait(prev_add_last)
        for k in range(NKI):
            r_m32 = t.matmul(bank[0][:], winT[:, k * MW + 32 * 128:k * MW + 33 * 128],
                             xnT[:, k * CW:(k + 1) * CW],
                             start=(k == 0), stop=(k == NKI - 1))
        v.wait(r_m32)
        if c == 0:
            v.wait(r_init)
        v.tensor_copy(out=browb[:], in_=bank[0][0:16, :])
        v.tensor_copy(out=crowb[:], in_=bank[0][32:48, :])
        r_dtl = v.tensor_copy(out=dtlow[:], in_=bank[0][64:128, :])

        ev_b = ev_c = None
        for n in range(16):
            if n == 0:
                t.wait(r_dtl)
            else:
                t.wait(ev_b)
            r_b1 = t.matmul(bank[2][:], bsel[:, n * 128:(n + 1) * 128], browb[:],
                            start=True, stop=True)
            s.wait(r_b1)
            if n == 0 and prev_last_hc is not None:
                s.wait(prev_last_hc)
            ev_b = s.copy(out=bb[:, n * CW:(n + 1) * CW], in_=bank[2][:])
            if n > 0:
                t.wait(ev_c)
            r_b2 = t.matmul(bank[3][:], bsel[:, n * 128:(n + 1) * 128], crowb[:],
                            start=True, stop=True)
            s.wait(r_b2)
            ev_c = s.copy(out=cbr[:, n * CW:(n + 1) * CW], in_=bank[3][:])
        bc_done = ev_c

        for dt in range(NDI):
            pj = bank[dt % 2]
            py = bank[4 + dt % 2]
            if c > 0 or dt >= 2:
                t.wait(relu_ref[(dt - 2) % NDI])
            if dt == 0:
                t.wait(r_dtl)
            for k in range(NKI):
                r_xs = t.matmul(pj[:], winT[:, k * MW + dt * 128:k * MW + (dt + 1) * 128],
                                xnT[:, k * CW:(k + 1) * CW],
                                start=(k == 0), stop=(k == NKI - 1))
            v.wait(r_xs)
            v.tensor_copy(out=xc[:, 3:3 + CW], in_=pj[:])
            v.tensor_copy(out=xc[:, 0:3], in_=tails[:, dt * 3:dt * 3 + 3])
            v.tensor_copy(out=tails[:, dt * 3:dt * 3 + 3], in_=xc[:, CW:CW + 3])
            v.tensor_scalar(out=at1[:], in0=xc[:, 0:CW],
                            scalar1=cw_[:, dt * 4:dt * 4 + 1], scalar2=None, op0=OP.mult)
            for jj in range(1, 4):
                r_cv = v.scalar_tensor_tensor(out=at1[:], in0=xc[:, jj:jj + CW],
                                              scalar=cw_[:, dt * 4 + jj:dt * 4 + jj + 1],
                                              in1=at1[:], op0=OP.mult, op1=OP.add)
            euv = bet2[0][:].bitcast(F32)[:, 0:CW]
            s.wait(r_cv)
            if scan_hist[0] is not None:
                s.wait(scan_hist[0])
            if scan_hist[1] is not None:
                s.wait(scan_hist[1])
            s.activation(out=euv, in_=at1[:], func=AF.Exp, scale=-1.0,
                         bias=cbn[:, dt:dt + 1])
            r_s2 = s.add(out=euv, in_=euv, add=1.0)
            v.wait(r_s2)
            v.reciprocal(out=euv, in_=euv)
            v.tensor_scalar(out=ud[:], in0=at1[:], scalar1=cb_[:, dt:dt + 1],
                            scalar2=None, op0=OP.add)
            r_u = v.tensor_tensor(out=uu[dt % 2][:], in0=ud[:], in1=euv, op=OP.mult)
            t.wait(r_cv)
            for k in range(NKI):
                r_zp = t.matmul(pj[:], winT[:, k * MW + (16 + dt) * 128:k * MW + (17 + dt) * 128],
                                xnT[:, k * CW:(k + 1) * CW],
                                start=(k == 0), stop=(k == NKI - 1))
            s.wait(r_zp)
            s.activation(out=sp_ab[:], in_=pj[:], func=AF.Exp, scale=-1.0)
            r_ez2 = s.add(out=sp_ab[:], in_=sp_ab[:], add=1.0)
            v.wait(r_ez2)
            v.reciprocal(out=sp_ab[:], in_=sp_ab[:])
            if c > 0 or dt >= 2:
                v.wait(yz_ref[(dt - 2) % NDI])
            r_zs = v.tensor_tensor(out=zs[dt % 2][:], in0=pj[:], in1=sp_ab[:],
                                   op=OP.mult)
            zs_ref[dt] = r_zs
            t.wait(r_zs)
            if c == 0 and dt == 0:
                t.wait(w2)
            r_dtp = t.matmul(pj[:], wdtT[:, dt * 128:(dt + 1) * 128], dtlow[:],
                             start=True, stop=True)
            s.wait(r_dtp)
            s.activation(out=sp_ab[:], in_=pj[:], func=AF.Abs, bias=bdt[:, dt:dt + 1])
            s.activation(out=sp_ab[:], in_=sp_ab[:], func=AF.Exp, scale=-1.0)
            s.activation(out=sp_ab[:], in_=sp_ab[:], func=AF.Ln, bias=1.0)
            relu_ref[dt] = s.activation(out=at1[:], in_=pj[:], func=AF.Relu,
                                        bias=bdt[:, dt:dt + 1])
            v.wait(relu_ref[dt])
            r_dt2 = v.tensor_tensor(out=dtt[dt % 2][:], in0=sp_ab[:], in1=at1[:], op=OP.add)
            v.wait(r_u)
            r_dtu = v.tensor_tensor(out=dtu[:], in0=dtt[dt % 2][:], in1=uu[dt % 2][:], op=OP.mult)
            uD_ref[dt] = v.tensor_scalar(out=ud[:], in0=uu[dt % 2][:],
                                         scalar1=dr[:, dt:dt + 1], scalar2=None, op0=OP.mult)
            r_ya = [None] * 8
            dtu3 = dtu[:].unsqueeze(1).to_broadcast([128, 2, CW])
            for np_ in range(8):
                n0 = 2 * np_
                pp = np_ % 2
                if np_ == 0:
                    s.wait(r_dt2)
                if scan_hist[0] is not None:
                    s.wait(scan_hist[0])
                r_da0 = s.activation(out=da[0][:], in_=dtt[dt % 2][:], func=AF.Exp,
                                     scale=acols[:, dt * 16 + n0:dt * 16 + n0 + 1])
                if scan_hist[1] is not None:
                    s.wait(scan_hist[1])
                r_da1 = s.activation(out=da[1][:], in_=dtt[dt % 2][:], func=AF.Exp,
                                     scale=acols[:, dt * 16 + n0 + 1:dt * 16 + n0 + 2])
                bp = bet2[np_ % 2]
                if scan2_hist[np_ % 2] is not None:
                    g.wait(scan2_hist[np_ % 2])
                g.wait(r_dtu)
                r_bet = g.tensor_tensor(out=bp[:].rearrange("p (a t) -> p a t", a=2),
                                        in0=dtu3,
                                        in1=bb[:, n0 * CW:(n0 + 2) * CW].rearrange("p (a t) -> p a t", a=2),
                                        op=OP.mult)
                if hc_hist[pp] is not None:
                    v.wait(hc_hist[pp])
                v.wait(r_bet)
                v.wait(r_da0)
                scan_hist[0] = v.tensor_tensor_scan(
                    out=hh2[pp][:, 0:CW], data0=da[0][:], data1=bp[:, 0:CW],
                    initial=state[:, dt * 16 + n0:dt * 16 + n0 + 1],
                    op0=OP.mult, op1=OP.add)
                v.wait(r_da1)
                scan_hist[1] = v.tensor_tensor_scan(
                    out=hh2[pp][:, CW:2 * CW], data0=da[1][:], data1=bp[:, CW:2 * CW],
                    initial=state[:, dt * 16 + n0 + 1:dt * 16 + n0 + 2],
                    op0=OP.mult, op1=OP.add)
                scan2_hist[np_ % 2] = scan_hist[1]
                v.tensor_copy(out=state[:, dt * 16 + n0:dt * 16 + n0 + 2],
                              in_=bass.AP(hh2[pp].tensor, hh2[pp].offset + CW - 1,
                                          [[hh2[pp].stride(0), 128], [CW, 2]])
                              if False else hh2[pp][:].rearrange("p (a t) -> p a t", a=2)[:, :, CW - 1])
                if dt == 0 and np_ == 0:
                    v.wait(bc_done)
                if np_ >= 2:
                    v.wait(r_ya[np_ - 2])
                elif prev_hcm_mm[np_] is not None:
                    v.wait(prev_hcm_mm[np_])
                r_hc = v.tensor_tensor(out=hcm2[pp][:], in0=hh2[pp][:],
                                       in1=cbr[:, n0 * CW:(n0 + 2) * CW], op=OP.mult)
                hc_hist[pp] = r_hc
                t.wait(r_hc)
                if np_ == 0 and (c > 0 or dt >= 2):
                    t.wait(yz_ref[(dt - 2) % NDI])
                t.matmul(py[:], idnb[:], hcm2[pp][:, 0:CW], start=(np_ == 0), stop=False)
                r_ya[np_] = t.matmul(py[:], idnb[:], hcm2[pp][:, CW:2 * CW],
                                     start=False, stop=False)
            prev_hcm_mm = [r_ya[6], r_ya[7]]
            prev_last_hc = r_hc
            t.wait(uD_ref[dt])
            r_yacc = t.matmul(py[:], idnb[:], ud[:], start=False, stop=True)
            v.wait(r_yacc)
            yz_ref[dt] = v.tensor_tensor(out=yz[:, dt * CW:(dt + 1) * CW],
                                         in0=py[:], in1=zs[dt % 2][:], op=OP.mult)

        for k in range(NDI):
            kp = k % 2
            if c > 0 or k >= 2:
                gq.wait(wo_mm_ref[(k - 2) % NDI])
            r_wl = gq.dma(f"wo{kp}", out=wo[kp][:], in_=w_outT[k * 128:(k + 1) * 128, :])
            t.wait(r_wl)
            t.wait(yz_ref[k])
            if k == 0:
                t.wait(yz_ref[NDI - 1])
            for o in range(NKI):
                r_wo = t.matmul(bank[o][:], wo[kp][:, o * 128:(o + 1) * 128],
                                yz[:, k * CW:(k + 1) * CW],
                                start=(k == 0), stop=(k == NDI - 1))
            wo_mm_ref[k] = r_wo
        wout_done = r_wo

        v.wait(wout_done)
        for o in range(NKI):
            r_mp = v.tensor_tensor(out=mpre[:, o * CW:(o + 1) * CW],
                                   in0=bank[o][:], in1=xnT[:, o * CW:(o + 1) * CW],
                                   op=OP.add)
        prev_mpre_ref = r_mp

        ts2 = [None] * NKI
        for o in range(NKI):
            if o == 0:
                s.wait(r_mp)
            elif o >= 2:
                s.wait(ts2[o - 2])
            r_sq = s.activation(out=sqt[o % 2][:], in_=mpre[:, o * CW:(o + 1) * CW],
                                func=AF.Square)
            t.wait(r_sq)
            if o == 0:
                t.wait(r_mp)
            t.matmul(bank[0][0:1, :], onecb[:], mpre[:, o * CW:(o + 1) * CW],
                     start=(o == 0), stop=(o == NKI - 1))
            ts2[o] = t.matmul(bank[1][0:1, :], onecb[:], sqt[o % 2][:],
                              start=(o == 0), stop=(o == NKI - 1))
        v.wait(ts2[NKI - 1])
        v.tensor_scalar(out=mu_row[:], in0=bank[0][0:1, :], scalar1=1.0 / D,
                        scalar2=None, op0=OP.mult)
        v.tensor_tensor(out=mu2r[:], in0=mu_row[:], in1=mu_row[:], op=OP.mult)
        r_var = v.scalar_tensor_tensor(out=var_row[:], in0=bank[1][0:1, :],
                                       scalar=1.0 / D, in1=mu2r[:],
                                       op0=OP.mult, op1=OP.subtract)
        prev_var_ref = r_var
        s.wait(r_var)
        s.activation(out=rstd_row[:], in_=var_row[:], func=AF.Ln, bias=LN_EPS)
        r_rstd = s.activation(out=rstd_row[:], in_=rstd_row[:], func=AF.Exp, scale=-0.5)
        v.wait(r_rstd)
        r_mr = v.tensor_tensor(out=mr_row[:], in0=mu_row[:], in1=rstd_row[:], op=OP.mult)
        t.wait(r_mr)
        t.wait(r_var)
        r_bc1 = t.matmul(bank[2][:], onen[:], rstd_row[:], start=True, stop=True)
        r_bc2 = t.matmul(bank[3][:], onen[:], mr_row[:], start=True, stop=True)
        v.wait(r_bc2)
        for o in range(NKI):
            v.tensor_tensor(out=at1[:], in0=mpre[:, o * CW:(o + 1) * CW],
                            in1=bank[2][:], op=OP.mult)
            v.tensor_tensor(out=at1[:], in0=at1[:], in1=bank[3][:], op=OP.subtract)
            r_att = v.tensor_scalar(out=att[:, o * CW:(o + 1) * CW], in0=at1[:],
                                    scalar1=lng[:, o:o + 1], scalar2=lnb[:, o:o + 1],
                                    op0=OP.mult, op1=OP.add)
        att_done = r_att

        prev_lx = None
        for s4 in range(4):
            sp = s4 % 2
            if x1st_ref[sp] is not None:
                q.wait(x1st_ref[sp])
            if prev_lx is not None:
                q.wait(add_ref[NKI - 1])
            r_lx = q.dma(f"x{sp}", out=xtok[:],
                         in_=x_own[t0 + s4 * 128:t0 + (s4 + 1) * 128, :])
            prev_lx = r_lx
            for o in range(NKI):
                bi = 6 + o % 2
                pt = bank[bi]
                if o == 0:
                    t.wait(att_done)
                if add_bank[bi] is not None:
                    t.wait(add_bank[bi])
                ptb = pt[:].bitcast(BF16)
                r_tr = t.transpose(ptb[:, 0:128],
                                   att[:, o * CW + s4 * 128:o * CW + (s4 + 1) * 128],
                                   idnb[:])
                v.wait(r_tr)
                v.wait(r_lx)
                add_ref[o] = v.tensor_tensor(out=x1t[sp][:, o * 128:(o + 1) * 128],
                                             in0=ptb[:, 0:128],
                                             in1=xtok[:, o * 128:(o + 1) * 128], op=OP.add)
                add_bank[bi] = add_ref[o]
            q.wait(add_ref[NKI - 1])
            x1st_ref[sp] = q.dma(f"sx{sp}", out=x1a_out[t0 + s4 * 128:t0 + (s4 + 1) * 128, :],
                                 in_=x1t[sp][:])
        prev_add_last = add_ref[NKI - 1]

    q.fence(x1st_ref[0], x1st_ref[1])
    ctx.close()




def emit_stage_c(p, t, v, s, g, q, gq, bank, dr):
    nc = p.nc
    ctx = ExitStack()
    def sb(nm, shape, dt):
        return ctx.enter_context(nc.sbuf_tensor(f'c_'+nm, shape, dt))

    xn_in = dr["xn"]
    x1a_in = dr["x1a"]
    lists_in = dr["lists0"]
    w1_in = dr["W1"]
    b1_in = dr["b1_r"]
    w2_in = dr["W2"]
    b2_in = dr["b2_r"]
    idnb_in = dr["ident_bf"]

    x1_out = dr["x1"]

    idnb_c = sb("idnb_c", [128, 128], BF16)
    lraw = sb("lraw", [128, NTPE * NE * 2], F32)
    idsu = sb("idsu", [128, NTPE * NE], U32)
    gts = sb("gts", [128, NTPE * NE], F32)
    b1s = sb("b1s", [128, NE * NFT], F32)
    b2s = sb("b2s", [128, NE * NKI], F32)
    w1sb = sb("w1sb", [128, NKI * FF], BF16)
    w2sb = sb("w2sb", [128, NFT * D], BF16)
    gx = [sb(f"gx{i}", [128, D], BF16) for i in range(2)]
    exT = [sb(f"exT{i}", [128, NKI * 512], BF16) for i in range(2)]
    hidT = sb("hidT", [128, NFT * 512], BF16)
    outT = sb("outT", [128, NKI * 512], BF16)
    otok = [sb(f"otok{i}", [128, D], F32) for i in range(2)]



    xcp = []
    for i in range(4):
        xcp.append(q.dma(f"xc{i}", out=x1_out[i * T // 4:(i + 1) * T // 4, :],
                         in_=x1a_in[i * T // 4:(i + 1) * T // 4, :]))

    r_id = q.dma("pre", out=idnb_c[:], in_=idnb_in)
    r_lr = q.dma("pre", out=lraw[:].rearrange("p (a n) -> p a n", n=2),
                 in_=lists_in.rearrange("(a p) n -> p a n", p=128))
    q.dma("pre", out=b1s[:], in_=b1_in)
    pre_done = q.dma("pre", out=b2s[:], in_=b2_in)

    v.wait(pre_done)
    lv = lraw[:].rearrange("p (a n) -> p a n", n=2)
    r_ids = v.tensor_copy(out=idsu[:], in_=lv[:, :, 0])
    r_gts = v.tensor_copy(out=gts[:], in_=lv[:, :, 1])

    w1_ref = gq.dma("w1", out=w1sb[:].rearrange("p (a f) -> p a f", a=NKI),
                    in_=w1_in[0].rearrange("(a p) f -> p a f", p=128))
    w2_ref = gq.dma("w2", out=w2sb[:].rearrange("p (a o) -> p a o", a=NFT),
                    in_=w2_in[0].rearrange("(a p) o -> p a o", p=128))

    r_bc = None
    prev_scat = None
    prev_gather = {}
    prev_exT_use = [None, None]
    prev_w1_last = None
    prev_w2_last = None
    prev_otok_use = [None, None]
    tr_bank = {6: None, 7: None}
    otr_bank = {4: None, 5: None}
    r_hid_done_prev = None

    for e in range(NE):
        g0 = 0
        for gi, gnt in enumerate(GRP):
            gp = (e * 3 + gi) % 2
            nslots = gnt * 128
            ex_done = None
            for jj in range(gnt):
                a = e * NTPE + g0 + jj
                bj = jj % 2
                if prev_gather.get(bj) is not None:
                    v.wait(prev_gather[bj])
                r_ms = v.memset(gx[bj][:], 0.0)
                gq.wait(r_ms)
                gq.wait(r_ids)
                r_g = gq.indirect(f"ga{bj}", out=gx[bj][:], out_offset=None,
                                  in_=xn_in,
                                  in_offset=IndirectOffsetOnAxis(idsu[:, a:a + 1], 0),
                                  bounds=T - 1, oob_is_err=False)
                for k in range(NKI):
                    bi = 6 + k % 2
                    ptb = bank[bi][:].bitcast(BF16)
                    t.wait(r_g)
                    if e == 0 and gi == 0 and jj == 0 and k == 0:
                        t.wait(pre_done)
                    if tr_bank[bi] is not None:
                        t.wait(tr_bank[bi])
                    r_tr = t.transpose(ptb[:, 0:128], gx[bj][:, k * 128:(k + 1) * 128],
                                       idnb_c[:])
                    v.wait(r_tr)
                    if prev_exT_use[gp] is not None and jj == 0 and k == 0:
                        v.wait(prev_exT_use[gp])
                    r_ev = v.tensor_copy(out=exT[gp][:, k * 512 + jj * 128:k * 512 + jj * 128 + 128],
                                         in_=ptb[:, 0:128])
                    tr_bank[bi] = r_ev
                prev_gather[bj] = r_ev
                ex_done = r_ev
            r_gelu = None
            for ft in range(NFT):
                pb = bank[ft % 2]
                t.wait(ex_done)
                if ft == 0:
                    t.wait(w1_ref)
                    if r_hid_done_prev is not None:
                        t.wait(r_hid_done_prev)
                if ft >= 2:
                    t.wait(r_gelu_prev2)
                for k in range(NKI):
                    r_w1 = t.matmul(pb[:, 0:nslots],
                                    w1sb[:, k * FF + ft * 128:k * FF + (ft + 1) * 128],
                                    exT[gp][:, k * 512:k * 512 + nslots],
                                    start=(k == 0), stop=(k == NKI - 1))
                s.wait(r_w1)
                r_gelu_prev2 = r_gelu if ft >= 1 else None
                r_gelu = s.activation(out=hidT[:, ft * 512:ft * 512 + nslots],
                                      in_=pb[:, 0:nslots], func=AF.Gelu,
                                      bias=b1s[:, e * NFT + ft:e * NFT + ft + 1])
                if ft == NFT - 1:
                    prev_w1_last = r_w1
            prev_exT_use[gp] = prev_w1_last
            if gi == len(GRP) - 1 and e < NE - 1:
                gq.wait(prev_w1_last)
                w1_ref = gq.dma("w1", out=w1sb[:].rearrange("p (a f) -> p a f", a=NKI),
                                in_=w1_in[e + 1].rearrange("(a p) f -> p a f", p=128))
            r_ot = None
            for o in range(NKI):
                pb = bank[2 + o % 2]
                if o == 0:
                    t.wait(r_gelu)
                    t.wait(w2_ref)
                if o >= 2:
                    t.wait(r_ot_prev2)
                for k in range(NFT):
                    r_w2 = t.matmul(pb[:, 0:nslots],
                                    w2sb[:, k * D + o * 128:k * D + (o + 1) * 128],
                                    hidT[:, k * 512:k * 512 + nslots],
                                    start=(k == 0), stop=(k == NFT - 1))
                v.wait(r_w2)
                r_ot_prev2 = r_ot if o >= 1 else None
                r_ot = v.tensor_scalar(out=outT[:, o * 512:o * 512 + nslots],
                                       in0=pb[:, 0:nslots],
                                       scalar1=b2s[:, e * NKI + o:e * NKI + o + 1],
                                       scalar2=None, op0=OP.add)
                if o == NKI - 1:
                    prev_w2_last = r_w2
            r_hid_done_prev = prev_w2_last
            if gi == len(GRP) - 1 and e < NE - 1:
                gq.wait(prev_w2_last)
                w2_ref = gq.dma("w2", out=w2sb[:].rearrange("p (a o) -> p a o", a=NFT),
                                in_=w2_in[e + 1].rearrange("(a p) o -> p a o", p=128))
            for jj in range(gnt):
                a = e * NTPE + g0 + jj
                op_ = jj % 2
                for o in range(NKI):
                    bi = 4 + o % 2
                    ptb = bank[bi][:].bitcast(BF16)
                    t.wait(r_ot)
                    if otr_bank[bi] is not None:
                        t.wait(otr_bank[bi])
                    r_tr2 = t.transpose(ptb[:, 0:128],
                                        outT[:, o * 512 + jj * 128:o * 512 + jj * 128 + 128],
                                        idnb_c[:])
                    v.wait(r_tr2)
                    if o == 0 and prev_otok_use[op_] is not None:
                        v.wait(prev_otok_use[op_])
                    r_go = v.tensor_scalar(out=otok[op_][:, o * 128:(o + 1) * 128],
                                           in0=ptb[:, 0:128],
                                           scalar1=gts[:, a:a + 1], scalar2=None,
                                           op0=OP.mult)
                    otr_bank[bi] = r_go
                gq.wait(r_go)
                if prev_scat is not None:
                    gq.wait(prev_scat)
                else:
                    for r in xcp:
                        gq.wait(r)
                prev_scat = gq.indirect(
                    "sc0", out=x1_out, out_offset=IndirectOffsetOnAxis(idsu[:, a:a + 1], 0),
                    in_=otok[op_][:], in_offset=None,
                    bounds=T - 1, oob_is_err=False, compute_op=OP.add)
                prev_otok_use[op_] = prev_scat
            g0 += gnt

    q.fence(prev_scat)
    ctx.close()




def emit_stage_d(p, t, v, s, g, q, gq, bank, dr):
    nc = p.nc
    ctx = ExitStack()
    def sb(nm, shape, dt):
        return ctx.enter_context(nc.sbuf_tensor(f'd_'+nm, shape, dt))

    x1_in = dr["x1"]
    wqkv_in = dr["Wqkv"]
    wo_in = dr["Wo"]
    bo_in = dr["bo_r"]
    cosT_in = dr["cosT"]
    sinT_in = dr["sinT"]
    p2_in = dr["P2"]
    masks_in = dr["masks"]
    onen_in = dr["ones_n"]
    consts_in = dr["consts"]

    xn1_out = dr["xn1"]
    attn1T_out = dr["attn1T"]

    qT_d = dr["qT_d"]
    kT_d = dr["kT_d"]
    asc_d = dr["asc_d"]

    wq = [sb(f"wq{i}", [128, NKI * 128], BF16) for i in range(2)]
    wv = sb("wv", [128, NKI * D], BF16)
    bo = sb("bo", [128, NKI], F32)
    cosT = sb("cosT_s", [128, T], BF16)
    sinT = sb("sinT_s", [128, T], BF16)
    p2 = sb("p2", [128, 128], BF16)
    msks = sb("msks", [128, 4 * 512], BF16)
    onen = sb("onen", [1, 128], F32)

    xt = [sb(f"xt{i}", [128, D], F32) for i in range(2)]
    st6 = sb("st6", [128, 12], F32)
    mv2 = sb("mv2", [128, 2], F32)
    rs_c = sb("rs_c", [128, 1], F32)
    xnb = [sb(f"xnb{i}", [128, D], BF16) for i in range(2)]

    xnT = sb("xnTd", [128, NKI * 512], BF16)
    qktmp = sb("qktmp", [128, 512], BF16)
    ro1 = sb("ro1", [128, 512], BF16)
    ro2 = sb("ro2", [128, 512], BF16)
    qstr = sb("qstr", [128, NKI * 512], BF16)
    ah = qstr[0:64, 0:NKI * 512]
    kstr = sb("kstr", [128, NKI * 512], BF16)
    vab = sb("vab", [128, 32 * NH * (HD + 1)], BF16)
    kbuf = [sb(f"kbuf{i}", [64, T], BF16) for i in range(2)]
    qh = [sb(f"qh{i}", [64, T], BF16) for i in range(2)]
    eT = [sb(f"eT{i}", [128, 512], BF16) for i in range(4)]
    avs = sb("avs", [64, 512], F32)
    zr = sb("zr", [1, 512], F32)
    aTc = sb("aTc", [128, NKI * 512], BF16)
    a1ev = [sb(f"a1ev{i}", [128, 512], F32) for i in range(2)]

    wo2 = [sb(f"wo2_{i}", [128, NKI * 128], BF16) for i in range(2)]



    pre = []
    for dst, src in [(bo, bo_in), (cosT, cosT_in), (sinT, sinT_in), (p2, p2_in),
                     (msks, masks_in), (onen, onen_in)]:
        pre.append(q.dma("pre", out=dst[:], in_=src))
    pre_done = pre[-1]
    r_wv = gq.dma("wv", out=wv[:].rearrange("p (a m) -> p a m", a=NKI),
                  in_=wqkv_in[:, 2 * D:3 * D].rearrange("(a p) m -> p a m", p=128))

    r_vz = v.memset(vab[:], 1.0)

    st_ref = [None] * NT
    xn_ref = [None] * NT
    for j in range(NT):
        bj = j % 2
        if j >= 2:
            q.wait(xn_ref[j - 2])
        r_lx = q.dma(f"lx{bj}", out=xt[bj][:], in_=x1_in[j * 128:(j + 1) * 128, :])
        v.wait(r_lx)
        if j == 0:
            v.wait(pre_done)
        v.bn_stats(out=st6[:, 0:6], in_=xt[bj][:, 0:512])
        v.bn_stats(out=st6[:, 6:12], in_=xt[bj][:, 512:1024])
        r_mv = v.bn_aggr(out=mv2[:], in_=st6[:])
        s.wait(r_mv)
        s.activation(out=rs_c[:], in_=mv2[:, 1:2], func=AF.Ln, bias=LN_EPS)
        r_rs = s.activation(out=rs_c[:], in_=rs_c[:], func=AF.Exp, scale=-0.5)
        v.wait(r_rs)
        if j >= 2:
            v.wait(st_ref[j - 2])
        xn_ref[j] = v.tensor_scalar(out=xnb[bj][:], in0=xt[bj][:],
                                    scalar1=mv2[:, 0:1], scalar2=rs_c[:],
                                    op0=OP.subtract, op1=OP.mult)
        q.wait(xn_ref[j])
        st_ref[j] = q.dma(f"sx{bj}", out=xn1_out[j * 128:(j + 1) * 128, :], in_=xnb[bj][:])

    r_q_store = r_k_store = None
    wq_ref = [None, None]
    wq_mm = [None, None]
    r_rope = None
    krot_use = None
    vab_ev = None
    for c in range(NCHB):
        t0 = c * CW
        xnt_refs = []
        for a in range(NKI):
            if a == 0:
                q.wait(st_ref[4 * c + 2])
                q.wait(st_ref[4 * c + 3])
                if c > 0:
                    q.wait(vab_ev)
                    q.wait(r_k_store)
            xnt_refs.append(q.dma_transpose(f"xc{a % 2}",
                                            out=xnT[:, a * CW:(a + 1) * CW],
                                            in_=xn1_out[t0:t0 + CW, a * 128:(a + 1) * 128]))
        xnt_done2 = xnt_refs[NKI - 2]
        xnt_done = xnt_refs[NKI - 1]
        for mi in range(16):
            mp = mi % 2
            if wq_mm[mp] is not None:
                gq.wait(wq_mm[mp])
            wq_ref[mp] = gq.dma(f"wq{mp}",
                                out=wq[mp][:].rearrange("p (a m) -> p a m", a=NKI),
                                in_=wqkv_in[:, mi * 128:(mi + 1) * 128].rearrange("(a p) m -> p a m", p=128))
            pb = bank[mp]
            t.wait(wq_ref[mp])
            t.wait(xnt_done)
            t.wait(xnt_done2)
            if r_rope is not None:
                t.wait(r_rope)
            for k in range(NKI):
                r_qk = t.matmul(pb[:], wq[mp][:, k * 128:(k + 1) * 128],
                                xnT[:, k * CW:(k + 1) * CW],
                                start=(k == 0), stop=(k == NKI - 1))
            wq_mm[mp] = r_qk
            s.wait(r_qk)
            if mi < 8:
                r_ev = s.activation(out=qktmp[:], in_=pb[:], func=AF.Copy, scale=0.125)
            else:
                r_ev = s.activation(out=qktmp[:], in_=pb[:], func=AF.Copy)
            t.wait(r_ev)
            if krot_use is not None:
                t.wait(krot_use)
            r_kr = t.matmul(bank[2][:], p2[:], qktmp[:], start=True, stop=True)
            v.wait(r_kr)
            dst = qstr if mi < 8 else kstr
            blk8 = mi % 8
            if mi < 8 and c >= 0 and r_q_store is not None and mi == 0:
                v.wait(r_q_store)
            if mi == 8 and r_k_store is not None:
                v.wait(r_k_store)
            v.tensor_tensor(out=ro1[:], in0=qktmp[:], in1=cosT[:, t0:t0 + CW], op=OP.mult)
            v.tensor_tensor(out=ro2[:], in0=bank[2][:], in1=sinT[:, t0:t0 + CW], op=OP.mult)
            krot_use = r_rope = v.tensor_tensor(out=dst[:, blk8 * CW:(blk8 + 1) * CW],
                                                in0=ro1[:], in1=ro2[:], op=OP.add)
            if mi == 7:
                q.wait(r_rope)
                r_q_store = q.dma("qs", out=qT_d[:, t0:t0 + CW].rearrange("(a p) t -> p a t", p=128),
                                  in_=qstr[:].rearrange("p (a t) -> p a t", a=NKI))
            if mi == 15:
                q.wait(r_rope)
                r_k_store = q.dma("ks", out=kT_d[:, t0:t0 + CW].rearrange("(a p) t -> p a t", p=128),
                                  in_=kstr[:].rearrange("p (a t) -> p a t", a=NKI))
        for tt in range(4):
            for nch in range(2):
                pb = bank[4 + nch]
                if vab_ev is not None and not (c == 0 and tt == 0):
                    t.wait(vab_ev)
                t.wait(r_wv)
                for k in range(NKI):
                    r_vmm = t.matmul(pb[:], xnT[:, k * CW + tt * 128:k * CW + (tt + 1) * 128],
                                     wv[:, k * D + nch * 512:k * D + (nch + 1) * 512],
                                     start=(k == 0), stop=(k == NKI - 1))
                v.wait(r_vmm)
                if c == 0 and tt == 0 and nch == 0:
                    v.wait(r_vz)
                itile = 4 * c + tt
                for hh in range(8):
                    h = nch * 8 + hh
                    vab_ev = v.tensor_copy(
                        out=vab[:, itile * NH * (HD + 1) + h * (HD + 1):
                                itile * NH * (HD + 1) + h * (HD + 1) + HD],
                        in_=pb[:, hh * HD:(hh + 1) * HD])

    ah_store = [None] * NH
    prev_avev = None
    r_cp_prev = None
    last_smm = [None, None]
    av_hist = {}
    exp_hist = {}
    for h in range(NH):
        hp = h % 2
        if last_smm[hp] is not None:
            q.wait(last_smm[hp])
        if h == 0:
            q.wait(r_k_store)
            q.wait(r_q_store)
        r_kb = q.dma(f"kb{hp}", out=kbuf[hp][:], in_=kT_d[64 * h:64 * h + 64, :])
        r_qb = q.dma(f"qb{hp}", out=qh[hp][:], in_=qT_d[64 * h:64 * h + 64, :])
        for jq in range(8):
            nlive = 4 * jq + 4
            r_av = None
            r_e_q = [None] * nlive
            for i in range(nlive + 1):
                if i < nlive:
                    pb = bank[i % 4]
                    if h == 0 and jq == 0 and i == 0:
                        t.wait(r_ev)
                        t.wait(vab_ev)
                    if i < 4:
                        t.wait(r_kb)
                        t.wait(r_qb)
                    if exp_hist.get(i % 4) is not None:
                        t.wait(exp_hist[i % 4])
                    r_s = t.matmul(pb[:], kbuf[hp][:, i * 128:(i + 1) * 128],
                                   qh[hp][:, jq * 512:(jq + 1) * 512],
                                   start=True, stop=True)
                    s.wait(r_s)
                    if av_hist.get(i % 4) is not None:
                        s.wait(av_hist[i % 4])
                    r_e = s.activation(out=eT[i % 4][:], in_=pb[:], func=AF.Exp,
                                       bias=EXP_SHIFT)
                    exp_hist[i % 4] = r_e
                    if i >= 4 * jq:
                        v.wait(r_e)
                        r_e = v.tensor_tensor(out=eT[i % 4][:], in0=eT[i % 4][:],
                                              in1=msks[:, (i - 4 * jq) * 512:(i - 4 * jq + 1) * 512],
                                              op=OP.mult)
                        exp_hist[i % 4] = r_e
                    r_e_q[i] = r_e
                j2 = i - 1
                if j2 >= 0:
                    t.wait(r_e_q[j2])
                    if j2 == 0:
                        if r_cp_prev is not None:
                            t.wait(r_cp_prev)
                        if prev_avev is not None:
                            t.wait(prev_avev)
                    r_av = t.matmul(bank[4][0:65, :],
                                    vab[:, j2 * NH * (HD + 1) + h * (HD + 1):
                                        j2 * NH * (HD + 1) + (h + 1) * (HD + 1)],
                                    eT[j2 % 4][:],
                                    start=(j2 == 0), stop=(j2 == nlive - 1))
                    av_hist[j2 % 4] = r_av
            last_smm[hp] = r_s
            v.wait(r_av)
            if jq == 0 and h >= 1:
                v.wait(ah_store[h - 1])
            r_rc = v.reciprocal(out=zr[:], in_=bank[4][64:65, :])
            r_cp_prev = v.tensor_copy(out=avs[:], in_=bank[4][0:64, :])
            t.wait(r_rc)
            r_zb = t.matmul(bank[5][0:64, :], onen[0:1, 0:64], zr[:],
                            start=True, stop=True)
            v.wait(r_zb)
            if h == 0 and jq == 0:
                v.wait(r_q_store)
            prev_avev = v.tensor_tensor(out=ah[:, jq * CW:(jq + 1) * CW],
                                        in0=avs[:], in1=bank[5][0:64, :], op=OP.mult)
        q.wait(prev_avev)
        ah_store[h] = q.dma(f"ah{hp}", out=asc_d[64 * h:64 * h + 64, :], in_=ah)

    r_a1 = None
    a1_store = [None, None]
    prev_wo_mm = [None, None]
    prev_wo_ld = [None, None]
    for jq in range(8):
        t0 = jq * CW
        if jq == 0:
            for hh in range(NH):
                q.wait(ah_store[hh])
        if prev_wo_mm[0] is not None:
            q.wait(prev_wo_mm[0])
        r_ac = q.dma(f"ac{jq % 2}", out=aTc[:].rearrange("p (a t) -> p a t", a=NKI),
                     in_=asc_d[:, t0:t0 + CW].rearrange("(a p) t -> p a t", p=128))
        for o in range(NKI):
            pb = bank[o % 2]
            if prev_wo_ld[o % 2] is not None:
                gq.wait(prev_wo_ld[o % 2])
            r_wol = gq.dma(f"wol{o % 2}",
                           out=wo2[o % 2][:].rearrange("p (a m) -> p a m", a=NKI),
                           in_=wo_in[:, o * 128:(o + 1) * 128].rearrange("(a p) m -> p a m", p=128))
            t.wait(r_ac)
            t.wait(r_wol)
            if r_a1 is not None:
                t.wait(r_a1)
            for k in range(NKI):
                r_womm = t.matmul(pb[:], wo2[o % 2][:, k * 128:(k + 1) * 128],
                                  aTc[:, k * CW:(k + 1) * CW],
                                  start=(k == 0), stop=(k == NKI - 1))
            prev_wo_ld[o % 2] = r_womm
            v.wait(r_womm)
            if a1_store[o % 2] is not None:
                v.wait(a1_store[o % 2])
            r_a1 = v.tensor_scalar(out=a1ev[o % 2][:], in0=pb[:],
                                   scalar1=bo[:, o:o + 1], scalar2=None, op0=OP.add)
            q.wait(r_a1)
            a1_store[o % 2] = q.dma(f"a1{o % 2}",
                                    out=attn1T_out[o * 128:(o + 1) * 128, t0:t0 + CW],
                                    in_=a1ev[o % 2][:])
            if o == NKI - 1:
                prev_wo_mm[0] = r_womm


    q.fence(a1_store[0], a1_store[1], st_ref[NT - 2], st_ref[NT - 1])
    ctx.close()




def emit_stage_e(p, t, v, s, g, q, gq, bank, dr):
    nc = p.nc
    ctx = ExitStack()
    def sb(nm, shape, dt):
        return ctx.enter_context(nc.sbuf_tensor(f'e_'+nm, shape, dt))

    x1_in = dr["x1"]
    xn1_in = dr["xn1"]
    attn1T_in = dr["attn1T"]
    nwbc_in = dr["nw_bc"]
    rwT_in = dr["router_wT"]
    sw1_in = dr["sw1"]
    sb1_in = dr["sb1_r"]
    sw2_in = dr["sw2"]
    sb2_in = dr["sb2_r"]
    ew1_in = dr["ew1"]
    eb1_in = dr["eb1_r"]
    ew2_in = dr["ew2"]
    eb2_in = dr["eb2_r"]
    idnf_in = dr["ident_f"]
    tokid_in = dr["tokid"]
    ecapbig_in = dr["ecapbig1"]
    li_in = dr["lists_init1"]
    onen_in = dr["ones_n"]
    consts_in = dr["consts"]

    out_f = dr["out"]

    normed_d = dr["normed_d"]
    lists1 = dr["lists1"]

    nwbc = sb("nwbc", [128, D], BF16)
    rwT = sb("rwT", [128, NKI * NE], BF16)
    sb1 = sb("sb1", [128, NFT], F32)
    sb2 = sb("sb2", [128, NKI], F32)
    eb1 = sb("eb1", [128, NE * NFT], F32)
    eb2 = sb("eb2", [128, NE * NKI], F32)
    idnf = sb("idnf_e", [128, 128], F32)
    idnb = sb("idnb_e", [128, 128], BF16)
    tokids = sb("tokids_e", [128, NT], F32)
    ecapbig = sb("ecapbig_e", [8, 1], F32)

    sqa = sb("sqa", [128, 1], F32)
    rs_c = sb("rs_ce", [128, 1], F32)

    nT = sb("nT", [128, NKI * 512], BF16)
    exT = [nT] * 2
    wbig1 = sb("wbig1", [128, NKI * FF], BF16)
    wbig2 = sb("wbig2", [128, NFT * D], BF16)
    hidc = sb("hidc", [128, NFT * 512], BF16)
    a1c = [sb(f"a1c{i}", [128, 512], F32) for i in range(2)]
    comb = sb("comb", [128, NKI * 512], F32)
    combb = comb[:].bitcast(BF16)
    xt = [combb[:, 0:D], combb[:, D:2 * D]]
    nrm = [combb[:, 2 * D:3 * D], combb[:, 3 * D:4 * D]]

    xtok = sb("xtok_e", [128, D], F32)
    otok = [sb(f"otok_e{i}", [128, D], F32) for i in range(2)]
    hidcF = hidc[:].bitcast(F32)
    gateC = hidcF[0:8, 0:1024]
    maskC = hidcF[0:8, 1024:2048]
    posC = hidcF[0:8, 2048:3072]
    carry = sb("carry", [8, 1], F32)
    scTc = sb("scTc", [8, 512], F32)
    ones8c = sb("ones8c", [8, 1], F32)
    sctok = sb("sctok", [128, 8], F32)
    top8 = sb("top8", [128, 8], F32)
    den = sb("den", [128, 1], F32)
    msk8 = sb("msk8", [128, 8], F32)
    gtok = sb("gtok", [128, NT * 8], F32)
    offu = [sb("offu_e0", [128, 8], U32)] * 2
    src2 = [sb("src2_e0", [128, 16], F32)] * 2

    lraw = xtok[:, 0:NTPE1 * NE * 2]
    idsu = xtok[:, 256:256 + NTPE1 * NE].bitcast(U32)
    gts = xtok[:, 384:384 + NTPE1 * NE]
    gx = [a1c[0][:].bitcast(BF16), a1c[1][:].bitcast(BF16)]




    pre = []
    for dst, src in [(nwbc, nwbc_in), (sb1, sb1_in), (sb2, sb2_in), (eb1, eb1_in),
                     (eb2, eb2_in), (idnf, idnf_in), (tokids, tokid_in),
                     (ecapbig, ecapbig_in)]:
        pre.append(q.dma("pre", out=dst[:], in_=src))
    pre_done = pre[-1]
    r_rw = gq.dma("rw", out=rwT[:].rearrange("p (a e) -> p a e", a=NKI),
                  in_=rwT_in.rearrange("(a p) e -> p a e", p=128))
    r_sw1 = gq.dma("w1", out=wbig1[:].rearrange("p (a f) -> p a f", a=NKI),
                   in_=sw1_in.rearrange("(a p) f -> p a f", p=128))
    r_idb = gq.dma("idb", out=idnb[:], in_=idnf_in)
    r_li = q.dma("sli", out=lists1, in_=li_in)

    st_ref = [None] * NT
    nm_ref = [None] * NT
    for j in range(NT):
        bj = j % 2
        if j >= 2:
            q.wait(nm_ref[j - 2])
        r_lx = q.dma(f"lx{bj}", out=xt[bj], in_=xn1_in[j * 128:(j + 1) * 128, :])
        s.wait(r_lx)
        if j == 0:
            s.wait(pre_done)
        if j >= 2:
            s.wait(nm_ref[j - 2])
        s.activation(out=exT[0][:, 0:D], in_=xt[bj], func=AF.Square, accum_out=sqa[:])
        s.activation(out=rs_c[:], in_=sqa[:], func=AF.Ln, bias=RMS_EPS, scale=1.0 / D)
        r_rs = s.activation(out=rs_c[:], in_=rs_c[:], func=AF.Exp, scale=-0.5)
        v.wait(r_rs)
        if j >= 2:
            v.wait(st_ref[j - 2])
        v.tensor_scalar(out=nrm[bj], in0=xt[bj], scalar1=rs_c[:], scalar2=None,
                        op0=OP.mult)
        nm_ref[j] = v.tensor_tensor(out=nrm[bj], in0=nrm[bj], in1=nwbc[:],
                                    op=OP.mult)
        q.wait(nm_ref[j])
        st_ref[j] = q.dma(f"sn{bj}", out=normed_d[j * 128:(j + 1) * 128, :], in_=nrm[bj])

    sc_tr = [None] * NT
    r_comb_st = [None, None]
    r_hid_use = None
    r_sh_ev = None
    r_sc_ev = None
    r_trs_last = None
    prev_nT_use = None
    sw2_ld = [None, None]
    sw2_mm = [None, None]
    a1_used = [None, None]
    sh_ev_bank = {4: None, 5: None}
    r_relu = None
    r_hid_last_prev = None
    ctr_bank = {6: None, 7: None}
    add_ref = [None] * NKI
    for c in range(NCHB):
        t0 = c * CW
        nt_refs = []
        for a in range(NKI):
            if a == 0:
                q.wait(st_ref[NT - 2])
                q.wait(st_ref[NT - 1])
                if prev_nT_use is not None:
                    q.wait(prev_nT_use)
            nt_refs.append(q.dma_transpose(f"ntc{a % 2}",
                                           out=nT[:, a * CW:(a + 1) * CW],
                                           in_=normed_d[t0:t0 + CW, a * 128:(a + 1) * 128]))
        nt_done2, nt_done = nt_refs[NKI - 2], nt_refs[NKI - 1]
        t.wait(nt_done)
        t.wait(nt_done2)
        if c == 0:
            t.wait(r_rw)
        if r_sc_ev is not None:
            t.wait(r_sc_ev)
        for k in range(NKI):
            r_scm = t.matmul(bank[2][0:8, :], rwT[:, k * NE:(k + 1) * NE],
                             nT[:, k * CW:(k + 1) * CW],
                             start=(k == 0), stop=(k == NKI - 1))
        s.wait(r_scm)
        if r_trs_last is not None:
            s.wait(r_trs_last)
        r_sc_ev = s.activation(out=scTc[:], in_=bank[2][0:8, :],
                               func=AF.Sigmoid)
        for ft in range(NFT):
            pb = bank[ft % 2]
            if c == 0 and ft == 0:
                t.wait(r_sw1)
            wref = r_relu_prev2 if ft >= 2 else r_hid_last_prev
            if wref is not None:
                t.wait(wref)
            for k in range(NKI):
                r_h = t.matmul(pb[:], wbig1[:, k * FF + ft * 128:k * FF + (ft + 1) * 128],
                               nT[:, k * CW:(k + 1) * CW],
                               start=(k == 0), stop=(k == NKI - 1))
            s.wait(r_h)
            r_relu_prev2 = r_relu if ft >= 1 else None
            r_relu = s.activation(out=hidc[:, ft * CW:(ft + 1) * CW], in_=pb[:],
                                  func=AF.Relu, bias=sb1[:, ft:ft + 1])
        r_hid_last_prev = r_relu
        prev_nT_use = r_h

        for o in range(NKI):
            op_ = o % 2
            if sw2_mm[op_] is not None:
                gq.wait(sw2_mm[op_])
            sw2_ld[op_] = gq.dma(f"sw2{op_}",
                                 out=wbig2[:, 0:NFT * 128].rearrange("p (a m) -> p a m", a=NFT)
                                 if False else wbig2[:, op_ * NFT * 128:(op_ + 1) * NFT * 128].rearrange("p (a m) -> p a m", a=NFT),
                                 in_=sw2_in[:, o * 128:(o + 1) * 128].rearrange("(a p) m -> p a m", p=128))
            pb = bank[4 + op_]
            t.wait(sw2_ld[op_])
            t.wait(r_relu)
            if sh_ev_bank[4 + op_] is not None:
                t.wait(sh_ev_bank[4 + op_])
            for k in range(NFT):
                r_sh = t.matmul(pb[:], wbig2[:, op_ * NFT * 128 + k * 128:op_ * NFT * 128 + (k + 1) * 128],
                                hidc[:, k * CW:(k + 1) * CW],
                                start=(k == 0), stop=(k == NFT - 1))
            sw2_mm[op_] = r_sh
            if a1_used[op_] is not None:
                q.wait(a1_used[op_])
            r_aload = q.dma(f"a1l{op_}", out=a1c[op_][:],
                            in_=attn1T_in[o * 128:(o + 1) * 128, t0:t0 + CW])
            v.wait(r_sh)
            v.wait(r_aload)
            v.tensor_scalar(out=comb[:, o * CW:(o + 1) * CW], in0=pb[:],
                            scalar1=sb2[:, o:o + 1], scalar2=None, op0=OP.add)
            r_sh_ev = v.tensor_tensor(out=comb[:, o * CW:(o + 1) * CW],
                                      in0=comb[:, o * CW:(o + 1) * CW],
                                      in1=a1c[op_][:], op=OP.add)
            a1_used[op_] = r_sh_ev
            sh_ev_bank[4 + op_] = r_sh_ev
        for s4 in range(4):
            sp = s4 % 2
            if r_comb_st[sp] is not None:
                q.wait(r_comb_st[sp])
            r_lx1 = q.dma(f"x1l{sp}", out=xtok[:],
                          in_=x1_in[t0 + s4 * 128:t0 + (s4 + 1) * 128, :]) if s4 < 2 else \
                q.dma(f"x1l{sp}", out=xtok[:], in_=x1_in[t0 + s4 * 128:t0 + (s4 + 1) * 128, :])
            for o in range(NKI):
                bi = 6 + o % 2
                pt = bank[bi]
                t.wait(r_sh_ev)
                if ctr_bank[bi] is not None:
                    t.wait(ctr_bank[bi])
                r_tr = t.transpose(pt[:, 0:128],
                                   comb[:, o * CW + s4 * 128:o * CW + (s4 + 1) * 128],
                                   idnf[:])
                v.wait(r_tr)
                v.wait(r_lx1)
                add_ref[o] = v.tensor_tensor(out=otok[sp][:, o * 128:(o + 1) * 128],
                                             in0=pt[:, 0:128],
                                             in1=xtok[:, o * 128:(o + 1) * 128], op=OP.add)
                ctr_bank[bi] = add_ref[o]
            q.wait(add_ref[NKI - 1])
            r_comb_st[sp] = q.dma(f"so{sp}", out=out_f[t0 + s4 * 128:t0 + (s4 + 1) * 128, :],
                                  in_=otok[sp][:])
        for tt in range(4):
            j = 4 * c + tt
            t.wait(r_sc_ev)
            if j >= 1 and sc_tr[j - 1] is not None:
                t.wait(sc_tr[j - 1])
            r_trs = t.transpose(bank[3][0:128, 0:8], scTc[:, tt * 128:(tt + 1) * 128],
                                idnf[0:8, 0:8])
            r_trs_last = r_trs
            v.wait(r_trs)
            v.tensor_copy(out=sctok[:], in_=bank[3][0:128, 0:8])
            v.max(out=top8[:], in_=sctok[:])
            v.tensor_tensor(out=den[:], in0=top8[:, 0:1], in1=top8[:, 1:2], op=OP.add)
            v.tensor_scalar(out=den[:], in0=den[:], scalar1=1e-6, scalar2=None, op0=OP.add)
            v.reciprocal(out=den[:], in_=den[:])
            v.tensor_scalar(out=msk8[:], in0=sctok[:], scalar1=top8[:, 1:2], scalar2=None,
                            op0=OP.is_ge)
            v.tensor_tensor(out=msk8[:], in0=sctok[:], in1=msk8[:], op=OP.mult)
            sc_tr[j] = v.tensor_scalar(out=gtok[:, j * 8:(j + 1) * 8], in0=msk8[:],
                                       scalar1=den[:], scalar2=None, op0=OP.mult)

    v.memset(ones8c[:], 1.0)
    v.memset(carry[:], 0.0)
    lb_ref = [None] * NT
    sc_refs = [[None] * NE for _ in range(NT)]
    gt_ev = None
    for ch4 in range(4):
        for jj in range(8):
            j = 8 * ch4 + jj
            t.wait(sc_tr[NT - 1] if j == 0 else sc_tr[j])
            if gt_ev is not None:
                t.wait(gt_ev)
            if ch4 > 0 and jj == 0:
                t.wait(lb_ref[8 * ch4 - 1])
            r_trg = t.transpose(bank[3][0:8, 0:128], gtok[:, j * 8:(j + 1) * 8], idnf[:])
            v.wait(r_trg)
            if j == 0:
                v.wait(sw2_mm[0])
                v.wait(sw2_mm[1])
            gt_ev = v.tensor_copy(out=gateC[:, jj * 128:(jj + 1) * 128],
                                  in_=bank[3][0:8, 0:128])
        v.tensor_scalar(out=maskC, in0=gateC, scalar1=0.0, scalar2=None,
                        op0=OP.is_gt)
        v.tensor_tensor_scan(out=posC, data0=ones8c[:].to_broadcast([8, 1024]),
                             data1=maskC, initial=carry[:, 0:1],
                             op0=OP.mult, op1=OP.add)
        v.tensor_copy(out=carry[:], in_=posC[:, 1023:1024])
        v.scalar_tensor_tensor(out=posC, in0=maskC, scalar=-BIG,
                               in1=posC, op0=OP.mult, op1=OP.add)
        r_mask = v.tensor_scalar(out=posC, in0=posC, scalar1=ecapbig[:],
                                 scalar2=None, op0=OP.add)
        for jj in range(8):
            j = 8 * ch4 + jj
            bj = j % 2
            t.wait(r_mask)
            if j >= 1:
                t.wait(lb_ref[j - 1])
            tr1 = t.transpose(bank[3][:, 0:8], posC[:, jj * 128:(jj + 1) * 128],
                              idnf[0:8, 0:8])
            tr2 = t.transpose(bank[3][:, 8:16], gateC[:, jj * 128:(jj + 1) * 128],
                              idnf[0:8, 0:8])
            v.wait(tr1)
            v.wait(tr2)
            if j >= 1:
                for e in range(NE):
                    v.wait(sc_refs[j - 1][e])
            v.tensor_copy(out=offu[bj][:], in_=bank[3][:, 0:8])
            s2v = src2[bj][:].rearrange("p (a b) -> p a b", b=2)
            v.tensor_copy(out=s2v[:, :, 1], in_=bank[3][:, 8:16])
            lb_ref[j] = v.tensor_copy(out=s2v[:, :, 0],
                                      in_=tokids[:, j:j + 1].to_broadcast([128, 8]))
            gq.wait(lb_ref[j])
            if j == 0:
                gq.wait(r_li)
            for e in range(NE):
                if j >= 1:
                    gq.wait(sc_refs[j - 1][e])
                sc_refs[j][e] = gq.indirect(
                    f"g{e}", out=lists1,
                    out_offset=IndirectOffsetOnAxis(offu[bj][:, e:e + 1], 0),
                    in_=src2[bj][:, 2 * e:2 * e + 2], in_offset=None,
                    bounds=NE * CAP1 - 1, oob_is_err=False)

    for e in range(NE):
        q.wait(sc_refs[NT - 1][e])
    r_ll = q.dma("llr", out=lraw.rearrange("p (a n) -> p a n", n=2),
                 in_=lists1.rearrange("(a p) n -> p a n", p=128))
    v.wait(r_ll)
    lv = lraw.rearrange("p (a n) -> p a n", n=2)
    r_ids = v.tensor_copy(out=idsu, in_=lv[:, :, 0])
    r_gts = v.tensor_copy(out=gts, in_=lv[:, :, 1])

    ew1_ref = gq.dma("w1", out=wbig1[:].rearrange("p (a f) -> p a f", a=NKI),
                     in_=ew1_in[0].rearrange("(a p) f -> p a f", p=128))
    ew2_ref = gq.dma("w2e", out=wbig2[:].rearrange("p (a o) -> p a o", a=NFT),
                     in_=ew2_in[0].rearrange("(a p) o -> p a o", p=128))

    prev_scat = None
    prev_gather = {}
    prev_exT_use = [None, None]
    prev_otok_use = [None, None]
    tr_bank = {6: None, 7: None}
    otr_bank = {4: None, 5: None}
    r_hid_done_prev = None
    GRP1 = [4, 4, 4]
    for e in range(NE):
        g0 = 0
        for gi, gnt in enumerate(GRP1):
            gp = (e * len(GRP1) + gi) % 2
            nslots = gnt * 128
            ex_done = None
            for jj in range(gnt):
                a = e * NTPE1 + g0 + jj
                bj = jj % 2
                if prev_gather.get(bj) is not None:
                    v.wait(prev_gather[bj])
                r_ms = v.memset(gx[bj][:, 0:D], 0.0)
                gq.wait(r_ms)
                gq.wait(r_ids)
                r_g = gq.indirect(f"ga{bj}", out=gx[bj][:, 0:D], out_offset=None,
                                  in_=normed_d,
                                  in_offset=IndirectOffsetOnAxis(idsu[:, a:a + 1], 0),
                                  bounds=T - 1, oob_is_err=False)
                for k in range(NKI):
                    bi = 6 + k % 2
                    ptb = bank[bi][:].bitcast(BF16)
                    t.wait(r_g)
                    if e == 0 and gi == 0 and jj == 0 and k == 0:
                        t.wait(r_idb)
                        t.wait(r_comb_st[0])
                        t.wait(r_comb_st[1])
                    if tr_bank[bi] is not None:
                        t.wait(tr_bank[bi])
                    r_tr = t.transpose(ptb[:, 0:128], gx[bj][:, k * 128:(k + 1) * 128],
                                       idnb[:])
                    v.wait(r_tr)
                    if prev_exT_use[gp] is not None and jj == 0 and k == 0:
                        v.wait(prev_exT_use[gp])
                    r_ev = v.tensor_copy(out=exT[gp][:, k * 512 + jj * 128:k * 512 + jj * 128 + 128],
                                         in_=ptb[:, 0:128])
                    tr_bank[bi] = r_ev
                prev_gather[bj] = r_ev
                ex_done = r_ev
            r_gelu = None
            for ft in range(NFT):
                pb = bank[ft % 2]
                t.wait(ex_done)
                if ft == 0:
                    t.wait(ew1_ref)
                    if r_hid_done_prev is not None:
                        t.wait(r_hid_done_prev)
                if ft >= 2:
                    t.wait(r_gelu_prev2)
                for k in range(NKI):
                    r_w1 = t.matmul(pb[:, 0:nslots],
                                    wbig1[:, k * FF + ft * 128:k * FF + (ft + 1) * 128],
                                    exT[gp][:, k * 512:k * 512 + nslots],
                                    start=(k == 0), stop=(k == NKI - 1))
                s.wait(r_w1)
                r_gelu_prev2 = r_gelu if ft >= 1 else None
                r_gelu = s.activation(out=hidc[:, ft * 512:ft * 512 + nslots],
                                      in_=pb[:, 0:nslots], func=AF.Relu,
                                      bias=eb1[:, e * NFT + ft:e * NFT + ft + 1])
                if ft == NFT - 1:
                    prev_w1_last = r_w1
            prev_exT_use[gp] = prev_w1_last
            if gi == len(GRP1) - 1 and e < NE - 1:
                gq.wait(prev_w1_last)
                ew1_ref = gq.dma("w1", out=wbig1[:].rearrange("p (a f) -> p a f", a=NKI),
                                 in_=ew1_in[e + 1].rearrange("(a p) f -> p a f", p=128))
            r_ot = None
            for o in range(NKI):
                pb = bank[2 + o % 2]
                if o == 0:
                    t.wait(r_gelu)
                    t.wait(ew2_ref)
                if o >= 2:
                    t.wait(r_ot_prev2)
                for k in range(NFT):
                    r_w2 = t.matmul(pb[:, 0:nslots],
                                    wbig2[:, k * D + o * 128:k * D + (o + 1) * 128],
                                    hidc[:, k * 512:k * 512 + nslots],
                                    start=(k == 0), stop=(k == NFT - 1))
                v.wait(r_w2)
                r_ot_prev2 = r_ot if o >= 1 else None
                r_ot = v.tensor_scalar(out=comb[:, o * 512:o * 512 + nslots],
                                       in0=pb[:, 0:nslots],
                                       scalar1=eb2[:, e * NKI + o:e * NKI + o + 1],
                                       scalar2=None, op0=OP.add)
                if o == NKI - 1:
                    prev_w2_last = r_w2
            r_hid_done_prev = prev_w2_last
            if gi == len(GRP1) - 1 and e < NE - 1:
                gq.wait(prev_w2_last)
                ew2_ref = gq.dma("w2e", out=wbig2[:].rearrange("p (a o) -> p a o", a=NFT),
                                 in_=ew2_in[e + 1].rearrange("(a p) o -> p a o", p=128))
            for jj in range(gnt):
                a = e * NTPE1 + g0 + jj
                op_ = jj % 2
                for o in range(NKI):
                    bi = 4 + o % 2
                    pt = bank[bi]
                    t.wait(r_ot)
                    if otr_bank[bi] is not None:
                        t.wait(otr_bank[bi])
                    r_tr2 = t.transpose(pt[:, 0:128],
                                        comb[:, o * 512 + jj * 128:o * 512 + jj * 128 + 128],
                                        idnf[:])
                    v.wait(r_tr2)
                    if o == 0 and prev_otok_use[op_] is not None:
                        v.wait(prev_otok_use[op_])
                    r_go = v.tensor_scalar(out=otok[op_][:, o * 128:(o + 1) * 128],
                                           in0=pt[:, 0:128],
                                           scalar1=gts[:, a:a + 1], scalar2=None,
                                           op0=OP.mult)
                    otr_bank[bi] = r_go
                gq.wait(r_go)
                if prev_scat is not None:
                    gq.wait(prev_scat)
                else:
                    gq.wait(r_comb_st[0])
                    gq.wait(r_comb_st[1])
                prev_scat = gq.indirect(
                    "sc0", out=out_f, out_offset=IndirectOffsetOnAxis(idsu[:, a:a + 1], 0),
                    in_=otok[op_][:], in_offset=None,
                    bounds=T - 1, oob_is_err=False, compute_op=OP.add)
                prev_otok_use[op_] = prev_scat
            g0 += gnt

    q.fence(prev_scat)
    ctx.close()




def build_fused():
    p = Prog()
    nc = p.nc
    f = np.float32
    dr = {}
    def din(nm, shape, dt):
        dr[nm] = p.din(nm, shape, dt)
    def dint(nm, shape, dt):
        dr[nm] = nc.dram_tensor(nm, shape, dt, kind="Internal").ap()

    din("x_own", [T, D], F32)
    din("x_allT", [D, TT_ALL], F32)
    din("Wg9", [D, 9], F32)
    din("ones_col", [128, 1], F32)
    din("ones_n", [1, 128], F32)
    din("tokid", [128, NT], F32)
    din("ecapbig", [8, 1], F32)
    din("lists_init", [128, NE * CAP0 * 2 // 128], F32)
    din("ident", [128, 128], F32)
    din("consts", [128, 8], F32)
    din("W_inT", [D, MW], F32)
    din("W_dtT", [DT_RANK, D_INNER], F32)
    din("W_outT", [D_INNER, D], F32)
    din("A_log_r", [128, NDI * 16], F32)
    din("b_dt_r", [128, NDI], F32)
    din("convw_r", [128, NDI * 4], F32)
    din("convb_r", [128, NDI], F32)
    din("D_r", [128, NDI], F32)
    din("lng_r", [128, NKI], F32)
    din("lnb_r", [128, NKI], F32)
    din("bsel", [16, 16 * 128], BF16)
    din("ident_bf", [128, 128], BF16)
    din("ident_f", [128, 128], F32)
    din("onec_bf", [128, 1], BF16)
    din("W1", [NE, D, FF], F32)
    din("b1_r", [128, NE * NFT], F32)
    din("W2", [NE, FF, D], F32)
    din("b2_r", [128, NE * NKI], F32)
    din("Wqkv", [D, 3 * D], F32)
    din("Wo", [D, D], F32)
    din("bo_r", [128, NKI], F32)
    din("cosT", [128, T], BF16)
    din("sinT", [128, T], BF16)
    din("P2", [128, 128], BF16)
    din("masks", [128, 4 * 512], BF16)
    din("nw_bc", [128, D], BF16)
    din("router_wT", [D, NE], F32)
    din("sw1", [D, FF], F32)
    din("sb1_r", [128, NFT], F32)
    din("sw2", [FF, D], F32)
    din("sb2_r", [128, NKI], F32)
    din("ew1", [NE, D, FF], F32)
    din("eb1_r", [128, NE * NFT], F32)
    din("ew2", [NE, FF, D], F32)
    din("eb2_r", [128, NE * NKI], F32)
    din("ecapbig1", [8, 1], F32)
    din("lists_init1", [NE * CAP1, 2], F32)

    dint("xn", [T, D], BF16)
    dint("lists0", [NE * CAP0, 2], F32)
    dint("thr", [8, 1], F32)
    dint("x1a", [T, D], F32)
    dint("x1", [T, D], F32)
    dint("xn1", [T, D], BF16)
    dint("attn1T", [D, T], F32)
    dint("qT_d", [D, T], BF16)
    dint("kT_d", [D, T], BF16)
    dint("asc_d", [D, T], BF16)
    dint("normed_d", [T, D], BF16)
    dint("lists1", [NE * CAP1, 2], F32)

    dr["out"] = p.dout("out", [T, D], F32)

    csts_g = p.sb("csts_g", [128, 8], F32)
    nc.const_aps.register(F32, LN_EPS, csts_g[:, 0:1])
    nc.const_aps.register(F32, -0.5, csts_g[:, 1:2])
    nc.const_aps.register(F32, -1.0, csts_g[:, 2:3])
    nc.const_aps.register(F32, 1.0, csts_g[:, 3:4])
    nc.const_aps.register(F32, 0.125, csts_g[:, 4:5])
    nc.const_aps.register(F32, EXP_SHIFT, csts_g[:, 5:6])
    nc.const_aps.register(F32, RMS_EPS, csts_g[:, 6:7])

    bank = [p.ps(f"bank{i}", [128, 512]) for i in range(8)]

    mkp = MKProg()
    t = mkp.eng("tensor")
    v = mkp.eng("vector")
    s = mkp.eng("scalar")
    g = mkp.eng("gpsimd")
    q = mkp.q("sync")
    gq = mkp.q("gpsimd")

    r_cst = q.dma("cst", out=csts_g[:], in_=dr["consts"])
    for eng in (t, v, s):
        eng.wait(r_cst)
    gq.wait(r_cst)

    emit_stage_a(p, t, v, s, g, q, gq, bank, dr)
    mkp.barrier()
    emit_stage_b(p, t, v, s, g, q, gq, bank, dr)
    mkp.barrier()
    emit_stage_c(p, t, v, s, g, q, gq, bank, dr)
    mkp.barrier()
    emit_stage_d(p, t, v, s, g, q, gq, bank, dr)
    mkp.barrier()
    emit_stage_e(p, t, v, s, g, q, gq, bank, dr)
    mkp.emit(p)
    return p


def stage_a_inputs(x_full, Wg, core):
    x_own = np.ascontiguousarray(x_full[core]).astype(np.float32)
    order = [core] + [b for b in range(B) if b != core]
    x_ord = x_full[order]
    x_allT = np.ascontiguousarray(x_ord.reshape(TT_ALL, D).T).astype(np.float32)
    wg9 = np.concatenate([Wg.astype(np.float32),
                          np.full((D, 1), 1.0 / D, np.float32)], axis=1)
    ecapbig = (np.arange(NE, dtype=np.float32) * CAP0 - 1.0 + BIG).reshape(NE, 1)
    li = np.zeros((NE * CAP0, 2), np.float32)
    li[:, 0] = BIG
    return {
        "x_own": x_own,
        "x_allT": x_allT,
        "Wg9": wg9,
        "ones_col": np.ones((128, 1), np.float32),
        "ones_n": np.ones((1, 128), np.float32),
        "tokid": np.arange(T, dtype=np.float32).reshape(NT, 128).T.copy(),
        "ecapbig": ecapbig,
        "lists_init": np.ascontiguousarray(
            li.reshape(-1, 128, 2).transpose(1, 0, 2).reshape(128, -1)),
        "ident": np.eye(128, dtype=np.float32),
        "consts": np.broadcast_to(np.array([LN_EPS, -0.5], np.float32), (128, 2)).copy(),
    }




def stage_b_inputs(xn, x_core, W_in, W_dt, b_dt, A_log, Dv, conv_w, conv_b, W_out,
                   ln_g, ln_b):
    f = np.float32
    Wp = np.zeros((MW, D), f)
    Wp[0:2048] = W_in[0:2048]
    Wp[2048:4096] = W_in[2048:4096]
    m32 = 32 * 128
    Wp[m32 + 0:m32 + 16] = W_in[4096:4112]
    Wp[m32 + 32:m32 + 48] = W_in[4112:4128]
    Wp[m32 + 64:m32 + 128] = W_in[4128:4192]
    bsel = np.zeros((16, 16 * 128), BF16NP)
    for n in range(16):
        bsel[n, n * 128:(n + 1) * 128] = 1
    return {
        "xn": xn,
        "x_own": None if x_core is None else x_core.astype(f),
        "W_inT": np.ascontiguousarray(Wp.T),
        "W_dtT": np.ascontiguousarray(W_dt.astype(f).T),
        "W_outT": np.ascontiguousarray(W_out.astype(f).T),
        "A_log_r": np.ascontiguousarray(
            A_log.astype(f).reshape(16, 128, 16).transpose(1, 0, 2).reshape(128, -1)),
        "b_dt_r": np.ascontiguousarray(b_dt.astype(f).reshape(16, 128).T),
        "convw_r": np.ascontiguousarray(
            conv_w.astype(f).reshape(16, 128, 4).transpose(1, 0, 2).reshape(128, -1)),
        "convb_r": np.ascontiguousarray(conv_b.astype(f).reshape(16, 128).T),
        "D_r": np.ascontiguousarray(Dv.astype(f).reshape(16, 128).T),
        "lng_r": np.ascontiguousarray(ln_g.astype(f).reshape(8, 128).T),
        "lnb_r": np.ascontiguousarray(ln_b.astype(f).reshape(8, 128).T),
        "bsel": bsel,
        "ident_bf": np.eye(128, dtype=BF16NP),
        "ident_f": np.eye(128, dtype=np.float32),
        "onec_bf": np.ones((128, 1), BF16NP),
        "ones_n": np.ones((1, 128), np.float32),
        "consts": np.broadcast_to(np.array([LN_EPS, -0.5, -1.0, 1.0], f), (128, 4)).copy(),
    }




def stage_c_inputs(xn, x1a, lists0, W1, b1, W2, b2):
    f = np.float32
    return {
        "xn": xn,
        "x1a": None if x1a is None else x1a.astype(f),
        "lists0": None if lists0 is None else lists0.astype(f),
        "W1": W1.astype(f),
        "b1_r": np.ascontiguousarray(
            b1.astype(f).reshape(NE, NFT, 128).transpose(2, 0, 1).reshape(128, -1)),
        "W2": W2.astype(f),
        "b2_r": np.ascontiguousarray(
            b2.astype(f).reshape(NE, NKI, 128).transpose(2, 0, 1).reshape(128, -1)),
        "ident_bf": np.eye(128, dtype=BF16NP),
    }




def stage_d_inputs(x1, Wqkv, Wo, bo):
    f = np.float32
    half = HD // 2
    inv_freq = 1.0 / 10000.0 ** (np.arange(half, dtype=f) / half)
    tv = np.arange(T, dtype=f)
    freqs = np.outer(tv, inv_freq)
    cos = np.repeat(np.cos(freqs), 2, axis=1)
    sin = np.repeat(np.sin(freqs), 2, axis=1)
    cosT = np.ascontiguousarray(np.vstack([cos.T, cos.T])).astype(BF16NP)
    sinT = np.ascontiguousarray(np.vstack([sin.T, sin.T])).astype(BF16NP)
    P64 = np.zeros((HD, HD), f)
    for dpr in range(HD):
        if dpr % 2 == 0:
            P64[dpr + 1, dpr] = -1.0
        else:
            P64[dpr - 1, dpr] = 1.0
    P2v = np.zeros((128, 128), f)
    P2v[:HD, :HD] = P64
    P2v[HD:, HD:] = P64
    masks = np.zeros((128, 4, 512), f)
    for r in range(4):
        for pp in range(128):
            masks[pp, r, r * 128 + pp:] = 1.0
    return {
        "x1": None if x1 is None else x1.astype(f),
        "Wqkv": Wqkv.astype(f),
        "Wo": Wo.astype(f),
        "bo_r": np.ascontiguousarray(bo.astype(f).reshape(NKI, 128).T),
        "cosT": cosT,
        "sinT": sinT,
        "P2": P2v.astype(BF16NP),
        "masks": np.ascontiguousarray(masks.reshape(128, -1)).astype(BF16NP),
        "ones_n": np.ones((1, 128), f),
        "consts": np.broadcast_to(np.array([LN_EPS, -0.5, 0.125, EXP_SHIFT], f),
                                  (128, 4)).copy(),
    }




def stage_e_inputs(x1, xn1, attn1T, nw, router_w, sw1, sb1, sw2, sb2,
                   ew1, eb1, ew2, eb2):
    f = np.float32
    ecapbig = (np.arange(NE, dtype=f) * CAP1 - 1.0 + BIG).reshape(NE, 1)
    li = np.zeros((NE * CAP1, 2), f)
    li[:, 0] = BIG
    return {
        "x1": None if x1 is None else x1.astype(f),
        "xn1": xn1,
        "attn1T": None if attn1T is None else attn1T.astype(f),
        "nw_bc": np.broadcast_to(nw.astype(f), (128, D)).astype(BF16NP).copy(),
        "router_wT": np.ascontiguousarray(router_w.astype(f).T),
        "sw1": sw1.astype(f),
        "sb1_r": np.ascontiguousarray(sb1.astype(f).reshape(NFT, 128).T),
        "sw2": sw2.astype(f),
        "sb2_r": np.ascontiguousarray(sb2.astype(f).reshape(NKI, 128).T),
        "ew1": ew1.astype(f),
        "eb1_r": np.ascontiguousarray(
            eb1.astype(f).reshape(NE, NFT, 128).transpose(2, 0, 1).reshape(128, -1)),
        "ew2": ew2.astype(f),
        "eb2_r": np.ascontiguousarray(
            eb2.astype(f).reshape(NE, NKI, 128).transpose(2, 0, 1).reshape(128, -1)),
        "ident_f": np.eye(128, dtype=f),
        "tokid": np.arange(T, dtype=f).reshape(NT, 128).T.copy(),
        "ecapbig1": ecapbig,
        "lists_init1": li,
        "ones_n": np.ones((1, 128), f),
        "consts": np.broadcast_to(np.array([RMS_EPS, -0.5, 0.0, 0.0], f), (128, 4)).copy(),
    }


def fused_inputs(ins, core):
    f = np.float32
    x = np.asarray(ins["L0_attn_x"], f)
    im = {}
    im.update(stage_a_inputs(x, np.asarray(ins["L0_ffn_Wg"], f), core))
    b = stage_b_inputs(None, x[core],
                       np.asarray(ins["L0_attn_W_in"], f),
                       np.asarray(ins["L0_attn_W_dt"], f),
                       np.asarray(ins["L0_attn_b_dt"], f),
                       np.asarray(ins["L0_attn_A_log"], f),
                       np.asarray(ins["L0_attn_D"], f),
                       np.asarray(ins["L0_attn_conv_w"], f),
                       np.asarray(ins["L0_attn_conv_b"], f),
                       np.asarray(ins["L0_attn_W_out"], f),
                       np.asarray(ins["L0_attn_ln_g"], f),
                       np.asarray(ins["L0_attn_ln_b"], f))
    b.pop("xn"); b.pop("x_own")
    im.update(b)
    c = stage_c_inputs(None, None, None,
                       np.asarray(ins["L0_ffn_W1"], f),
                       np.asarray(ins["L0_ffn_b1"], f),
                       np.asarray(ins["L0_ffn_W2"], f),
                       np.asarray(ins["L0_ffn_b2"], f))
    for k in ("xn", "x1a", "lists0"):
        c.pop(k)
    im.update(c)
    d = stage_d_inputs(None,
                       np.asarray(ins["L1_attn_Wqkv"], f),
                       np.asarray(ins["L1_attn_Wo"], f),
                       np.asarray(ins["L1_attn_bo"], f))
    d.pop("x1")
    im.update(d)
    e = stage_e_inputs(None, None, None,
                       np.asarray(ins["L1_ffn_norm_weight"], f),
                       np.asarray(ins["L1_ffn_router_w"], f),
                       np.asarray(ins["L1_ffn_sw1"], f),
                       np.asarray(ins["L1_ffn_sb1"], f),
                       np.asarray(ins["L1_ffn_sw2"], f),
                       np.asarray(ins["L1_ffn_sb2"], f),
                       np.asarray(ins["L1_ffn_ew1"], f),
                       np.asarray(ins["L1_ffn_eb1"], f),
                       np.asarray(ins["L1_ffn_ew2"], f),
                       np.asarray(ins["L1_ffn_eb2"], f))
    for k in ("x1", "xn1", "attn1T"):
        e.pop(k)
    im.update(e)
    im["consts"] = np.broadcast_to(np.array(
        [LN_EPS, -0.5, -1.0, 1.0, 0.125, EXP_SHIFT, RMS_EPS, 0.0], f),
        (128, 8)).copy()
    return im


_FPROG = []

def kernel(**ins):
    if not _FPROG:
        _FPROG.append(build_fused())
    p = _FPROG[0]
    in_maps = [fused_inputs(ins, c) for c in range(B)]
    res = run_bass_kernel_spmd(p.nc, in_maps, core_ids=list(range(B)))
    return np.stack([np.asarray(r["out"], np.float32) for r in res.results], 0)


# version 11
# speedup vs baseline: 1.0808x; 1.0684x over previous
import os
import concourse.bass as bass
import concourse.mybir as mybir

SELF_EDGE_WAITS = os.environ.get("FW_SELF_WAITS", "1") == "1"


class OpRef:
    __slots__ = ("eng", "idx")
    def __init__(self, eng, idx):
        self.eng = eng
        self.idx = idx


class DmaRef:
    __slots__ = ("q", "sem_name", "value")
    def __init__(self, q, sem_name, value):
        self.q = q
        self.sem_name = sem_name
        self.value = value


_SEQ = [0]

class _Rec:
    __slots__ = ("opname", "args", "kwargs", "waits", "inc_needed", "is_dma",
                 "dma_sem", "seq")
    def __init__(self, opname, args, kwargs, is_dma=False, dma_sem=None):
        self.opname = opname
        self.args = args
        self.kwargs = kwargs
        self.waits = []
        self.inc_needed = False
        self.is_dma = is_dma
        self.dma_sem = dma_sem
        _SEQ[0] += 1
        self.seq = _SEQ[0]


class Eng:
    def __init__(self, mk, name, self_edges=True):
        self.mk = mk
        self.name = name
        self.self_edges = self_edges
        self.recs = []
        self.pending_waits = []

    def wait(self, ref):
        assert ref is not None
        if isinstance(ref, OpRef) and not ref.eng.self_edges:
            ref.eng.mark(ref)
        self.pending_waits.append(ref)

    def __getattr__(self, opname):
        if opname.startswith("_"):
            raise AttributeError(opname)
        def rec(*args, **kwargs):
            r = _Rec(opname, args, kwargs)
            r.waits = self.pending_waits
            self.pending_waits = []
            self.recs.append(r)
            ref = OpRef(self, len(self.recs))
            if self.self_edges:
                r.inc_needed = True
            return ref
        return rec

    def mark(self, ref):
        self.recs[ref.idx - 1].inc_needed = True


class Q:
    def __init__(self, mk, name):
        self.mk = mk
        self.name = name
        self.recs = []
        self.pending_waits = []
        self.sem_counts = {}

    def wait(self, ref):
        if isinstance(ref, OpRef) and not ref.eng.self_edges:
            ref.eng.mark(ref)
        self.pending_waits.append(ref)

    def dma(self, sem_name, out, in_, **kwargs):
        key = (self.name, sem_name)
        self.mk.dma_sems.setdefault(sem_name, 0)
        self.mk.dma_sems[sem_name] += 16
        r = _Rec("dma_start", (), dict(out=out, in_=in_, **kwargs),
                 is_dma=True, dma_sem=sem_name)
        r.waits = self.pending_waits
        self.pending_waits = []
        self.recs.append(r)
        return DmaRef(self, sem_name, self.mk.dma_sems[sem_name])

    def dma_transpose(self, sem_name, out, in_):
        self.mk.dma_sems.setdefault(sem_name, 0)
        self.mk.dma_sems[sem_name] += 16
        r = _Rec("dma_start_transpose", (), dict(out=out, in_=in_),
                 is_dma=True, dma_sem=sem_name)
        r.waits = self.pending_waits
        self.pending_waits = []
        self.recs.append(r)
        return DmaRef(self, sem_name, self.mk.dma_sems[sem_name])

    def indirect(self, sem_name, out, out_offset, in_, in_offset,
                 bounds=None, **kwargs):
        self.mk.dma_sems.setdefault(sem_name, 0)
        self.mk.dma_sems[sem_name] += 16
        r = _Rec("indirect_dma_start", (),
                 dict(out=out, out_offset=out_offset, in_=in_,
                      in_offset=in_offset, _bounds=bounds, **kwargs),
                 is_dma=True, dma_sem=sem_name)
        r.waits = self.pending_waits
        self.pending_waits = []
        self.recs.append(r)
        return DmaRef(self, sem_name, self.mk.dma_sems[sem_name])

    def fence(self, *refs):
        r = _Rec("__fence__", (), {})
        r.waits = self.pending_waits + list(refs)
        self.pending_waits = []
        for ref in r.waits:
            if isinstance(ref, OpRef) and not ref.eng.self_edges:
                ref.eng.mark(ref)
        self.recs.append(r)

    def collective(self, sem_name, *args, **kwargs):
        self.mk.dma_sems.setdefault(sem_name, 0)
        self.mk.dma_sems[sem_name] += 16
        r = _Rec("collective_compute", args, kwargs, is_dma=True,
                 dma_sem=sem_name)
        r.waits = self.pending_waits
        self.pending_waits = []
        self.recs.append(r)
        return DmaRef(self, sem_name, self.mk.dma_sems[sem_name])


class MKProg:
    ENGINES = ("tensor", "vector", "scalar", "gpsimd")
    QUEUES = ("sync", "gpsimd", "scalar")

    def __init__(self):
        self.engs = {}
        self.qs = {}
        self.dma_sems = {}

    def eng(self, name):
        if name not in self.engs:
            self.engs[name] = Eng(self, name, self_edges=(name != "tensor"))
        return self.engs[name]

    def q(self, name):
        if name not in self.qs:
            self.qs[name] = Q(self, name)
        return self.qs[name]

    def barrier(self):
        refs = []
        for e in self.engs.values():
            if e.recs:
                refs.append(OpRef(e, len(e.recs)))
        for semname, val in self.dma_sems.items():
            refs.append(DmaRef(None, semname, val))
        for e in self.engs.values():
            for r in refs:
                if isinstance(r, OpRef) and r.eng is e:
                    continue
                e.wait(r)
        for qq in self.qs.values():
            for r in refs:
                if isinstance(r, OpRef):
                    qq.wait(r)
                else:
                    qq.wait(r)

    def emit(self, p):
        nc = p.nc
        esems = {n: p.sem(f"E_{n}") for n in self.engs}
        dsems = {n: p.sem(f"D_{n}") for n in self.dma_sems}

        inc_prefix = {}
        for n, e in self.engs.items():
            pref = []
            c = 0
            for r in e.recs:
                if r.inc_needed:
                    c += 1
                pref.append(c)
            inc_prefix[n] = pref

        def wait_val(ref):
            if isinstance(ref, OpRef):
                return esems[ref.eng.name], inc_prefix[ref.eng.name][ref.idx - 1]
            else:
                return dsems[ref.sem_name], ref.value

        blk = p.block()

        def emit_engine_ops(handle, ename, recs):
            own_sem = esems.get(ename)
            self_edges = ename in self.engs and self.engs[ename].self_edges
            prev_incs = 0
            for i, r in enumerate(recs):
                for w in r.waits:
                    sem, val = wait_val(w)
                    handle.wait_ge(sem, val)
                if SELF_EDGE_WAITS and self_edges and prev_incs > 0:
                    handle.wait_ge(own_sem, prev_incs)
                try:
                    inst = getattr(handle, r.opname)(*r.args, **r.kwargs)
                except Exception as ex:
                    print('FAIL op', ename, i, r.opname,
                          [getattr(a, 'shape', a) for a in r.args],
                          {k: getattr(vv, 'shape', None) for k, vv in r.kwargs.items()})
                    raise
                if r.inc_needed:
                    inst.then_inc(own_sem, 1)
                    prev_incs += 1

        def emit_queue_ops(handle, q, bounds_regs):
            for r in q.recs:
                for w in r.waits:
                    sem, val = wait_val(w)
                    handle.wait_ge(sem, val)
                if r.opname == "__fence__":
                    continue
                kw = dict(r.kwargs)
                if r.opname == "indirect_dma_start":
                    b = kw.pop("_bounds")
                    if b is not None:
                        if b not in bounds_regs:
                            bounds_regs[b] = handle.to_reg(b)
                        kw["bounds_check"] = bounds_regs[b]
                inst = getattr(handle, r.opname)(*r.args, **kw)
                inst.then_inc(dsems[r.dma_sem], 16)

        sections = {}
        for n in self.engs:
            sections[n] = ("eng", n)
        def make_body(ename):
            def body(handle):
                bounds_regs = {}
                has_e = ename in self.engs
                has_q = ename in self.qs
                if has_e and not has_q:
                    emit_engine_ops(handle, ename, self.engs[ename].recs)
                elif has_q and not has_e:
                    emit_queue_ops(handle, self.qs[ename], bounds_regs)
                else:
                    er = self.engs[ename].recs
                    qr = self.qs[ename].recs
                    own_sem = esems.get(ename)
                    self_edges = self.engs[ename].self_edges
                    prev_incs = 0
                    ei = qi = 0
                    while ei < len(er) or qi < len(qr):
                        take_e = qi >= len(qr) or (ei < len(er) and er[ei].seq < qr[qi].seq)
                        if take_e:
                            r = er[ei]; ei += 1
                            for w in r.waits:
                                sem, val = wait_val(w)
                                handle.wait_ge(sem, val)
                            if SELF_EDGE_WAITS and self_edges and prev_incs > 0:
                                handle.wait_ge(own_sem, prev_incs)
                            inst = getattr(handle, r.opname)(*r.args, **r.kwargs)
                            if r.inc_needed:
                                inst.then_inc(own_sem, 1)
                                prev_incs += 1
                        else:
                            r = qr[qi]; qi += 1
                            for w in r.waits:
                                sem, val = wait_val(w)
                                handle.wait_ge(sem, val)
                            if r.opname == "__fence__":
                                continue
                            kw = dict(r.kwargs)
                            if r.opname == "indirect_dma_start":
                                bnd = kw.pop("_bounds")
                                if bnd is not None:
                                    if bnd not in bounds_regs:
                                        bounds_regs[bnd] = handle.to_reg(bnd)
                                    kw["bounds_check"] = bounds_regs[bnd]
                            inst = getattr(handle, r.opname)(*r.args, **kw)
                            inst.then_inc(dsems[r.dma_sem], 16)
            return body

        used = set(self.engs) | set(self.qs)
        for ename in ("sync", "tensor", "scalar", "vector", "gpsimd"):
            if ename not in used:
                continue
            body = make_body(ename)
            getattr(blk, ename)(body)
        return esems, dsems


"""Megakernel builders for mk_prb_d2. Becomes kernel.py when complete.

Layouts: token-major (tokens on partitions) for LN/gathers; feature-major
("T") for matmuls (contraction on partitions). Host numpy = layout only.
Sync discipline: fw.py recording framework; self-edges on DVE/ACT/Pool;
<=1 outstanding DMA per semaphore; waits on exact recorded counts.
"""
import numpy as np
import ml_dtypes
from contextlib import ExitStack

import concourse.bass as bass
import concourse.mybir as mybir
from concourse.bass import IndirectOffsetOnAxis
from concourse.bass_utils import run_bass_kernel_spmd
from concourse.alu_op_type import AluOpType as OP


F32 = mybir.dt.float32
BF16 = mybir.dt.bfloat16
U32 = mybir.dt.uint32
AF = mybir.ActivationFunctionType
BF16NP = ml_dtypes.bfloat16

B = 8
T = 4096
D = 1024
TT_ALL = B * T
NE = 8
CAP0 = 1280
K0 = 8192
LN_EPS = 1e-5
BIS_ITERS = 28
BIG = 65536.0

NT = T // 128
NCH = TT_ALL // 512
NDT = D // 128
NTILES_ALL = TT_ALL // 128


class Prog:
    def __init__(self):
        self.nc = bass.Bass("TRN2", target_bir_lowering=False)
        self.ctx = ExitStack()

    def sem(self, name):
        return self.ctx.enter_context(self.nc.semaphore(name))

    def sb(self, name, shape, dtype):
        return self.ctx.enter_context(self.nc.sbuf_tensor(name, shape, dtype))

    def ps(self, name, shape, dtype=F32):
        return self.ctx.enter_context(self.nc.psum_tensor(name, shape, dtype))

    def din(self, name, shape, dtype):
        return self.nc.dram_tensor(name, shape, dtype, kind="ExternalInput").ap()

    def dout(self, name, shape, dtype):
        return self.nc.dram_tensor(name, shape, dtype, kind="ExternalOutput").ap()

    def block(self):
        return self.ctx.enter_context(self.nc.Block())


D_INNER = 2048
D_STATE = 16
DT_RANK = 64
NKI = D // 128
NDI = D_INNER // 128
MW = 33 * 128
NCHB = 8
CW = 512
FF = 4096
NFT = FF // 128
NTPE = CAP0 // 128
GRP = [4, 4, 2]
NH = 16
HD = 64
EXP_SHIFT = -10.0
CAP1 = 1536
NTPE1 = CAP1 // 128
RMS_EPS = 1e-8


def emit_stage_a(p, t, v, s, g, q, gq, bank, dr):
    nc = p.nc
    ctx = ExitStack()
    def sb(nm, shape, dt):
        return ctx.enter_context(nc.sbuf_tensor(f'a_'+nm, shape, dt))

    x_own = dr["x_own"]
    x_allT = dr["x_allT"]
    wg9 = dr["Wg9"]
    ones_col = dr["ones_col"]
    ones_n = dr["ones_n"]
    tokid = dr["tokid"]
    ecapbig_in = dr["ecapbig"]
    lists_init = dr["lists_init"]
    idn_in = dr["ident"]
    consts_in = dr["consts"]

    xn_out = dr["xn"]
    lists0 = dr["lists0"]
    thr_out = dr["thr"]

    xs = [sb(f"xs{i}", [128, NDT * 512], F32) for i in range(2)]
    sqs = [sb(f"sqs{i}", [128, NDT * 512], F32) for i in range(2)]
    wg_sb = sb("wg_sb", [128, NDT * 9], F32)
    onec = sb("onec", [128, 1], F32)
    onen = sb("onen", [1, 128], F32)
    crow = sb("crow", [1, 8], F32)
    cexp = sb("cexp", [128, 8], F32)
    tokids = sb("tokids", [128, NT], F32)
    ecapbig = sb("ecapbig_s", [8, 1], F32)
    li = sb("li", [128, NE * CAP0 * 2 // 128], F32)
    idn = sb("idn", [128, 128], F32)

    scoresTok = sb("scoresTok", [128, NE * NTILES_ALL], F32)
    rs_all = sb("rs_all", [128, NCH * 4], F32)
    muG = sb("muG", [128, NCH * 4], F32)
    var4 = [sb(f"var4_{i}", [128, 4], F32) for i in range(2)]
    rs4 = [sb(f"rs4_{i}", [128, 4], F32) for i in range(2)]
    mu2 = sb("mu2", [128, 1], F32)
    m2t = sb("m2t", [128, 8], F32)
    dlt = sb("dlt", [128, 8], F32)
    logit4 = [sb(f"logit4_{i}", [128, 32], F32) for i in range(2)]

    lo = sb("lo", [1, 8], F32)
    hi = sb("hi", [1, 8], F32)
    mid = sb("mid", [1, 8], F32)
    prd = sb("prd", [1, 8], F32)
    dmh = sb("dmh", [1, 8], F32)
    u8t = sb("u8t", [1, 8], F32)
    cntp = sb("cntp", [128, 8], F32)
    cjunk = sb("cjunk", [128, 256], F32)
    thr_c = sb("thr_c", [8, 1], F32)

    scoresT = sb("scoresT", [8, T], F32)
    maskT = sb("maskT", [8, T], F32)
    gateT = sb("gateT", [8, T], F32)
    ones8T = sb("ones8T", [8, T], F32)
    posT = sb("posT", [8, T], F32)
    offT = sb("offT", [8, T], F32)

    offu = [sb(f"offu{i}", [128, 8], U32) for i in range(2)]
    src2 = [sb(f"src2_{i}", [128, 16], F32) for i in range(2)]

    xt = [sb(f"xt{i}", [128, D], F32) for i in range(2)]
    xnb = [sb(f"xnb{i}", [128, D], BF16) for i in range(2)]

    psS = [bank[0][:, 0:40], bank[1][:, 0:40]]
    psQ = [bank[2][:, 0:4], bank[3][:, 0:4]]
    psT1 = bank[4][:, 0:128]
    psB = bank[5][:, 0:8]
    psC = bank[6][0:1, 0:8]
    psTh = bank[7][0:8, 0:1]


    pre = []
    pre.append(q.dma("pre", out=wg_sb[:].rearrange("p (a n) -> p a n", n=9),
                     in_=wg9.rearrange("(a p) n -> p a n", p=128)))
    pre.append(q.dma("pre", out=onec[:], in_=ones_col))
    pre.append(q.dma("pre", out=onen[:], in_=ones_n))
    pre.append(q.dma("pre", out=tokids[:], in_=tokid))
    pre.append(q.dma("pre", out=ecapbig[:], in_=ecapbig_in))
    pre.append(q.dma("pre", out=li[:], in_=lists_init))
    pre.append(q.dma("pre", out=idn[:], in_=idn_in))
    pre_done = pre[-1]

    q.wait(pre_done)
    r_li = q.dma("sli", out=lists0.rearrange("(a p) n -> p a n", p=128),
                 in_=li[:].rearrange("p (a n) -> p a n", n=2))

    t.wait(pre_done)
    for k in range(NDT):
        r_csum = t.matmul(psC, onec[:], wg_sb[:, 9 * k:9 * k + 8],
                          start=(k == 0), stop=(k == NDT - 1))
    v.wait(r_csum)
    r_crow = v.tensor_copy(out=crow[:], in_=psC)
    t.wait(r_crow)
    r_cb = t.matmul(psB, onen[:], crow[:], start=True, stop=True)
    v.wait(r_cb)
    r_cexp = v.tensor_copy(out=cexp[:], in_=psB)
    v.memset(ones8T[:], 1.0)

    te_ref = [None] * NCH
    v2_ref = [None] * NCH
    sg_ref = [None] * NCH
    sg_s_ref = [None] * NCH
    for c in range(NCH):
        b = c % 2
        if c >= 2:
            q.wait(te_ref[c - 2])
        r_lc = q.dma(f"lc{b}", out=xs[b][:].rearrange("p (a t) -> p a t", a=NDT),
                     in_=x_allT[:, c * 512:(c + 1) * 512].rearrange("(a p) t -> p a t", p=128))
        s.wait(r_lc)
        if c >= 2:
            s.wait(te_ref[c - 2])
        for k in range(NDT):
            r_sq = s.activation(out=sqs[b][:, k * 512:(k + 1) * 512],
                                in_=xs[b][:, k * 512:(k + 1) * 512], func=AF.Square)
        t.wait(r_lc)
        t.wait(r_sq)
        if c >= 2:
            t.wait(v2_ref[c - 2])
        for jj in range(4):
            for k in range(NDT):
                t.matmul(psS[b][:, 9 * jj:9 * jj + 9],
                         xs[b][:, k * 512 + jj * 128:k * 512 + (jj + 1) * 128],
                         wg_sb[:, 9 * k:9 * k + 9],
                         start=(k == 0), stop=(k == NDT - 1))
        for jj in range(4):
            for k in range(NDT):
                r_mm = t.matmul(psQ[b][:, jj:jj + 1],
                                sqs[b][:, k * 512 + jj * 128:k * 512 + (jj + 1) * 128],
                                onec[:], start=(k == 0), stop=(k == NDT - 1))
        te_ref[c] = r_mm
        v.wait(r_mm)
        if c >= 2:
            v.wait(sg_s_ref[c - 2])
        for jj in range(4):
            v.tensor_copy(out=muG[:, 4 * c + jj:4 * c + jj + 1],
                          in_=psS[b][:, 9 * jj + 8:9 * jj + 9])
            v.tensor_tensor(out=mu2[:], in0=muG[:, 4 * c + jj:4 * c + jj + 1],
                            in1=muG[:, 4 * c + jj:4 * c + jj + 1], op=OP.mult)
            r_v1 = v.scalar_tensor_tensor(out=var4[b][:, jj:jj + 1],
                                          in0=psQ[b][:, jj:jj + 1], scalar=1.0 / D,
                                          in1=mu2[:], op0=OP.mult, op1=OP.subtract)
        s.wait(r_v1)
        s.activation(out=rs4[b][:], in_=var4[b][:], func=AF.Ln, bias=LN_EPS)
        s.activation(out=rs4[b][:], in_=rs4[b][:], func=AF.Exp, scale=-0.5)
        r_rs = s.copy(out=rs_all[:, 4 * c:4 * c + 4], in_=rs4[b][:])
        v.wait(r_rs)
        for jj in range(4):
            v.tensor_scalar(out=m2t[:], in0=cexp[:],
                            scalar1=muG[:, 4 * c + jj:4 * c + jj + 1], scalar2=None,
                            op0=OP.mult)
            v.tensor_tensor(out=dlt[:], in0=psS[b][:, 9 * jj:9 * jj + 8],
                            in1=m2t[:], op=OP.subtract)
            r_v2 = v.tensor_scalar(out=logit4[b][:, 8 * jj:8 * jj + 8], in0=dlt[:],
                                   scalar1=rs_all[:, 4 * c + jj:4 * c + jj + 1],
                                   scalar2=None, op0=OP.mult)
        v2_ref[c] = r_v2
        s.wait(r_v2)
        s.activation(out=logit4[b][:], in_=logit4[b][:], func=AF.Exp, scale=-1.0)
        sg_s_ref[c] = s.add(out=logit4[b][:], in_=logit4[b][:], add=1.0)
        v.wait(sg_s_ref[c])
        dst = scoresTok[:].rearrange("p (e t) -> p t e", e=NE)[:, 4 * c:4 * c + 4, :]
        sg_ref[c] = v.reciprocal(out=dst,
                                 in_=logit4[b][:].rearrange("p (t e) -> p t e", e=NE))

    ev_ref = [None] * NT
    for j in range(NT):
        t.wait(sg_ref[j // 4])
        if j >= 1:
            t.wait(ev_ref[j - 1])
        src = scoresTok[:].rearrange("p (e t) -> p t e", e=NE)[:, j, :]
        r_tr = t.transpose(psT1[0:8, 0:128], src, idn[:])
        v.wait(r_tr)
        ev_ref[j] = v.tensor_copy(out=scoresT[:, j * 128:(j + 1) * 128],
                                  in_=psT1[0:8, 0:128])

    v.wait(sg_ref[NCH - 1])
    v.memset(lo[:], 0.0)
    v.memset(hi[:], 1.0)
    r_upd = v.memset(mid[:], 0.5)
    for i in range(BIS_ITERS):
        t.wait(r_upd)
        r_bc = t.matmul(psB, onen[:], mid[:], start=True, stop=True)
        v.wait(r_bc)
        for e in range(NE):
            r_cnt = v.tensor_scalar(out=cjunk[:],
                                    in0=scoresTok[:, e * NTILES_ALL:(e + 1) * NTILES_ALL],
                                    scalar1=psB[:, e:e + 1], scalar2=0.0,
                                    op0=OP.is_gt, op1=OP.add,
                                    accum_out=cntp[:, e:e + 1])
        t.wait(r_cnt)
        r_red = t.matmul(psC, onec[:], cntp[:], start=True, stop=True)
        v.wait(r_red)
        v.tensor_scalar(out=prd[:], in0=psC, scalar1=float(K0), scalar2=None,
                        op0=OP.is_ge)
        v.tensor_tensor(out=dmh[:], in0=mid[:], in1=lo[:], op=OP.subtract)
        v.tensor_tensor(out=u8t[:], in0=prd[:], in1=dmh[:], op=OP.mult)
        v.tensor_tensor(out=lo[:], in0=lo[:], in1=u8t[:], op=OP.add)
        v.tensor_tensor(out=dmh[:], in0=mid[:], in1=hi[:], op=OP.subtract)
        v.tensor_tensor(out=u8t[:], in0=prd[:], in1=dmh[:], op=OP.mult)
        v.tensor_tensor(out=hi[:], in0=hi[:], in1=dmh[:], op=OP.add)
        v.tensor_tensor(out=hi[:], in0=hi[:], in1=u8t[:], op=OP.subtract)
        v.tensor_tensor(out=mid[:], in0=lo[:], in1=hi[:], op=OP.add)
        r_upd = v.tensor_scalar(out=mid[:], in0=mid[:], scalar1=0.5, scalar2=None,
                                op0=OP.mult)

    t.wait(r_upd)
    r_th = t.matmul(psTh, lo[:], onen[0:1, 0:1], start=True, stop=True)
    v.wait(r_th)
    r_thr = v.tensor_copy(out=thr_c[:], in_=psTh)
    q.wait(r_thr)
    r_sth = q.dma("sth", out=thr_out, in_=thr_c[:])

    v.tensor_scalar(out=maskT[:], in0=scoresT[:], scalar1=thr_c[:], scalar2=None,
                    op0=OP.is_gt)
    v.tensor_tensor(out=gateT[:], in0=scoresT[:], in1=maskT[:], op=OP.mult)
    v.tensor_tensor_scan(out=posT[:], data0=ones8T[:], data1=maskT[:],
                         initial=0.0, op0=OP.mult, op1=OP.add)
    v.scalar_tensor_tensor(out=offT[:], in0=maskT[:], scalar=-BIG,
                           in1=posT[:], op0=OP.mult, op1=OP.add)
    r_mask = v.tensor_scalar(out=offT[:], in0=offT[:], scalar1=ecapbig[:],
                             scalar2=None, op0=OP.add)

    lb_ref = [None] * NT
    sc_refs = [[None] * NE for _ in range(NT)]
    for j in range(NT):
        bj = j % 2
        t.wait(r_mask)
        if j >= 1:
            t.wait(lb_ref[j - 1])
        tr1 = t.transpose(psT1[:, 0:8], offT[:, j * 128:(j + 1) * 128], idn[0:8, 0:8])
        tr2 = t.transpose(psT1[:, 8:16], gateT[:, j * 128:(j + 1) * 128], idn[0:8, 0:8])
        v.wait(tr1)
        v.wait(tr2)
        if j >= 2:
            for e in range(NE):
                v.wait(sc_refs[j - 2][e])
        v.tensor_copy(out=offu[bj][:], in_=psT1[:, 0:8])
        s2v = src2[bj][:].rearrange("p (a b) -> p a b", b=2)
        v.tensor_copy(out=s2v[:, :, 1], in_=psT1[:, 8:16])
        lb_ref[j] = v.tensor_copy(out=s2v[:, :, 0],
                                  in_=tokids[:, j:j + 1].to_broadcast([128, 8]))
        gq.wait(lb_ref[j])
        if j == 0:
            gq.wait(r_li)
        for e in range(NE):
            if j >= 1:
                gq.wait(sc_refs[j - 1][e])
            sc_refs[j][e] = gq.indirect(
                f"g{e}", out=lists0,
                out_offset=IndirectOffsetOnAxis(offu[bj][:, e:e + 1], 0),
                in_=src2[bj][:, 2 * e:2 * e + 2], in_offset=None,
                bounds=NE * CAP0 - 1, oob_is_err=False)

    xn_ref = [None] * NT
    st_ref = [None] * NT
    for j in range(NT):
        bj = j % 2
        if j >= 2:
            q.wait(xn_ref[j - 2])
        r_lx = q.dma(f"lx{bj}", out=xt[bj][:], in_=x_own[j * 128:(j + 1) * 128, :])
        v.wait(r_lx)
        if j >= 2:
            v.wait(st_ref[j - 2])
        xn_ref[j] = v.tensor_scalar(out=xnb[bj][:], in0=xt[bj][:],
                                    scalar1=muG[:, j:j + 1], scalar2=rs_all[:, j:j + 1],
                                    op0=OP.subtract, op1=OP.mult)
        q.wait(xn_ref[j])
        st_ref[j] = q.dma(f"sx{bj}", out=xn_out[j * 128:(j + 1) * 128, :], in_=xnb[bj][:])

    fence = [st_ref[NT - 2], st_ref[NT - 1], r_li, r_sth]
    fence += [sc_refs[NT - 1][e] for e in range(NE)]
    fence += [sc_refs[NT - 2][e] for e in range(NE)]
    q.fence(*fence)

    ctx.close()




def emit_stage_b(p, t, v, s, g, q, gq, bank, dr):
    nc = p.nc
    ctx = ExitStack()
    def sb(nm, shape, dt):
        return ctx.enter_context(nc.sbuf_tensor(f'b_'+nm, shape, dt))

    xn_in = dr["xn"]
    x_own = dr["x_own"]
    w_inT = dr["W_inT"]
    w_dtT = dr["W_dtT"]
    w_outT = dr["W_outT"]
    alog_r = dr["A_log_r"]
    bdt_r = dr["b_dt_r"]
    convw_r = dr["convw_r"]
    convb_r = dr["convb_r"]
    d_r = dr["D_r"]
    lng_r = dr["lng_r"]
    lnb_r = dr["lnb_r"]
    bsel_in = dr["bsel"]
    idnb_in = dr["ident_bf"]
    idnf_in = dr["ident_f"]
    onecb_in = dr["onec_bf"]
    onen_in = dr["ones_n"]
    consts_in = dr["consts"]

    x1a_out = dr["x1a"]

    winT = sb("winT", [128, NKI * MW], BF16)
    wdtT = sb("wdtT", [64, D_INNER], BF16)
    wo = [sb(f"wo{i}", [128, D], BF16) for i in range(2)]
    acols = sb("acols", [128, NDI * 16], F32)
    bdt = sb("bdt", [128, NDI], F32)
    cw_ = sb("cw_", [128, NDI * 4], F32)
    cb_ = sb("cb_", [128, NDI], F32)
    dr = sb("dr", [128, NDI], F32)
    lng = sb("lng", [128, NKI], F32)
    lnb = sb("lnb", [128, NKI], F32)
    bsel = sb("bsel_s", [16, 16 * 128], BF16)
    idnb = sb("idnb", [128, 128], BF16)
    idnf = sb("idnf", [128, 128], F32)
    onecb = sb("onecb", [128, 1], BF16)
    onen = sb("onen", [1, 128], F32)

    xnT = sb("xnT", [128, NKI * CW], BF16)
    browb = sb("browb", [16, CW], BF16)
    crowb = sb("crowb", [16, CW], BF16)
    dtlow = sb("dtlow", [64, CW], BF16)
    bb = sb("bb", [128, 16 * CW], BF16)
    mpre = bb[:, 0:NKI * CW]
    cbr = sb("cbr", [128, 16 * CW], BF16)
    xc = sb("xc", [128, CW + 3], F32)
    tails = sb("tails", [128, NDI * 3], F32)
    uu = [sb(f"uu{i}", [128, CW], BF16) for i in range(2)]
    zs = [sb(f"zs{i}", [128, CW], BF16) for i in range(2)]
    dtt = [sb(f"dtt{i}", [128, CW], F32) for i in range(2)]
    sp_ab = sb("sp_ab", [128, CW], F32)
    dtu = sb("dtu", [128, CW], BF16)
    ud = sb("ud", [128, CW], BF16)
    da = [sb(f"da{i}", [128, CW], F32) for i in range(2)]
    bet2 = [sb(f"bet2_{i}", [128, 2 * CW], BF16) for i in range(2)]
    hh2 = [sb(f"hh2_{i}", [128, 2 * CW], F32) for i in range(2)]
    hcm2 = [sb(f"hcm2_{i}", [128, 2 * CW], BF16) for i in range(2)]
    state = sb("state", [128, NDI * 16], F32)
    yz = sb("yz", [128, NDI * CW], BF16)
    att = yz[:, 0:NKI * CW]
    sqt = [sb(f"sqt{i}", [128, CW], BF16) for i in range(2)]
    mu_row = sb("mu_row", [1, CW], F32)
    mu2r = sb("mu2r", [1, CW], F32)
    var_row = sb("var_row", [1, CW], F32)
    rstd_row = sb("rstd_row", [1, CW], F32)
    mr_row = sb("mr_row", [1, CW], F32)
    at1 = sb("at1", [128, CW], F32)
    xtok = sb("xtok", [128, D], F32)
    x1t = [sb(f"x1t{i}", [128, D], F32) for i in range(2)]



    pre = []
    for dst, src in [(acols, alog_r), (bdt, bdt_r), (cw_, convw_r), (cb_, convb_r),
                     (dr, d_r), (lng, lng_r), (lnb, lnb_r), (bsel, bsel_in),
                     (idnb, idnb_in), (idnf, idnf_in), (onecb, onecb_in),
                     (onen, onen_in)]:
        pre.append(q.dma("pre", out=dst[:], in_=src))
    pre_done = pre[-1]
    w1 = gq.dma("win", out=winT[:].rearrange("p (a m) -> p a m", a=NKI),
                in_=w_inT.rearrange("(a p) m -> p a m", p=128))
    w2 = gq.dma("wdt", out=wdtT[:], in_=w_dtT)

    cbn = sb("cbn", [128, NDI], F32)
    s.wait(pre_done)
    r_ae = s.activation(out=acols[:], in_=acols[:], func=AF.Exp)
    v.wait(r_ae)
    v.tensor_scalar(out=acols[:], in0=acols[:], scalar1=-1.0, scalar2=None, op0=OP.mult)
    v.tensor_scalar(out=cbn[:], in0=cb_[:], scalar1=-1.0, scalar2=None, op0=OP.mult)
    v.memset(tails[:], 0.0)
    r_init = v.memset(state[:], 0.0)

    yz_ref = [None] * NDI
    relu_ref = [None] * NDI
    uD_ref = [None] * NDI
    zs_ref = [None] * NDI
    add_bank = {6: None, 7: None}
    prev_mpre_ref = None
    prev_var_ref = None
    prev_add_last = None
    prev_last_hc = None
    x1st_ref = [None, None]
    wo_mm_ref = [None] * NDI
    add_ref = [None] * NKI
    prev_hcm_mm = [None, None]
    scan_hist = [None, None]
    scan2_hist = [None, None]
    hc_hist = [None, None]

    for c in range(NCHB):
        t0 = c * CW
        xnt_refs = []
        for a in range(NKI):
            if prev_mpre_ref is not None and a == 0:
                q.wait(prev_mpre_ref)
            xnt_refs.append(q.dma_transpose(f"xt{a % 2}",
                                            out=xnT[:, a * CW:(a + 1) * CW],
                                            in_=xn_in[t0:t0 + CW, a * 128:(a + 1) * 128]))

        t.wait(xnt_refs[NKI - 2])
        t.wait(xnt_refs[NKI - 1])
        if c == 0:
            t.wait(w1)
        else:
            t.wait(prev_add_last)
        for k in range(NKI):
            r_m32 = t.matmul(bank[0][:], winT[:, k * MW + 32 * 128:k * MW + 33 * 128],
                             xnT[:, k * CW:(k + 1) * CW],
                             start=(k == 0), stop=(k == NKI - 1))
        v.wait(r_m32)
        if c == 0:
            v.wait(r_init)
        v.tensor_copy(out=browb[:], in_=bank[0][0:16, :])
        v.tensor_copy(out=crowb[:], in_=bank[0][32:48, :])
        r_dtl = v.tensor_copy(out=dtlow[:], in_=bank[0][64:128, :])

        ev_b = ev_c = None
        for n in range(16):
            if n == 0:
                t.wait(r_dtl)
            else:
                t.wait(ev_b)
            r_b1 = t.matmul(bank[2][:], bsel[:, n * 128:(n + 1) * 128], browb[:],
                            start=True, stop=True)
            s.wait(r_b1)
            if n == 0 and prev_last_hc is not None:
                s.wait(prev_last_hc)
            ev_b = s.copy(out=bb[:, n * CW:(n + 1) * CW], in_=bank[2][:])
            if n > 0:
                t.wait(ev_c)
            r_b2 = t.matmul(bank[3][:], bsel[:, n * 128:(n + 1) * 128], crowb[:],
                            start=True, stop=True)
            s.wait(r_b2)
            ev_c = s.copy(out=cbr[:, n * CW:(n + 1) * CW], in_=bank[3][:])
        bc_done = ev_c

        for dt in range(NDI):
            pj = bank[dt % 2]
            py = bank[4 + dt % 2]
            if c > 0 or dt >= 2:
                t.wait(relu_ref[(dt - 2) % NDI])
            if dt == 0:
                t.wait(r_dtl)
            for k in range(NKI):
                r_xs = t.matmul(pj[:], winT[:, k * MW + dt * 128:k * MW + (dt + 1) * 128],
                                xnT[:, k * CW:(k + 1) * CW],
                                start=(k == 0), stop=(k == NKI - 1))
            v.wait(r_xs)
            v.tensor_copy(out=xc[:, 3:3 + CW], in_=pj[:])
            v.tensor_copy(out=xc[:, 0:3], in_=tails[:, dt * 3:dt * 3 + 3])
            v.tensor_copy(out=tails[:, dt * 3:dt * 3 + 3], in_=xc[:, CW:CW + 3])
            v.tensor_scalar(out=at1[:], in0=xc[:, 0:CW],
                            scalar1=cw_[:, dt * 4:dt * 4 + 1], scalar2=None, op0=OP.mult)
            for jj in range(1, 4):
                r_cv = v.scalar_tensor_tensor(out=at1[:], in0=xc[:, jj:jj + CW],
                                              scalar=cw_[:, dt * 4 + jj:dt * 4 + jj + 1],
                                              in1=at1[:], op0=OP.mult, op1=OP.add)
            euv = bet2[0][:].bitcast(F32)[:, 0:CW]
            s.wait(r_cv)
            if scan_hist[0] is not None:
                s.wait(scan_hist[0])
            if scan_hist[1] is not None:
                s.wait(scan_hist[1])
            s.activation(out=euv, in_=at1[:], func=AF.Exp, scale=-1.0,
                         bias=cbn[:, dt:dt + 1])
            r_s2 = s.add(out=euv, in_=euv, add=1.0)
            v.wait(r_s2)
            v.reciprocal(out=euv, in_=euv)
            v.tensor_scalar(out=ud[:], in0=at1[:], scalar1=cb_[:, dt:dt + 1],
                            scalar2=None, op0=OP.add)
            r_u = v.tensor_tensor(out=uu[dt % 2][:], in0=ud[:], in1=euv, op=OP.mult)
            t.wait(r_cv)
            for k in range(NKI):
                r_zp = t.matmul(pj[:], winT[:, k * MW + (16 + dt) * 128:k * MW + (17 + dt) * 128],
                                xnT[:, k * CW:(k + 1) * CW],
                                start=(k == 0), stop=(k == NKI - 1))
            s.wait(r_zp)
            s.activation(out=sp_ab[:], in_=pj[:], func=AF.Exp, scale=-1.0)
            r_ez2 = s.add(out=sp_ab[:], in_=sp_ab[:], add=1.0)
            v.wait(r_ez2)
            v.reciprocal(out=sp_ab[:], in_=sp_ab[:])
            if c > 0 or dt >= 2:
                v.wait(yz_ref[(dt - 2) % NDI])
            r_zs = v.tensor_tensor(out=zs[dt % 2][:], in0=pj[:], in1=sp_ab[:],
                                   op=OP.mult)
            zs_ref[dt] = r_zs
            t.wait(r_zs)
            if c == 0 and dt == 0:
                t.wait(w2)
            r_dtp = t.matmul(pj[:], wdtT[:, dt * 128:(dt + 1) * 128], dtlow[:],
                             start=True, stop=True)
            s.wait(r_dtp)
            s.activation(out=sp_ab[:], in_=pj[:], func=AF.Abs, bias=bdt[:, dt:dt + 1])
            s.activation(out=sp_ab[:], in_=sp_ab[:], func=AF.Exp, scale=-1.0)
            s.activation(out=sp_ab[:], in_=sp_ab[:], func=AF.Ln, bias=1.0)
            relu_ref[dt] = s.activation(out=at1[:], in_=pj[:], func=AF.Relu,
                                        bias=bdt[:, dt:dt + 1])
            v.wait(relu_ref[dt])
            r_dt2 = v.tensor_tensor(out=dtt[dt % 2][:], in0=sp_ab[:], in1=at1[:], op=OP.add)
            v.wait(r_u)
            r_dtu = v.tensor_tensor(out=dtu[:], in0=dtt[dt % 2][:], in1=uu[dt % 2][:], op=OP.mult)
            uD_ref[dt] = v.tensor_scalar(out=ud[:], in0=uu[dt % 2][:],
                                         scalar1=dr[:, dt:dt + 1], scalar2=None, op0=OP.mult)
            r_ya = [None] * 8
            dtu3 = dtu[:].unsqueeze(1).to_broadcast([128, 2, CW])
            for np_ in range(8):
                n0 = 2 * np_
                pp = np_ % 2
                if np_ == 0:
                    s.wait(r_dt2)
                if scan_hist[0] is not None:
                    s.wait(scan_hist[0])
                r_da0 = s.activation(out=da[0][:], in_=dtt[dt % 2][:], func=AF.Exp,
                                     scale=acols[:, dt * 16 + n0:dt * 16 + n0 + 1])
                if scan_hist[1] is not None:
                    s.wait(scan_hist[1])
                r_da1 = s.activation(out=da[1][:], in_=dtt[dt % 2][:], func=AF.Exp,
                                     scale=acols[:, dt * 16 + n0 + 1:dt * 16 + n0 + 2])
                bp = bet2[np_ % 2]
                if hc_hist[pp] is not None:
                    v.wait(hc_hist[pp])
                v.tensor_tensor(out=bp[:].rearrange("p (a t) -> p a t", a=2),
                                in0=dtu3,
                                in1=bb[:, n0 * CW:(n0 + 2) * CW].rearrange("p (a t) -> p a t", a=2),
                                op=OP.mult)
                v.wait(r_da0)
                scan_hist[0] = v.tensor_tensor_scan(
                    out=hh2[pp][:, 0:CW], data0=da[0][:], data1=bp[:, 0:CW],
                    initial=state[:, dt * 16 + n0:dt * 16 + n0 + 1],
                    op0=OP.mult, op1=OP.add)
                v.wait(r_da1)
                scan_hist[1] = v.tensor_tensor_scan(
                    out=hh2[pp][:, CW:2 * CW], data0=da[1][:], data1=bp[:, CW:2 * CW],
                    initial=state[:, dt * 16 + n0 + 1:dt * 16 + n0 + 2],
                    op0=OP.mult, op1=OP.add)
                scan2_hist[np_ % 2] = scan_hist[1]
                v.tensor_copy(out=state[:, dt * 16 + n0:dt * 16 + n0 + 2],
                              in_=bass.AP(hh2[pp].tensor, hh2[pp].offset + CW - 1,
                                          [[hh2[pp].stride(0), 128], [CW, 2]])
                              if False else hh2[pp][:].rearrange("p (a t) -> p a t", a=2)[:, :, CW - 1])
                if dt == 0 and np_ == 0:
                    v.wait(bc_done)
                if np_ >= 2:
                    v.wait(r_ya[np_ - 2])
                elif prev_hcm_mm[np_] is not None:
                    v.wait(prev_hcm_mm[np_])
                r_hc = v.tensor_tensor(out=hcm2[pp][:], in0=hh2[pp][:],
                                       in1=cbr[:, n0 * CW:(n0 + 2) * CW], op=OP.mult)
                hc_hist[pp] = r_hc
                t.wait(r_hc)
                if np_ == 0 and (c > 0 or dt >= 2):
                    t.wait(yz_ref[(dt - 2) % NDI])
                t.matmul(py[:], idnb[:], hcm2[pp][:, 0:CW], start=(np_ == 0), stop=False)
                r_ya[np_] = t.matmul(py[:], idnb[:], hcm2[pp][:, CW:2 * CW],
                                     start=False, stop=False)
            prev_hcm_mm = [r_ya[6], r_ya[7]]
            prev_last_hc = r_hc
            t.wait(uD_ref[dt])
            r_yacc = t.matmul(py[:], idnb[:], ud[:], start=False, stop=True)
            v.wait(r_yacc)
            yz_ref[dt] = v.tensor_tensor(out=yz[:, dt * CW:(dt + 1) * CW],
                                         in0=py[:], in1=zs[dt % 2][:], op=OP.mult)

        for k in range(NDI):
            kp = k % 2
            if c > 0 or k >= 2:
                gq.wait(wo_mm_ref[(k - 2) % NDI])
            r_wl = gq.dma(f"wo{kp}", out=wo[kp][:], in_=w_outT[k * 128:(k + 1) * 128, :])
            t.wait(r_wl)
            t.wait(yz_ref[k])
            if k == 0:
                t.wait(yz_ref[NDI - 1])
            for o in range(NKI):
                r_wo = t.matmul(bank[o][:], wo[kp][:, o * 128:(o + 1) * 128],
                                yz[:, k * CW:(k + 1) * CW],
                                start=(k == 0), stop=(k == NDI - 1))
            wo_mm_ref[k] = r_wo
        wout_done = r_wo

        v.wait(wout_done)
        for o in range(NKI):
            r_mp = v.tensor_tensor(out=mpre[:, o * CW:(o + 1) * CW],
                                   in0=bank[o][:], in1=xnT[:, o * CW:(o + 1) * CW],
                                   op=OP.add)
        prev_mpre_ref = r_mp

        ts2 = [None] * NKI
        for o in range(NKI):
            if o == 0:
                s.wait(r_mp)
            elif o >= 2:
                s.wait(ts2[o - 2])
            r_sq = s.activation(out=sqt[o % 2][:], in_=mpre[:, o * CW:(o + 1) * CW],
                                func=AF.Square)
            t.wait(r_sq)
            if o == 0:
                t.wait(r_mp)
            t.matmul(bank[0][0:1, :], onecb[:], mpre[:, o * CW:(o + 1) * CW],
                     start=(o == 0), stop=(o == NKI - 1))
            ts2[o] = t.matmul(bank[1][0:1, :], onecb[:], sqt[o % 2][:],
                              start=(o == 0), stop=(o == NKI - 1))
        v.wait(ts2[NKI - 1])
        v.tensor_scalar(out=mu_row[:], in0=bank[0][0:1, :], scalar1=1.0 / D,
                        scalar2=None, op0=OP.mult)
        v.tensor_tensor(out=mu2r[:], in0=mu_row[:], in1=mu_row[:], op=OP.mult)
        r_var = v.scalar_tensor_tensor(out=var_row[:], in0=bank[1][0:1, :],
                                       scalar=1.0 / D, in1=mu2r[:],
                                       op0=OP.mult, op1=OP.subtract)
        prev_var_ref = r_var
        s.wait(r_var)
        s.activation(out=rstd_row[:], in_=var_row[:], func=AF.Ln, bias=LN_EPS)
        r_rstd = s.activation(out=rstd_row[:], in_=rstd_row[:], func=AF.Exp, scale=-0.5)
        v.wait(r_rstd)
        r_mr = v.tensor_tensor(out=mr_row[:], in0=mu_row[:], in1=rstd_row[:], op=OP.mult)
        t.wait(r_mr)
        t.wait(r_var)
        r_bc1 = t.matmul(bank[2][:], onen[:], rstd_row[:], start=True, stop=True)
        r_bc2 = t.matmul(bank[3][:], onen[:], mr_row[:], start=True, stop=True)
        v.wait(r_bc2)
        for o in range(NKI):
            v.tensor_tensor(out=at1[:], in0=mpre[:, o * CW:(o + 1) * CW],
                            in1=bank[2][:], op=OP.mult)
            v.tensor_tensor(out=at1[:], in0=at1[:], in1=bank[3][:], op=OP.subtract)
            r_att = v.tensor_scalar(out=att[:, o * CW:(o + 1) * CW], in0=at1[:],
                                    scalar1=lng[:, o:o + 1], scalar2=lnb[:, o:o + 1],
                                    op0=OP.mult, op1=OP.add)
        att_done = r_att

        prev_lx = None
        for s4 in range(4):
            sp = s4 % 2
            if x1st_ref[sp] is not None:
                q.wait(x1st_ref[sp])
            if prev_lx is not None:
                q.wait(add_ref[NKI - 1])
            r_lx = q.dma(f"x{sp}", out=xtok[:],
                         in_=x_own[t0 + s4 * 128:t0 + (s4 + 1) * 128, :])
            prev_lx = r_lx
            for o in range(NKI):
                bi = 6 + o % 2
                pt = bank[bi]
                if o == 0:
                    t.wait(att_done)
                if add_bank[bi] is not None:
                    t.wait(add_bank[bi])
                ptb = pt[:].bitcast(BF16)
                r_tr = t.transpose(ptb[:, 0:128],
                                   att[:, o * CW + s4 * 128:o * CW + (s4 + 1) * 128],
                                   idnb[:])
                v.wait(r_tr)
                v.wait(r_lx)
                add_ref[o] = v.tensor_tensor(out=x1t[sp][:, o * 128:(o + 1) * 128],
                                             in0=ptb[:, 0:128],
                                             in1=xtok[:, o * 128:(o + 1) * 128], op=OP.add)
                add_bank[bi] = add_ref[o]
            q.wait(add_ref[NKI - 1])
            x1st_ref[sp] = q.dma(f"sx{sp}", out=x1a_out[t0 + s4 * 128:t0 + (s4 + 1) * 128, :],
                                 in_=x1t[sp][:])
        prev_add_last = add_ref[NKI - 1]

    q.fence(x1st_ref[0], x1st_ref[1])
    ctx.close()




def emit_stage_c(p, t, v, s, g, q, gq, bank, dr):
    nc = p.nc
    ctx = ExitStack()
    def sb(nm, shape, dt):
        return ctx.enter_context(nc.sbuf_tensor(f'c_'+nm, shape, dt))

    xn_in = dr["xn"]
    x1a_in = dr["x1a"]
    lists_in = dr["lists0"]
    w1_in = dr["W1"]
    b1_in = dr["b1_r"]
    w2_in = dr["W2"]
    b2_in = dr["b2_r"]
    idnb_in = dr["ident_bf"]

    x1_out = dr["x1"]

    idnb_c = sb("idnb_c", [128, 128], BF16)
    lraw = sb("lraw", [128, NTPE * NE * 2], F32)
    idsu = sb("idsu", [128, NTPE * NE], U32)
    gts = sb("gts", [128, NTPE * NE], F32)
    b1s = sb("b1s", [128, NE * NFT], F32)
    b2s = sb("b2s", [128, NE * NKI], F32)
    w1sb = sb("w1sb", [128, NKI * FF], BF16)
    w2sb = sb("w2sb", [128, NFT * D], BF16)
    gx = [sb(f"gx{i}", [128, D], BF16) for i in range(2)]
    exT = [sb(f"exT{i}", [128, NKI * 512], BF16) for i in range(2)]
    hidT = sb("hidT", [128, NFT * 512], BF16)
    outT = sb("outT", [128, NKI * 512], BF16)
    otok = [sb(f"otok{i}", [128, D], F32) for i in range(2)]



    xcp = []
    for i in range(4):
        xcp.append(q.dma(f"xc{i}", out=x1_out[i * T // 4:(i + 1) * T // 4, :],
                         in_=x1a_in[i * T // 4:(i + 1) * T // 4, :]))

    r_id = q.dma("pre", out=idnb_c[:], in_=idnb_in)
    r_lr = q.dma("pre", out=lraw[:].rearrange("p (a n) -> p a n", n=2),
                 in_=lists_in.rearrange("(a p) n -> p a n", p=128))
    q.dma("pre", out=b1s[:], in_=b1_in)
    pre_done = q.dma("pre", out=b2s[:], in_=b2_in)

    v.wait(pre_done)
    lv = lraw[:].rearrange("p (a n) -> p a n", n=2)
    r_ids = v.tensor_copy(out=idsu[:], in_=lv[:, :, 0])
    r_gts = v.tensor_copy(out=gts[:], in_=lv[:, :, 1])

    w1_ref = gq.dma("w1", out=w1sb[:].rearrange("p (a f) -> p a f", a=NKI),
                    in_=w1_in[0].rearrange("(a p) f -> p a f", p=128))
    w2_ref = gq.dma("w2", out=w2sb[:].rearrange("p (a o) -> p a o", a=NFT),
                    in_=w2_in[0].rearrange("(a p) o -> p a o", p=128))

    r_bc = None
    prev_scat = None
    prev_gather = {}
    prev_exT_use = [None, None]
    prev_w1_last = None
    prev_w2_last = None
    prev_otok_use = [None, None]
    tr_bank = {6: None, 7: None}
    otr_bank = {4: None, 5: None}
    r_hid_done_prev = None

    for e in range(NE):
        g0 = 0
        for gi, gnt in enumerate(GRP):
            gp = (e * 3 + gi) % 2
            nslots = gnt * 128
            ex_done = None
            for jj in range(gnt):
                a = e * NTPE + g0 + jj
                bj = jj % 2
                if prev_gather.get(bj) is not None:
                    v.wait(prev_gather[bj])
                r_ms = v.memset(gx[bj][:], 0.0)
                gq.wait(r_ms)
                gq.wait(r_ids)
                r_g = gq.indirect(f"ga{bj}", out=gx[bj][:], out_offset=None,
                                  in_=xn_in,
                                  in_offset=IndirectOffsetOnAxis(idsu[:, a:a + 1], 0),
                                  bounds=T - 1, oob_is_err=False)
                for k in range(NKI):
                    bi = 6 + k % 2
                    ptb = bank[bi][:].bitcast(BF16)
                    t.wait(r_g)
                    if e == 0 and gi == 0 and jj == 0 and k == 0:
                        t.wait(pre_done)
                    if tr_bank[bi] is not None:
                        t.wait(tr_bank[bi])
                    r_tr = t.transpose(ptb[:, 0:128], gx[bj][:, k * 128:(k + 1) * 128],
                                       idnb_c[:])
                    v.wait(r_tr)
                    if prev_exT_use[gp] is not None and jj == 0 and k == 0:
                        v.wait(prev_exT_use[gp])
                    r_ev = v.tensor_copy(out=exT[gp][:, k * 512 + jj * 128:k * 512 + jj * 128 + 128],
                                         in_=ptb[:, 0:128])
                    tr_bank[bi] = r_ev
                prev_gather[bj] = r_ev
                ex_done = r_ev
            r_gelu = None
            for ft in range(NFT):
                pb = bank[ft % 2]
                t.wait(ex_done)
                if ft == 0:
                    t.wait(w1_ref)
                    if r_hid_done_prev is not None:
                        t.wait(r_hid_done_prev)
                if ft >= 2:
                    t.wait(r_gelu_prev2)
                for k in range(NKI):
                    r_w1 = t.matmul(pb[:, 0:nslots],
                                    w1sb[:, k * FF + ft * 128:k * FF + (ft + 1) * 128],
                                    exT[gp][:, k * 512:k * 512 + nslots],
                                    start=(k == 0), stop=(k == NKI - 1))
                s.wait(r_w1)
                r_gelu_prev2 = r_gelu if ft >= 1 else None
                r_gelu = s.activation(out=hidT[:, ft * 512:ft * 512 + nslots],
                                      in_=pb[:, 0:nslots], func=AF.Gelu,
                                      bias=b1s[:, e * NFT + ft:e * NFT + ft + 1])
                if ft == NFT - 1:
                    prev_w1_last = r_w1
            prev_exT_use[gp] = prev_w1_last
            if gi == len(GRP) - 1 and e < NE - 1:
                gq.wait(prev_w1_last)
                w1_ref = gq.dma("w1", out=w1sb[:].rearrange("p (a f) -> p a f", a=NKI),
                                in_=w1_in[e + 1].rearrange("(a p) f -> p a f", p=128))
            r_ot = None
            for o in range(NKI):
                pb = bank[2 + o % 2]
                if o == 0:
                    t.wait(r_gelu)
                    t.wait(w2_ref)
                if o >= 2:
                    t.wait(r_ot_prev2)
                for k in range(NFT):
                    r_w2 = t.matmul(pb[:, 0:nslots],
                                    w2sb[:, k * D + o * 128:k * D + (o + 1) * 128],
                                    hidT[:, k * 512:k * 512 + nslots],
                                    start=(k == 0), stop=(k == NFT - 1))
                v.wait(r_w2)
                r_ot_prev2 = r_ot if o >= 1 else None
                r_ot = v.tensor_scalar(out=outT[:, o * 512:o * 512 + nslots],
                                       in0=pb[:, 0:nslots],
                                       scalar1=b2s[:, e * NKI + o:e * NKI + o + 1],
                                       scalar2=None, op0=OP.add)
                if o == NKI - 1:
                    prev_w2_last = r_w2
            r_hid_done_prev = prev_w2_last
            if gi == len(GRP) - 1 and e < NE - 1:
                gq.wait(prev_w2_last)
                w2_ref = gq.dma("w2", out=w2sb[:].rearrange("p (a o) -> p a o", a=NFT),
                                in_=w2_in[e + 1].rearrange("(a p) o -> p a o", p=128))
            for jj in range(gnt):
                a = e * NTPE + g0 + jj
                op_ = jj % 2
                for o in range(NKI):
                    bi = 4 + o % 2
                    ptb = bank[bi][:].bitcast(BF16)
                    t.wait(r_ot)
                    if otr_bank[bi] is not None:
                        t.wait(otr_bank[bi])
                    r_tr2 = t.transpose(ptb[:, 0:128],
                                        outT[:, o * 512 + jj * 128:o * 512 + jj * 128 + 128],
                                        idnb_c[:])
                    v.wait(r_tr2)
                    if o == 0 and prev_otok_use[op_] is not None:
                        v.wait(prev_otok_use[op_])
                    r_go = v.tensor_scalar(out=otok[op_][:, o * 128:(o + 1) * 128],
                                           in0=ptb[:, 0:128],
                                           scalar1=gts[:, a:a + 1], scalar2=None,
                                           op0=OP.mult)
                    otr_bank[bi] = r_go
                gq.wait(r_go)
                if prev_scat is not None:
                    gq.wait(prev_scat)
                else:
                    for r in xcp:
                        gq.wait(r)
                prev_scat = gq.indirect(
                    "sc0", out=x1_out, out_offset=IndirectOffsetOnAxis(idsu[:, a:a + 1], 0),
                    in_=otok[op_][:], in_offset=None,
                    bounds=T - 1, oob_is_err=False, compute_op=OP.add)
                prev_otok_use[op_] = prev_scat
            g0 += gnt

    q.fence(prev_scat)
    ctx.close()




def emit_stage_d(p, t, v, s, g, q, gq, bank, dr):
    nc = p.nc
    ctx = ExitStack()
    def sb(nm, shape, dt):
        return ctx.enter_context(nc.sbuf_tensor(f'd_'+nm, shape, dt))

    x1_in = dr["x1"]
    wqkv_in = dr["Wqkv"]
    wo_in = dr["Wo"]
    bo_in = dr["bo_r"]
    cosT_in = dr["cosT"]
    sinT_in = dr["sinT"]
    p2_in = dr["P2"]
    masks_in = dr["masks"]
    onen_in = dr["ones_n"]
    consts_in = dr["consts"]

    xn1_out = dr["xn1"]
    attn1T_out = dr["attn1T"]

    qT_d = dr["qT_d"]
    kT_d = dr["kT_d"]
    asc_d = dr["asc_d"]

    wq = [sb(f"wq{i}", [128, NKI * 128], BF16) for i in range(2)]
    wv = sb("wv", [128, NKI * D], BF16)
    bo = sb("bo", [128, NKI], F32)
    cosT = sb("cosT_s", [128, T], BF16)
    sinT = sb("sinT_s", [128, T], BF16)
    p2 = sb("p2", [128, 128], BF16)
    msks = sb("msks", [128, 4 * 512], BF16)
    onen = sb("onen", [1, 128], F32)

    xt = [sb(f"xt{i}", [128, D], F32) for i in range(2)]
    st6 = sb("st6", [128, 12], F32)
    mv2 = sb("mv2", [128, 2], F32)
    rs_c = sb("rs_c", [128, 1], F32)
    xnb = [sb(f"xnb{i}", [128, D], BF16) for i in range(2)]

    xnT = sb("xnTd", [128, NKI * 512], BF16)
    qktmp = sb("qktmp", [128, 512], BF16)
    ro1 = sb("ro1", [128, 512], BF16)
    ro2 = sb("ro2", [128, 512], BF16)
    qstr = sb("qstr", [128, NKI * 512], BF16)
    ah2 = qstr
    kstr = sb("kstr", [128, NKI * 512], BF16)
    vab = sb("vab", [128, 32 * NH * (HD + 1)], BF16)
    kbuf = [sb(f"kbuf{i}", [128, T], BF16) for i in range(2)]
    qh = [sb(f"qh{i}", [128, T], BF16) for i in range(2)]
    eT = [sb(f"eT{i}", [128, 512], BF16) for i in range(4)]
    avs2 = sb("avs2", [128, 512], F32)
    zr2 = [sb(f"zr2_{i}", [1, 512], F32) for i in range(2)]
    aTc = sb("aTc", [128, NKI * 512], BF16)
    a1ev = [sb(f"a1ev{i}", [128, 512], F32) for i in range(2)]

    wo2 = [sb(f"wo2_{i}", [128, NKI * 128], BF16) for i in range(2)]



    pre = []
    for dst, src in [(bo, bo_in), (cosT, cosT_in), (sinT, sinT_in), (p2, p2_in),
                     (msks, masks_in), (onen, onen_in)]:
        pre.append(q.dma("pre", out=dst[:], in_=src))
    pre_done = pre[-1]
    r_wv = gq.dma("wv", out=wv[:].rearrange("p (a m) -> p a m", a=NKI),
                  in_=wqkv_in[:, 2 * D:3 * D].rearrange("(a p) m -> p a m", p=128))

    r_vz = v.memset(vab[:], 1.0)

    st_ref = [None] * NT
    xn_ref = [None] * NT
    for j in range(NT):
        bj = j % 2
        if j >= 2:
            q.wait(xn_ref[j - 2])
        r_lx = q.dma(f"lx{bj}", out=xt[bj][:], in_=x1_in[j * 128:(j + 1) * 128, :])
        v.wait(r_lx)
        if j == 0:
            v.wait(pre_done)
        v.bn_stats(out=st6[:, 0:6], in_=xt[bj][:, 0:512])
        v.bn_stats(out=st6[:, 6:12], in_=xt[bj][:, 512:1024])
        r_mv = v.bn_aggr(out=mv2[:], in_=st6[:])
        s.wait(r_mv)
        s.activation(out=rs_c[:], in_=mv2[:, 1:2], func=AF.Ln, bias=LN_EPS)
        r_rs = s.activation(out=rs_c[:], in_=rs_c[:], func=AF.Exp, scale=-0.5)
        v.wait(r_rs)
        if j >= 2:
            v.wait(st_ref[j - 2])
        xn_ref[j] = v.tensor_scalar(out=xnb[bj][:], in0=xt[bj][:],
                                    scalar1=mv2[:, 0:1], scalar2=rs_c[:],
                                    op0=OP.subtract, op1=OP.mult)
        q.wait(xn_ref[j])
        st_ref[j] = q.dma(f"sx{bj}", out=xn1_out[j * 128:(j + 1) * 128, :], in_=xnb[bj][:])

    r_q_store = r_k_store = None
    wq_ref = [None, None]
    wq_mm = [None, None]
    r_rope = None
    krot_use = None
    vab_ev = None
    for c in range(NCHB):
        t0 = c * CW
        xnt_refs = []
        for a in range(NKI):
            if a == 0:
                q.wait(st_ref[4 * c + 2])
                q.wait(st_ref[4 * c + 3])
                if c > 0:
                    q.wait(vab_ev)
                    q.wait(r_k_store)
            xnt_refs.append(q.dma_transpose(f"xc{a % 2}",
                                            out=xnT[:, a * CW:(a + 1) * CW],
                                            in_=xn1_out[t0:t0 + CW, a * 128:(a + 1) * 128]))
        xnt_done2 = xnt_refs[NKI - 2]
        xnt_done = xnt_refs[NKI - 1]
        for mi in range(16):
            mp = mi % 2
            if wq_mm[mp] is not None:
                gq.wait(wq_mm[mp])
            wq_ref[mp] = gq.dma(f"wq{mp}",
                                out=wq[mp][:].rearrange("p (a m) -> p a m", a=NKI),
                                in_=wqkv_in[:, mi * 128:(mi + 1) * 128].rearrange("(a p) m -> p a m", p=128))
            pb = bank[mp]
            t.wait(wq_ref[mp])
            t.wait(xnt_done)
            t.wait(xnt_done2)
            if r_rope is not None:
                t.wait(r_rope)
            for k in range(NKI):
                r_qk = t.matmul(pb[:], wq[mp][:, k * 128:(k + 1) * 128],
                                xnT[:, k * CW:(k + 1) * CW],
                                start=(k == 0), stop=(k == NKI - 1))
            wq_mm[mp] = r_qk
            s.wait(r_qk)
            if mi < 8:
                r_ev = s.activation(out=qktmp[:], in_=pb[:], func=AF.Copy, scale=0.125)
            else:
                r_ev = s.activation(out=qktmp[:], in_=pb[:], func=AF.Copy)
            t.wait(r_ev)
            if krot_use is not None:
                t.wait(krot_use)
            r_kr = t.matmul(bank[2][:], p2[:], qktmp[:], start=True, stop=True)
            v.wait(r_kr)
            dst = qstr if mi < 8 else kstr
            blk8 = mi % 8
            if mi < 8 and c >= 0 and r_q_store is not None and mi == 0:
                v.wait(r_q_store)
            if mi == 8 and r_k_store is not None:
                v.wait(r_k_store)
            v.tensor_tensor(out=ro1[:], in0=qktmp[:], in1=cosT[:, t0:t0 + CW], op=OP.mult)
            v.tensor_tensor(out=ro2[:], in0=bank[2][:], in1=sinT[:, t0:t0 + CW], op=OP.mult)
            krot_use = r_rope = v.tensor_tensor(out=dst[:, blk8 * CW:(blk8 + 1) * CW],
                                                in0=ro1[:], in1=ro2[:], op=OP.add)
            if mi == 7:
                q.wait(r_rope)
                r_q_store = q.dma("qs", out=qT_d[:, t0:t0 + CW].rearrange("(a p) t -> p a t", p=128),
                                  in_=qstr[:].rearrange("p (a t) -> p a t", a=NKI))
            if mi == 15:
                q.wait(r_rope)
                r_k_store = q.dma("ks", out=kT_d[:, t0:t0 + CW].rearrange("(a p) t -> p a t", p=128),
                                  in_=kstr[:].rearrange("p (a t) -> p a t", a=NKI))
        for tt in range(4):
            for nch in range(2):
                pb = bank[4 + nch]
                if vab_ev is not None and not (c == 0 and tt == 0):
                    t.wait(vab_ev)
                t.wait(r_wv)
                for k in range(NKI):
                    r_vmm = t.matmul(pb[:], xnT[:, k * CW + tt * 128:k * CW + (tt + 1) * 128],
                                     wv[:, k * D + nch * 512:k * D + (nch + 1) * 512],
                                     start=(k == 0), stop=(k == NKI - 1))
                v.wait(r_vmm)
                if c == 0 and tt == 0 and nch == 0:
                    v.wait(r_vz)
                itile = 4 * c + tt
                for hh in range(8):
                    h = nch * 8 + hh
                    vab_ev = v.tensor_copy(
                        out=vab[:, itile * NH * (HD + 1) + h * (HD + 1):
                                itile * NH * (HD + 1) + h * (HD + 1) + HD],
                        in_=pb[:, hh * HD:(hh + 1) * HD])

    ah_store = [None] * NH
    prev_avev = {0: None, 1: None}
    r_cp_prev = {0: None, 1: None}
    last_smm = [None, None]
    av_hist = {}
    exp_hist = {}
    for u in range(NH // 2):
        up = u % 2
        if last_smm[up] is not None:
            q.wait(last_smm[up])
        if u == 0:
            q.wait(r_k_store)
            q.wait(r_q_store)
        r_kb = q.dma(f"kb{up}", out=kbuf[up][:], in_=kT_d[128 * u:128 * u + 128, :])
        r_qb = q.dma(f"qb{up}", out=qh[up][:], in_=qT_d[128 * u:128 * u + 128, :])
        for jq in range(8):
            nlive = 4 * jq + 4
            r_e_q = [[None] * nlive, [None] * nlive]
            r_av = None
            for i in range(nlive + 1):
                if i < nlive:
                    if u == 0 and jq == 0 and i == 0:
                        t.wait(r_ev)
                        t.wait(vab_ev)
                    if i < 4:
                        t.wait(r_kb)
                        t.wait(r_qb)
                    for hh_ in range(2):
                        bi = 2 * hh_ + i % 2
                        if exp_hist.get(bi) is not None:
                            t.wait(exp_hist[bi])
                        r_s = t.matmul(bank[bi][:],
                                       kbuf[up][64 * hh_:64 * hh_ + 64, i * 128:(i + 1) * 128],
                                       qh[up][64 * hh_:64 * hh_ + 64, jq * 512:(jq + 1) * 512],
                                       start=True, stop=True,
                                       tile_position=(64 * hh_, 0))
                        s.wait(r_s)
                        if av_hist.get(bi) is not None:
                            s.wait(av_hist[bi])
                        r_e = s.activation(out=eT[bi][:], in_=bank[bi][:], func=AF.Exp,
                                           bias=EXP_SHIFT)
                        exp_hist[bi] = r_e
                        if i >= 4 * jq:
                            v.wait(r_e)
                            r_e = v.tensor_tensor(out=eT[bi][:], in0=eT[bi][:],
                                                  in1=msks[:, (i - 4 * jq) * 512:(i - 4 * jq + 1) * 512],
                                                  op=OP.mult)
                            exp_hist[bi] = r_e
                        r_e_q[hh_][i] = r_e
                j2 = i - 1
                if j2 >= 0:
                    for hh_ in range(2):
                        h = 2 * u + hh_
                        bi = 2 * hh_ + j2 % 2
                        t.wait(r_e_q[hh_][j2])
                        if j2 == 0:
                            if r_cp_prev[hh_] is not None:
                                t.wait(r_cp_prev[hh_])
                            if prev_avev[hh_] is not None:
                                t.wait(prev_avev[hh_])
                        r_av = t.matmul(bank[4 + hh_][0:65, :],
                                        vab[:, j2 * NH * (HD + 1) + h * (HD + 1):
                                            j2 * NH * (HD + 1) + (h + 1) * (HD + 1)],
                                        eT[bi][:],
                                        start=(j2 == 0), stop=(j2 == nlive - 1))
                        av_hist[bi] = r_av
            last_smm[up] = r_s
            for hh_ in range(2):
                v.wait(av_hist[2 * hh_ + (nlive - 1) % 2])
                if jq == 0 and u >= 1:
                    v.wait(ah_store[2 * (u - 1) + hh_])
                if u == 0 and jq == 0 and hh_ == 0:
                    v.wait(r_q_store)
                r_rc = v.reciprocal(out=zr2[hh_][:], in_=bank[4 + hh_][64:65, :])
                r_cp_prev[hh_] = v.tensor_copy(out=avs2[64 * hh_:64 * hh_ + 64, :],
                                               in_=bank[4 + hh_][0:64, :])
                t.wait(r_rc)
                if hh_ == 0:
                    r_zb = t.matmul(bank[6][0:64, :], onen[0:1, 0:64], zr2[0][:],
                                    start=True, stop=True)
                else:
                    r_zb = t.matmul(bank[6][64:128, :], onen[0:1, 0:64], zr2[1][:],
                                    start=True, stop=True, tile_position=(0, 64))
                v.wait(r_zb)
                prev_avev[hh_] = v.tensor_tensor(
                    out=ah2[64 * hh_:64 * hh_ + 64, jq * CW:(jq + 1) * CW],
                    in0=avs2[64 * hh_:64 * hh_ + 64, :],
                    in1=bank[6][64 * hh_:64 * hh_ + 64, :], op=OP.mult)
        q.wait(prev_avev[0])
        q.wait(prev_avev[1])
        ah_store[2 * u] = q.dma(f"ah{up}", out=asc_d[128 * u:128 * u + 128, :], in_=ah2[:])
        ah_store[2 * u + 1] = ah_store[2 * u]

    r_a1 = None
    a1_store = [None, None]
    prev_wo_mm = [None, None]
    prev_wo_ld = [None, None]
    for jq in range(8):
        t0 = jq * CW
        if jq == 0:
            for hh in range(NH):
                q.wait(ah_store[hh])
        if prev_wo_mm[0] is not None:
            q.wait(prev_wo_mm[0])
        r_ac = q.dma(f"ac{jq % 2}", out=aTc[:].rearrange("p (a t) -> p a t", a=NKI),
                     in_=asc_d[:, t0:t0 + CW].rearrange("(a p) t -> p a t", p=128))
        for o in range(NKI):
            pb = bank[o % 2]
            if prev_wo_ld[o % 2] is not None:
                gq.wait(prev_wo_ld[o % 2])
            r_wol = gq.dma(f"wol{o % 2}",
                           out=wo2[o % 2][:].rearrange("p (a m) -> p a m", a=NKI),
                           in_=wo_in[:, o * 128:(o + 1) * 128].rearrange("(a p) m -> p a m", p=128))
            t.wait(r_ac)
            t.wait(r_wol)
            if r_a1 is not None:
                t.wait(r_a1)
            for k in range(NKI):
                r_womm = t.matmul(pb[:], wo2[o % 2][:, k * 128:(k + 1) * 128],
                                  aTc[:, k * CW:(k + 1) * CW],
                                  start=(k == 0), stop=(k == NKI - 1))
            prev_wo_ld[o % 2] = r_womm
            v.wait(r_womm)
            if a1_store[o % 2] is not None:
                v.wait(a1_store[o % 2])
            r_a1 = v.tensor_scalar(out=a1ev[o % 2][:], in0=pb[:],
                                   scalar1=bo[:, o:o + 1], scalar2=None, op0=OP.add)
            q.wait(r_a1)
            a1_store[o % 2] = q.dma(f"a1{o % 2}",
                                    out=attn1T_out[o * 128:(o + 1) * 128, t0:t0 + CW],
                                    in_=a1ev[o % 2][:])
            if o == NKI - 1:
                prev_wo_mm[0] = r_womm


    q.fence(a1_store[0], a1_store[1], st_ref[NT - 2], st_ref[NT - 1])
    ctx.close()




def emit_stage_e(p, t, v, s, g, q, gq, bank, dr):
    nc = p.nc
    ctx = ExitStack()
    def sb(nm, shape, dt):
        return ctx.enter_context(nc.sbuf_tensor(f'e_'+nm, shape, dt))

    x1_in = dr["x1"]
    xn1_in = dr["xn1"]
    attn1T_in = dr["attn1T"]
    nwbc_in = dr["nw_bc"]
    rwT_in = dr["router_wT"]
    sw1_in = dr["sw1"]
    sb1_in = dr["sb1_r"]
    sw2_in = dr["sw2"]
    sb2_in = dr["sb2_r"]
    ew1_in = dr["ew1"]
    eb1_in = dr["eb1_r"]
    ew2_in = dr["ew2"]
    eb2_in = dr["eb2_r"]
    idnf_in = dr["ident_f"]
    tokid_in = dr["tokid"]
    ecapbig_in = dr["ecapbig1"]
    li_in = dr["lists_init1"]
    onen_in = dr["ones_n"]
    consts_in = dr["consts"]

    out_f = dr["out"]

    normed_d = dr["normed_d"]
    lists1 = dr["lists1"]

    nwbc = sb("nwbc", [128, D], BF16)
    rwT = sb("rwT", [128, NKI * NE], BF16)
    sb1 = sb("sb1", [128, NFT], F32)
    sb2 = sb("sb2", [128, NKI], F32)
    eb1 = sb("eb1", [128, NE * NFT], F32)
    eb2 = sb("eb2", [128, NE * NKI], F32)
    idnf = sb("idnf_e", [128, 128], F32)
    idnb = sb("idnb_e", [128, 128], BF16)
    tokids = sb("tokids_e", [128, NT], F32)
    ecapbig = sb("ecapbig_e", [8, 1], F32)

    sqa = sb("sqa", [128, 1], F32)
    rs_c = sb("rs_ce", [128, 1], F32)

    nT = sb("nT", [128, NKI * 512], BF16)
    exT = [nT] * 2
    wbig1 = sb("wbig1", [128, NKI * FF], BF16)
    wbig2 = sb("wbig2", [128, NFT * D], BF16)
    hidc = sb("hidc", [128, NFT * 512], BF16)
    a1c = [sb(f"a1c{i}", [128, 512], F32) for i in range(2)]
    comb = sb("comb", [128, NKI * 512], F32)
    combb = comb[:].bitcast(BF16)
    xt = [combb[:, 0:D], combb[:, D:2 * D]]
    nrm = [combb[:, 2 * D:3 * D], combb[:, 3 * D:4 * D]]

    xtok = sb("xtok_e", [128, D], F32)
    otok = [sb(f"otok_e{i}", [128, D], F32) for i in range(2)]
    hidcF = hidc[:].bitcast(F32)
    gateC = hidcF[0:8, 0:1024]
    maskC = hidcF[0:8, 1024:2048]
    posC = hidcF[0:8, 2048:3072]
    carry = sb("carry", [8, 1], F32)
    scTc = sb("scTc", [8, 512], F32)
    ones8c = sb("ones8c", [8, 1], F32)
    sctok = sb("sctok", [128, 8], F32)
    top8 = sb("top8", [128, 8], F32)
    den = sb("den", [128, 1], F32)
    msk8 = sb("msk8", [128, 8], F32)
    gtok = sb("gtok", [128, NT * 8], F32)
    offu = [sb("offu_e0", [128, 8], U32)] * 2
    src2 = [sb("src2_e0", [128, 16], F32)] * 2

    lraw = xtok[:, 0:NTPE1 * NE * 2]
    idsu = xtok[:, 256:256 + NTPE1 * NE].bitcast(U32)
    gts = xtok[:, 384:384 + NTPE1 * NE]
    gx = [a1c[0][:].bitcast(BF16), a1c[1][:].bitcast(BF16)]




    pre = []
    for dst, src in [(nwbc, nwbc_in), (sb1, sb1_in), (sb2, sb2_in), (eb1, eb1_in),
                     (eb2, eb2_in), (idnf, idnf_in), (tokids, tokid_in),
                     (ecapbig, ecapbig_in)]:
        pre.append(q.dma("pre", out=dst[:], in_=src))
    pre_done = pre[-1]
    r_rw = gq.dma("rw", out=rwT[:].rearrange("p (a e) -> p a e", a=NKI),
                  in_=rwT_in.rearrange("(a p) e -> p a e", p=128))
    r_sw1 = gq.dma("w1", out=wbig1[:].rearrange("p (a f) -> p a f", a=NKI),
                   in_=sw1_in.rearrange("(a p) f -> p a f", p=128))
    r_idb = gq.dma("idb", out=idnb[:], in_=idnf_in)
    r_li = q.dma("sli", out=lists1, in_=li_in)

    st_ref = [None] * NT
    nm_ref = [None] * NT
    for j in range(NT):
        bj = j % 2
        if j >= 2:
            q.wait(nm_ref[j - 2])
        r_lx = q.dma(f"lx{bj}", out=xt[bj], in_=xn1_in[j * 128:(j + 1) * 128, :])
        s.wait(r_lx)
        if j == 0:
            s.wait(pre_done)
        if j >= 2:
            s.wait(nm_ref[j - 2])
        s.activation(out=exT[0][:, 0:D], in_=xt[bj], func=AF.Square, accum_out=sqa[:])
        s.activation(out=rs_c[:], in_=sqa[:], func=AF.Ln, bias=RMS_EPS, scale=1.0 / D)
        r_rs = s.activation(out=rs_c[:], in_=rs_c[:], func=AF.Exp, scale=-0.5)
        v.wait(r_rs)
        if j >= 2:
            v.wait(st_ref[j - 2])
        v.tensor_scalar(out=nrm[bj], in0=xt[bj], scalar1=rs_c[:], scalar2=None,
                        op0=OP.mult)
        nm_ref[j] = v.tensor_tensor(out=nrm[bj], in0=nrm[bj], in1=nwbc[:],
                                    op=OP.mult)
        q.wait(nm_ref[j])
        st_ref[j] = q.dma(f"sn{bj}", out=normed_d[j * 128:(j + 1) * 128, :], in_=nrm[bj])

    sc_tr = [None] * NT
    r_comb_st = [None, None]
    r_hid_use = None
    r_sh_ev = None
    r_sc_ev = None
    r_trs_last = None
    prev_nT_use = None
    sw2_ld = [None, None]
    sw2_mm = [None, None]
    a1_used = [None, None]
    sh_ev_bank = {4: None, 5: None}
    r_relu = None
    r_hid_last_prev = None
    ctr_bank = {6: None, 7: None}
    add_ref = [None] * NKI
    for c in range(NCHB):
        t0 = c * CW
        nt_refs = []
        for a in range(NKI):
            if a == 0:
                q.wait(st_ref[NT - 2])
                q.wait(st_ref[NT - 1])
                if prev_nT_use is not None:
                    q.wait(prev_nT_use)
            nt_refs.append(q.dma_transpose(f"ntc{a % 2}",
                                           out=nT[:, a * CW:(a + 1) * CW],
                                           in_=normed_d[t0:t0 + CW, a * 128:(a + 1) * 128]))
        nt_done2, nt_done = nt_refs[NKI - 2], nt_refs[NKI - 1]
        t.wait(nt_done)
        t.wait(nt_done2)
        if c == 0:
            t.wait(r_rw)
        if r_sc_ev is not None:
            t.wait(r_sc_ev)
        for k in range(NKI):
            r_scm = t.matmul(bank[2][0:8, :], rwT[:, k * NE:(k + 1) * NE],
                             nT[:, k * CW:(k + 1) * CW],
                             start=(k == 0), stop=(k == NKI - 1))
        s.wait(r_scm)
        if r_trs_last is not None:
            s.wait(r_trs_last)
        r_sc_ev = s.activation(out=scTc[:], in_=bank[2][0:8, :],
                               func=AF.Sigmoid)
        for ft in range(NFT):
            pb = bank[ft % 2]
            if c == 0 and ft == 0:
                t.wait(r_sw1)
            wref = r_relu_prev2 if ft >= 2 else r_hid_last_prev
            if wref is not None:
                t.wait(wref)
            for k in range(NKI):
                r_h = t.matmul(pb[:], wbig1[:, k * FF + ft * 128:k * FF + (ft + 1) * 128],
                               nT[:, k * CW:(k + 1) * CW],
                               start=(k == 0), stop=(k == NKI - 1))
            s.wait(r_h)
            r_relu_prev2 = r_relu if ft >= 1 else None
            r_relu = s.activation(out=hidc[:, ft * CW:(ft + 1) * CW], in_=pb[:],
                                  func=AF.Relu, bias=sb1[:, ft:ft + 1])
        r_hid_last_prev = r_relu
        prev_nT_use = r_h

        for o in range(NKI):
            op_ = o % 2
            if sw2_mm[op_] is not None:
                gq.wait(sw2_mm[op_])
            sw2_ld[op_] = gq.dma(f"sw2{op_}",
                                 out=wbig2[:, 0:NFT * 128].rearrange("p (a m) -> p a m", a=NFT)
                                 if False else wbig2[:, op_ * NFT * 128:(op_ + 1) * NFT * 128].rearrange("p (a m) -> p a m", a=NFT),
                                 in_=sw2_in[:, o * 128:(o + 1) * 128].rearrange("(a p) m -> p a m", p=128))
            pb = bank[4 + op_]
            t.wait(sw2_ld[op_])
            t.wait(r_relu)
            if sh_ev_bank[4 + op_] is not None:
                t.wait(sh_ev_bank[4 + op_])
            for k in range(NFT):
                r_sh = t.matmul(pb[:], wbig2[:, op_ * NFT * 128 + k * 128:op_ * NFT * 128 + (k + 1) * 128],
                                hidc[:, k * CW:(k + 1) * CW],
                                start=(k == 0), stop=(k == NFT - 1))
            sw2_mm[op_] = r_sh
            if a1_used[op_] is not None:
                q.wait(a1_used[op_])
            r_aload = q.dma(f"a1l{op_}", out=a1c[op_][:],
                            in_=attn1T_in[o * 128:(o + 1) * 128, t0:t0 + CW])
            v.wait(r_sh)
            v.wait(r_aload)
            v.tensor_scalar(out=comb[:, o * CW:(o + 1) * CW], in0=pb[:],
                            scalar1=sb2[:, o:o + 1], scalar2=None, op0=OP.add)
            r_sh_ev = v.tensor_tensor(out=comb[:, o * CW:(o + 1) * CW],
                                      in0=comb[:, o * CW:(o + 1) * CW],
                                      in1=a1c[op_][:], op=OP.add)
            a1_used[op_] = r_sh_ev
            sh_ev_bank[4 + op_] = r_sh_ev
        for s4 in range(4):
            sp = s4 % 2
            if r_comb_st[sp] is not None:
                q.wait(r_comb_st[sp])
            r_lx1 = q.dma(f"x1l{sp}", out=xtok[:],
                          in_=x1_in[t0 + s4 * 128:t0 + (s4 + 1) * 128, :]) if s4 < 2 else \
                q.dma(f"x1l{sp}", out=xtok[:], in_=x1_in[t0 + s4 * 128:t0 + (s4 + 1) * 128, :])
            for o in range(NKI):
                bi = 6 + o % 2
                pt = bank[bi]
                t.wait(r_sh_ev)
                if ctr_bank[bi] is not None:
                    t.wait(ctr_bank[bi])
                r_tr = t.transpose(pt[:, 0:128],
                                   comb[:, o * CW + s4 * 128:o * CW + (s4 + 1) * 128],
                                   idnf[:])
                v.wait(r_tr)
                v.wait(r_lx1)
                add_ref[o] = v.tensor_tensor(out=otok[sp][:, o * 128:(o + 1) * 128],
                                             in0=pt[:, 0:128],
                                             in1=xtok[:, o * 128:(o + 1) * 128], op=OP.add)
                ctr_bank[bi] = add_ref[o]
            q.wait(add_ref[NKI - 1])
            r_comb_st[sp] = q.dma(f"so{sp}", out=out_f[t0 + s4 * 128:t0 + (s4 + 1) * 128, :],
                                  in_=otok[sp][:])
        for tt in range(4):
            j = 4 * c + tt
            t.wait(r_sc_ev)
            if j >= 1 and sc_tr[j - 1] is not None:
                t.wait(sc_tr[j - 1])
            r_trs = t.transpose(bank[3][0:128, 0:8], scTc[:, tt * 128:(tt + 1) * 128],
                                idnf[0:8, 0:8])
            r_trs_last = r_trs
            v.wait(r_trs)
            v.tensor_copy(out=sctok[:], in_=bank[3][0:128, 0:8])
            v.max(out=top8[:], in_=sctok[:])
            v.tensor_tensor(out=den[:], in0=top8[:, 0:1], in1=top8[:, 1:2], op=OP.add)
            v.tensor_scalar(out=den[:], in0=den[:], scalar1=1e-6, scalar2=None, op0=OP.add)
            v.reciprocal(out=den[:], in_=den[:])
            v.tensor_scalar(out=msk8[:], in0=sctok[:], scalar1=top8[:, 1:2], scalar2=None,
                            op0=OP.is_ge)
            v.tensor_tensor(out=msk8[:], in0=sctok[:], in1=msk8[:], op=OP.mult)
            sc_tr[j] = v.tensor_scalar(out=gtok[:, j * 8:(j + 1) * 8], in0=msk8[:],
                                       scalar1=den[:], scalar2=None, op0=OP.mult)

    v.memset(ones8c[:], 1.0)
    v.memset(carry[:], 0.0)
    lb_ref = [None] * NT
    sc_refs = [[None] * NE for _ in range(NT)]
    gt_ev = None
    for ch4 in range(4):
        for jj in range(8):
            j = 8 * ch4 + jj
            t.wait(sc_tr[NT - 1] if j == 0 else sc_tr[j])
            if gt_ev is not None:
                t.wait(gt_ev)
            if ch4 > 0 and jj == 0:
                t.wait(lb_ref[8 * ch4 - 1])
            r_trg = t.transpose(bank[3][0:8, 0:128], gtok[:, j * 8:(j + 1) * 8], idnf[:])
            v.wait(r_trg)
            if j == 0:
                v.wait(sw2_mm[0])
                v.wait(sw2_mm[1])
            gt_ev = v.tensor_copy(out=gateC[:, jj * 128:(jj + 1) * 128],
                                  in_=bank[3][0:8, 0:128])
        v.tensor_scalar(out=maskC, in0=gateC, scalar1=0.0, scalar2=None,
                        op0=OP.is_gt)
        v.tensor_tensor_scan(out=posC, data0=ones8c[:].to_broadcast([8, 1024]),
                             data1=maskC, initial=carry[:, 0:1],
                             op0=OP.mult, op1=OP.add)
        v.tensor_copy(out=carry[:], in_=posC[:, 1023:1024])
        v.scalar_tensor_tensor(out=posC, in0=maskC, scalar=-BIG,
                               in1=posC, op0=OP.mult, op1=OP.add)
        r_mask = v.tensor_scalar(out=posC, in0=posC, scalar1=ecapbig[:],
                                 scalar2=None, op0=OP.add)
        for jj in range(8):
            j = 8 * ch4 + jj
            bj = j % 2
            t.wait(r_mask)
            if j >= 1:
                t.wait(lb_ref[j - 1])
            tr1 = t.transpose(bank[3][:, 0:8], posC[:, jj * 128:(jj + 1) * 128],
                              idnf[0:8, 0:8])
            tr2 = t.transpose(bank[3][:, 8:16], gateC[:, jj * 128:(jj + 1) * 128],
                              idnf[0:8, 0:8])
            v.wait(tr1)
            v.wait(tr2)
            if j >= 1:
                for e in range(NE):
                    v.wait(sc_refs[j - 1][e])
            v.tensor_copy(out=offu[bj][:], in_=bank[3][:, 0:8])
            s2v = src2[bj][:].rearrange("p (a b) -> p a b", b=2)
            v.tensor_copy(out=s2v[:, :, 1], in_=bank[3][:, 8:16])
            lb_ref[j] = v.tensor_copy(out=s2v[:, :, 0],
                                      in_=tokids[:, j:j + 1].to_broadcast([128, 8]))
            gq.wait(lb_ref[j])
            if j == 0:
                gq.wait(r_li)
            for e in range(NE):
                if j >= 1:
                    gq.wait(sc_refs[j - 1][e])
                sc_refs[j][e] = gq.indirect(
                    f"g{e}", out=lists1,
                    out_offset=IndirectOffsetOnAxis(offu[bj][:, e:e + 1], 0),
                    in_=src2[bj][:, 2 * e:2 * e + 2], in_offset=None,
                    bounds=NE * CAP1 - 1, oob_is_err=False)

    for e in range(NE):
        q.wait(sc_refs[NT - 1][e])
    r_ll = q.dma("llr", out=lraw.rearrange("p (a n) -> p a n", n=2),
                 in_=lists1.rearrange("(a p) n -> p a n", p=128))
    v.wait(r_ll)
    lv = lraw.rearrange("p (a n) -> p a n", n=2)
    r_ids = v.tensor_copy(out=idsu, in_=lv[:, :, 0])
    r_gts = v.tensor_copy(out=gts, in_=lv[:, :, 1])

    ew1_ref = gq.dma("w1", out=wbig1[:].rearrange("p (a f) -> p a f", a=NKI),
                     in_=ew1_in[0].rearrange("(a p) f -> p a f", p=128))
    ew2_ref = gq.dma("w2e", out=wbig2[:].rearrange("p (a o) -> p a o", a=NFT),
                     in_=ew2_in[0].rearrange("(a p) o -> p a o", p=128))

    prev_scat = None
    prev_gather = {}
    prev_exT_use = [None, None]
    prev_otok_use = [None, None]
    tr_bank = {6: None, 7: None}
    otr_bank = {4: None, 5: None}
    r_hid_done_prev = None
    GRP1 = [4, 4, 4]
    for e in range(NE):
        g0 = 0
        for gi, gnt in enumerate(GRP1):
            gp = (e * len(GRP1) + gi) % 2
            nslots = gnt * 128
            ex_done = None
            for jj in range(gnt):
                a = e * NTPE1 + g0 + jj
                bj = jj % 2
                if prev_gather.get(bj) is not None:
                    v.wait(prev_gather[bj])
                r_ms = v.memset(gx[bj][:, 0:D], 0.0)
                gq.wait(r_ms)
                gq.wait(r_ids)
                r_g = gq.indirect(f"ga{bj}", out=gx[bj][:, 0:D], out_offset=None,
                                  in_=normed_d,
                                  in_offset=IndirectOffsetOnAxis(idsu[:, a:a + 1], 0),
                                  bounds=T - 1, oob_is_err=False)
                for k in range(NKI):
                    bi = 6 + k % 2
                    ptb = bank[bi][:].bitcast(BF16)
                    t.wait(r_g)
                    if e == 0 and gi == 0 and jj == 0 and k == 0:
                        t.wait(r_idb)
                        t.wait(r_comb_st[0])
                        t.wait(r_comb_st[1])
                    if tr_bank[bi] is not None:
                        t.wait(tr_bank[bi])
                    r_tr = t.transpose(ptb[:, 0:128], gx[bj][:, k * 128:(k + 1) * 128],
                                       idnb[:])
                    v.wait(r_tr)
                    if prev_exT_use[gp] is not None and jj == 0 and k == 0:
                        v.wait(prev_exT_use[gp])
                    r_ev = v.tensor_copy(out=exT[gp][:, k * 512 + jj * 128:k * 512 + jj * 128 + 128],
                                         in_=ptb[:, 0:128])
                    tr_bank[bi] = r_ev
                prev_gather[bj] = r_ev
                ex_done = r_ev
            r_gelu = None
            for ft in range(NFT):
                pb = bank[ft % 2]
                t.wait(ex_done)
                if ft == 0:
                    t.wait(ew1_ref)
                    if r_hid_done_prev is not None:
                        t.wait(r_hid_done_prev)
                if ft >= 2:
                    t.wait(r_gelu_prev2)
                for k in range(NKI):
                    r_w1 = t.matmul(pb[:, 0:nslots],
                                    wbig1[:, k * FF + ft * 128:k * FF + (ft + 1) * 128],
                                    exT[gp][:, k * 512:k * 512 + nslots],
                                    start=(k == 0), stop=(k == NKI - 1))
                s.wait(r_w1)
                r_gelu_prev2 = r_gelu if ft >= 1 else None
                r_gelu = s.activation(out=hidc[:, ft * 512:ft * 512 + nslots],
                                      in_=pb[:, 0:nslots], func=AF.Relu,
                                      bias=eb1[:, e * NFT + ft:e * NFT + ft + 1])
                if ft == NFT - 1:
                    prev_w1_last = r_w1
            prev_exT_use[gp] = prev_w1_last
            if gi == len(GRP1) - 1 and e < NE - 1:
                gq.wait(prev_w1_last)
                ew1_ref = gq.dma("w1", out=wbig1[:].rearrange("p (a f) -> p a f", a=NKI),
                                 in_=ew1_in[e + 1].rearrange("(a p) f -> p a f", p=128))
            r_ot = None
            for o in range(NKI):
                pb = bank[2 + o % 2]
                if o == 0:
                    t.wait(r_gelu)
                    t.wait(ew2_ref)
                if o >= 2:
                    t.wait(r_ot_prev2)
                for k in range(NFT):
                    r_w2 = t.matmul(pb[:, 0:nslots],
                                    wbig2[:, k * D + o * 128:k * D + (o + 1) * 128],
                                    hidc[:, k * 512:k * 512 + nslots],
                                    start=(k == 0), stop=(k == NFT - 1))
                v.wait(r_w2)
                r_ot_prev2 = r_ot if o >= 1 else None
                r_ot = v.tensor_scalar(out=comb[:, o * 512:o * 512 + nslots],
                                       in0=pb[:, 0:nslots],
                                       scalar1=eb2[:, e * NKI + o:e * NKI + o + 1],
                                       scalar2=None, op0=OP.add)
                if o == NKI - 1:
                    prev_w2_last = r_w2
            r_hid_done_prev = prev_w2_last
            if gi == len(GRP1) - 1 and e < NE - 1:
                gq.wait(prev_w2_last)
                ew2_ref = gq.dma("w2e", out=wbig2[:].rearrange("p (a o) -> p a o", a=NFT),
                                 in_=ew2_in[e + 1].rearrange("(a p) o -> p a o", p=128))
            for jj in range(gnt):
                a = e * NTPE1 + g0 + jj
                op_ = jj % 2
                for o in range(NKI):
                    bi = 4 + o % 2
                    pt = bank[bi]
                    t.wait(r_ot)
                    if otr_bank[bi] is not None:
                        t.wait(otr_bank[bi])
                    r_tr2 = t.transpose(pt[:, 0:128],
                                        comb[:, o * 512 + jj * 128:o * 512 + jj * 128 + 128],
                                        idnf[:])
                    v.wait(r_tr2)
                    if o == 0 and prev_otok_use[op_] is not None:
                        v.wait(prev_otok_use[op_])
                    r_go = v.tensor_scalar(out=otok[op_][:, o * 128:(o + 1) * 128],
                                           in0=pt[:, 0:128],
                                           scalar1=gts[:, a:a + 1], scalar2=None,
                                           op0=OP.mult)
                    otr_bank[bi] = r_go
                gq.wait(r_go)
                if prev_scat is not None:
                    gq.wait(prev_scat)
                else:
                    gq.wait(r_comb_st[0])
                    gq.wait(r_comb_st[1])
                prev_scat = gq.indirect(
                    "sc0", out=out_f, out_offset=IndirectOffsetOnAxis(idsu[:, a:a + 1], 0),
                    in_=otok[op_][:], in_offset=None,
                    bounds=T - 1, oob_is_err=False, compute_op=OP.add)
                prev_otok_use[op_] = prev_scat
            g0 += gnt

    q.fence(prev_scat)
    ctx.close()




def build_fused():
    p = Prog()
    nc = p.nc
    f = np.float32
    dr = {}
    def din(nm, shape, dt):
        dr[nm] = p.din(nm, shape, dt)
    def dint(nm, shape, dt):
        dr[nm] = nc.dram_tensor(nm, shape, dt, kind="Internal").ap()

    din("x_own", [T, D], F32)
    din("x_allT", [D, TT_ALL], F32)
    din("Wg9", [D, 9], F32)
    din("ones_col", [128, 1], F32)
    din("ones_n", [1, 128], F32)
    din("tokid", [128, NT], F32)
    din("ecapbig", [8, 1], F32)
    din("lists_init", [128, NE * CAP0 * 2 // 128], F32)
    din("ident", [128, 128], F32)
    din("consts", [128, 8], F32)
    din("W_inT", [D, MW], F32)
    din("W_dtT", [DT_RANK, D_INNER], F32)
    din("W_outT", [D_INNER, D], F32)
    din("A_log_r", [128, NDI * 16], F32)
    din("b_dt_r", [128, NDI], F32)
    din("convw_r", [128, NDI * 4], F32)
    din("convb_r", [128, NDI], F32)
    din("D_r", [128, NDI], F32)
    din("lng_r", [128, NKI], F32)
    din("lnb_r", [128, NKI], F32)
    din("bsel", [16, 16 * 128], BF16)
    din("ident_bf", [128, 128], BF16)
    din("ident_f", [128, 128], F32)
    din("onec_bf", [128, 1], BF16)
    din("W1", [NE, D, FF], F32)
    din("b1_r", [128, NE * NFT], F32)
    din("W2", [NE, FF, D], F32)
    din("b2_r", [128, NE * NKI], F32)
    din("Wqkv", [D, 3 * D], F32)
    din("Wo", [D, D], F32)
    din("bo_r", [128, NKI], F32)
    din("cosT", [128, T], BF16)
    din("sinT", [128, T], BF16)
    din("P2", [128, 128], BF16)
    din("masks", [128, 4 * 512], BF16)
    din("nw_bc", [128, D], BF16)
    din("router_wT", [D, NE], F32)
    din("sw1", [D, FF], F32)
    din("sb1_r", [128, NFT], F32)
    din("sw2", [FF, D], F32)
    din("sb2_r", [128, NKI], F32)
    din("ew1", [NE, D, FF], F32)
    din("eb1_r", [128, NE * NFT], F32)
    din("ew2", [NE, FF, D], F32)
    din("eb2_r", [128, NE * NKI], F32)
    din("ecapbig1", [8, 1], F32)
    din("lists_init1", [NE * CAP1, 2], F32)

    dint("xn", [T, D], BF16)
    dint("lists0", [NE * CAP0, 2], F32)
    dint("thr", [8, 1], F32)
    dint("x1a", [T, D], F32)
    dint("x1", [T, D], F32)
    dint("xn1", [T, D], BF16)
    dint("attn1T", [D, T], F32)
    dint("qT_d", [D, T], BF16)
    dint("kT_d", [D, T], BF16)
    dint("asc_d", [D, T], BF16)
    dint("normed_d", [T, D], BF16)
    dint("lists1", [NE * CAP1, 2], F32)

    dr["out"] = p.dout("out", [T, D], F32)

    csts_g = p.sb("csts_g", [128, 8], F32)
    nc.const_aps.register(F32, LN_EPS, csts_g[:, 0:1])
    nc.const_aps.register(F32, -0.5, csts_g[:, 1:2])
    nc.const_aps.register(F32, -1.0, csts_g[:, 2:3])
    nc.const_aps.register(F32, 1.0, csts_g[:, 3:4])
    nc.const_aps.register(F32, 0.125, csts_g[:, 4:5])
    nc.const_aps.register(F32, EXP_SHIFT, csts_g[:, 5:6])
    nc.const_aps.register(F32, RMS_EPS, csts_g[:, 6:7])

    bank = [p.ps(f"bank{i}", [128, 512]) for i in range(8)]

    mkp = MKProg()
    t = mkp.eng("tensor")
    v = mkp.eng("vector")
    s = mkp.eng("scalar")
    g = mkp.eng("gpsimd")
    q = mkp.q("sync")
    gq = mkp.q("gpsimd")

    r_cst = q.dma("cst", out=csts_g[:], in_=dr["consts"])
    for eng in (t, v, s):
        eng.wait(r_cst)
    gq.wait(r_cst)

    emit_stage_a(p, t, v, s, g, q, gq, bank, dr)
    mkp.barrier()
    emit_stage_b(p, t, v, s, g, q, gq, bank, dr)
    mkp.barrier()
    emit_stage_c(p, t, v, s, g, q, gq, bank, dr)
    mkp.barrier()
    emit_stage_d(p, t, v, s, g, q, gq, bank, dr)
    mkp.barrier()
    emit_stage_e(p, t, v, s, g, q, gq, bank, dr)
    mkp.emit(p)
    return p


def stage_a_inputs(x_full, Wg, core):
    x_own = np.ascontiguousarray(x_full[core]).astype(np.float32)
    order = [core] + [b for b in range(B) if b != core]
    x_ord = x_full[order]
    x_allT = np.ascontiguousarray(x_ord.reshape(TT_ALL, D).T).astype(np.float32)
    wg9 = np.concatenate([Wg.astype(np.float32),
                          np.full((D, 1), 1.0 / D, np.float32)], axis=1)
    ecapbig = (np.arange(NE, dtype=np.float32) * CAP0 - 1.0 + BIG).reshape(NE, 1)
    li = np.zeros((NE * CAP0, 2), np.float32)
    li[:, 0] = BIG
    return {
        "x_own": x_own,
        "x_allT": x_allT,
        "Wg9": wg9,
        "ones_col": np.ones((128, 1), np.float32),
        "ones_n": np.ones((1, 128), np.float32),
        "tokid": np.arange(T, dtype=np.float32).reshape(NT, 128).T.copy(),
        "ecapbig": ecapbig,
        "lists_init": np.ascontiguousarray(
            li.reshape(-1, 128, 2).transpose(1, 0, 2).reshape(128, -1)),
        "ident": np.eye(128, dtype=np.float32),
        "consts": np.broadcast_to(np.array([LN_EPS, -0.5], np.float32), (128, 2)).copy(),
    }




def stage_b_inputs(xn, x_core, W_in, W_dt, b_dt, A_log, Dv, conv_w, conv_b, W_out,
                   ln_g, ln_b):
    f = np.float32
    Wp = np.zeros((MW, D), f)
    Wp[0:2048] = W_in[0:2048]
    Wp[2048:4096] = W_in[2048:4096]
    m32 = 32 * 128
    Wp[m32 + 0:m32 + 16] = W_in[4096:4112]
    Wp[m32 + 32:m32 + 48] = W_in[4112:4128]
    Wp[m32 + 64:m32 + 128] = W_in[4128:4192]
    bsel = np.zeros((16, 16 * 128), BF16NP)
    for n in range(16):
        bsel[n, n * 128:(n + 1) * 128] = 1
    return {
        "xn": xn,
        "x_own": None if x_core is None else x_core.astype(f),
        "W_inT": np.ascontiguousarray(Wp.T),
        "W_dtT": np.ascontiguousarray(W_dt.astype(f).T),
        "W_outT": np.ascontiguousarray(W_out.astype(f).T),
        "A_log_r": np.ascontiguousarray(
            A_log.astype(f).reshape(16, 128, 16).transpose(1, 0, 2).reshape(128, -1)),
        "b_dt_r": np.ascontiguousarray(b_dt.astype(f).reshape(16, 128).T),
        "convw_r": np.ascontiguousarray(
            conv_w.astype(f).reshape(16, 128, 4).transpose(1, 0, 2).reshape(128, -1)),
        "convb_r": np.ascontiguousarray(conv_b.astype(f).reshape(16, 128).T),
        "D_r": np.ascontiguousarray(Dv.astype(f).reshape(16, 128).T),
        "lng_r": np.ascontiguousarray(ln_g.astype(f).reshape(8, 128).T),
        "lnb_r": np.ascontiguousarray(ln_b.astype(f).reshape(8, 128).T),
        "bsel": bsel,
        "ident_bf": np.eye(128, dtype=BF16NP),
        "ident_f": np.eye(128, dtype=np.float32),
        "onec_bf": np.ones((128, 1), BF16NP),
        "ones_n": np.ones((1, 128), np.float32),
        "consts": np.broadcast_to(np.array([LN_EPS, -0.5, -1.0, 1.0], f), (128, 4)).copy(),
    }




def stage_c_inputs(xn, x1a, lists0, W1, b1, W2, b2):
    f = np.float32
    return {
        "xn": xn,
        "x1a": None if x1a is None else x1a.astype(f),
        "lists0": None if lists0 is None else lists0.astype(f),
        "W1": W1.astype(f),
        "b1_r": np.ascontiguousarray(
            b1.astype(f).reshape(NE, NFT, 128).transpose(2, 0, 1).reshape(128, -1)),
        "W2": W2.astype(f),
        "b2_r": np.ascontiguousarray(
            b2.astype(f).reshape(NE, NKI, 128).transpose(2, 0, 1).reshape(128, -1)),
        "ident_bf": np.eye(128, dtype=BF16NP),
    }




def stage_d_inputs(x1, Wqkv, Wo, bo):
    f = np.float32
    half = HD // 2
    inv_freq = 1.0 / 10000.0 ** (np.arange(half, dtype=f) / half)
    tv = np.arange(T, dtype=f)
    freqs = np.outer(tv, inv_freq)
    cos = np.repeat(np.cos(freqs), 2, axis=1)
    sin = np.repeat(np.sin(freqs), 2, axis=1)
    cosT = np.ascontiguousarray(np.vstack([cos.T, cos.T])).astype(BF16NP)
    sinT = np.ascontiguousarray(np.vstack([sin.T, sin.T])).astype(BF16NP)
    P64 = np.zeros((HD, HD), f)
    for dpr in range(HD):
        if dpr % 2 == 0:
            P64[dpr + 1, dpr] = -1.0
        else:
            P64[dpr - 1, dpr] = 1.0
    P2v = np.zeros((128, 128), f)
    P2v[:HD, :HD] = P64
    P2v[HD:, HD:] = P64
    masks = np.zeros((128, 4, 512), f)
    for r in range(4):
        for pp in range(128):
            masks[pp, r, r * 128 + pp:] = 1.0
    return {
        "x1": None if x1 is None else x1.astype(f),
        "Wqkv": Wqkv.astype(f),
        "Wo": Wo.astype(f),
        "bo_r": np.ascontiguousarray(bo.astype(f).reshape(NKI, 128).T),
        "cosT": cosT,
        "sinT": sinT,
        "P2": P2v.astype(BF16NP),
        "masks": np.ascontiguousarray(masks.reshape(128, -1)).astype(BF16NP),
        "ones_n": np.ones((1, 128), f),
        "consts": np.broadcast_to(np.array([LN_EPS, -0.5, 0.125, EXP_SHIFT], f),
                                  (128, 4)).copy(),
    }




def stage_e_inputs(x1, xn1, attn1T, nw, router_w, sw1, sb1, sw2, sb2,
                   ew1, eb1, ew2, eb2):
    f = np.float32
    ecapbig = (np.arange(NE, dtype=f) * CAP1 - 1.0 + BIG).reshape(NE, 1)
    li = np.zeros((NE * CAP1, 2), f)
    li[:, 0] = BIG
    return {
        "x1": None if x1 is None else x1.astype(f),
        "xn1": xn1,
        "attn1T": None if attn1T is None else attn1T.astype(f),
        "nw_bc": np.broadcast_to(nw.astype(f), (128, D)).astype(BF16NP).copy(),
        "router_wT": np.ascontiguousarray(router_w.astype(f).T),
        "sw1": sw1.astype(f),
        "sb1_r": np.ascontiguousarray(sb1.astype(f).reshape(NFT, 128).T),
        "sw2": sw2.astype(f),
        "sb2_r": np.ascontiguousarray(sb2.astype(f).reshape(NKI, 128).T),
        "ew1": ew1.astype(f),
        "eb1_r": np.ascontiguousarray(
            eb1.astype(f).reshape(NE, NFT, 128).transpose(2, 0, 1).reshape(128, -1)),
        "ew2": ew2.astype(f),
        "eb2_r": np.ascontiguousarray(
            eb2.astype(f).reshape(NE, NKI, 128).transpose(2, 0, 1).reshape(128, -1)),
        "ident_f": np.eye(128, dtype=f),
        "tokid": np.arange(T, dtype=f).reshape(NT, 128).T.copy(),
        "ecapbig1": ecapbig,
        "lists_init1": li,
        "ones_n": np.ones((1, 128), f),
        "consts": np.broadcast_to(np.array([RMS_EPS, -0.5, 0.0, 0.0], f), (128, 4)).copy(),
    }


def fused_inputs(ins, core):
    f = np.float32
    x = np.asarray(ins["L0_attn_x"], f)
    im = {}
    im.update(stage_a_inputs(x, np.asarray(ins["L0_ffn_Wg"], f), core))
    b = stage_b_inputs(None, x[core],
                       np.asarray(ins["L0_attn_W_in"], f),
                       np.asarray(ins["L0_attn_W_dt"], f),
                       np.asarray(ins["L0_attn_b_dt"], f),
                       np.asarray(ins["L0_attn_A_log"], f),
                       np.asarray(ins["L0_attn_D"], f),
                       np.asarray(ins["L0_attn_conv_w"], f),
                       np.asarray(ins["L0_attn_conv_b"], f),
                       np.asarray(ins["L0_attn_W_out"], f),
                       np.asarray(ins["L0_attn_ln_g"], f),
                       np.asarray(ins["L0_attn_ln_b"], f))
    b.pop("xn"); b.pop("x_own")
    im.update(b)
    c = stage_c_inputs(None, None, None,
                       np.asarray(ins["L0_ffn_W1"], f),
                       np.asarray(ins["L0_ffn_b1"], f),
                       np.asarray(ins["L0_ffn_W2"], f),
                       np.asarray(ins["L0_ffn_b2"], f))
    for k in ("xn", "x1a", "lists0"):
        c.pop(k)
    im.update(c)
    d = stage_d_inputs(None,
                       np.asarray(ins["L1_attn_Wqkv"], f),
                       np.asarray(ins["L1_attn_Wo"], f),
                       np.asarray(ins["L1_attn_bo"], f))
    d.pop("x1")
    im.update(d)
    e = stage_e_inputs(None, None, None,
                       np.asarray(ins["L1_ffn_norm_weight"], f),
                       np.asarray(ins["L1_ffn_router_w"], f),
                       np.asarray(ins["L1_ffn_sw1"], f),
                       np.asarray(ins["L1_ffn_sb1"], f),
                       np.asarray(ins["L1_ffn_sw2"], f),
                       np.asarray(ins["L1_ffn_sb2"], f),
                       np.asarray(ins["L1_ffn_ew1"], f),
                       np.asarray(ins["L1_ffn_eb1"], f),
                       np.asarray(ins["L1_ffn_ew2"], f),
                       np.asarray(ins["L1_ffn_eb2"], f))
    for k in ("x1", "xn1", "attn1T"):
        e.pop(k)
    im.update(e)
    im["consts"] = np.broadcast_to(np.array(
        [LN_EPS, -0.5, -1.0, 1.0, 0.125, EXP_SHIFT, RMS_EPS, 0.0], f),
        (128, 8)).copy()
    return im


_FPROG = []

def kernel(**ins):
    if not _FPROG:
        _FPROG.append(build_fused())
    p = _FPROG[0]
    in_maps = [fused_inputs(ins, c) for c in range(B)]
    res = run_bass_kernel_spmd(p.nc, in_maps, core_ids=list(range(B)))
    return np.stack([np.asarray(r["out"], np.float32) for r in res.results], 0)
